# version 08
# baseline (speedup 1.0000x reference)
import numpy as np
import ml_dtypes
import concourse.bass as bass
import concourse.mybir as mybir
from concourse.bass_utils import run_bass_kernel_spmd
from contextlib import ExitStack

F32 = mybir.dt.float32
BF16 = mybir.dt.bfloat16
I32 = mybir.dt.int32
I16 = mybir.dt.int16
U8 = mybir.dt.uint8
AF = mybir.ActivationFunctionType
ALU = mybir.AluOpType
AX = mybir.AxisListType
BF = ml_dtypes.bfloat16

NCORES = 8
TOK = 2048
NT = TOK // 128
D = 1024
DC = D // 128
I_FFN = 2816
I_PHASES = [(0, 768), (768, 768), (1536, 768), (2304, 512)]
H = 16
DH = 64
LN2 = float(np.log(2.0))
NUDGE6 = float(np.float32(6.0 * (1 - 2**-20)))

ENGS = ("sync", "gpsimd", "vector", "scalar", "tensor")


class Sched:

    def __init__(self, nc):
        self.nc = nc
        self.q = {e: [] for e in ENGS}
        self.cnt = {}
        self.sems = {}
        self._es = None
        self.pfx = ""
        self.base = {}
        self.name_map = {}
        self.trace = {e: [] for e in ENGS}

    def _n(self, name):
        if name.startswith("!"):
            return name
        return self.name_map.setdefault(name, "s%d" % len(self.name_map))

    def _sem_raw(self, name):
        if name not in self.sems:
            self.sems[name] = self._es.enter_context(self.nc.semaphore(name))
            self.cnt[name] = 0
        return self.sems[name]

    def sem(self, name):
        return self._sem_raw(self._n(name))

    def op(self, eng, fn, inc=None, n=1, drain=None):
        if drain is None:
            drain = eng in ("vector", "gpsimd", "scalar")
        if inc is not None:
            inc = self._n(inc)
            self._sem_raw(inc)
            self.cnt[inc] += n
        self.trace[eng].append(("inc", inc, n) if inc else ("op", None, 0))
        def run(E, fn=fn, inc=inc, n=n, drain=drain):
            ins = fn(E)
            if drain:
                d = E.drain()
                if inc is not None:
                    d.then_inc(self.sems[inc], n)
            elif inc is not None:
                ins.then_inc(self.sems[inc], n)
        self.q[eng].append(run)

    def dma(self, eng, out, in_, inc, **kw):
        self.op(eng, lambda E, o=out, i=in_: E.dma_start(out=o, in_=i, **kw), inc=inc, n=16, drain=False)

    def rebase(self):
        self.base = dict(self.cnt)
        self.name_map = {}

    def _b(self, name):
        return 0 if name.startswith("!") else self.base.get(name, 0)

    def wait(self, eng, semname, val=None):
        semname = self._n(semname)
        self._sem_raw(semname)
        v = self.cnt[semname] if val is None else self._b(semname) + val
        if v == 0:
            return
        self.trace[eng].append(("wait", semname, v))
        self.q[eng].append(lambda E, s=semname, v=v: E.wait_ge(self.sems[s], v))

    def val(self, semname):
        n = self._n(semname)
        return self.cnt.get(n, 0) - self._b(n)

    def barrier(self):
        for e in ENGS:
            def mk(e=e):
                def run(E):
                    d = E.drain()
                    d.then_inc(self.sems["!bar"], 1)
                return run
            self._sem_raw("!bar")
            self.cnt["!bar"] += 1
            self.trace[e].append(("inc", "!bar", 1))
            self.q[e].append(mk())
        tot = self.cnt["!bar"]
        for e in ENGS:
            self.trace[e].append(("wait", "!bar", tot))
            self.q[e].append(lambda E, v=tot: E.wait_ge(self.sems["!bar"], v))

    def setup_consts(self, values, sb):
        cst = sb("cst", [128, len(values)])
        self.consts = {}
        self.consts = {}
        for i, v in enumerate(values):
            ap = cst[:, i:i + 1]
            self.consts[v] = ap
            self.op("gpsimd", lambda E, ap=ap, v=v: E.memset(ap, float(v)),
                    inc="c_init" if i == len(values) - 1 else None)
        return self.consts

    def check(self):
        pos = {e: 0 for e in ENGS}
        cnt = {}
        progress = True
        while progress:
            progress = False
            for e in ENGS:
                tr = self.trace[e]
                while pos[e] < len(tr):
                    kind, sem, v = tr[pos[e]]
                    if kind == "wait" and cnt.get(sem, 0) < v:
                        break
                    if kind == "inc":
                        cnt[sem] = cnt.get(sem, 0) + v
                    pos[e] += 1
                    progress = True
        stuck = {e: (pos[e], len(self.trace[e]), self.trace[e][pos[e]] if pos[e] < len(self.trace[e]) else None)
                 for e in ENGS if pos[e] < len(self.trace[e])}
        if stuck:
            for e, (p, n, item) in stuck.items():
                print(f"DEADLOCK {e}: op {p}/{n} waiting {item}; sem at {cnt.get(item[1], 0)}")
            raise RuntimeError(f"schedule deadlock: {stuck}")

    def build(self, block, es):
        self.check()
        for e in ENGS:
            ops = self.q[e]
            if not ops:
                continue
            def emit(E, ops=ops):
                for f in ops:
                    f(E)
            getattr(block, e)(emit)


def mkbass():
    return bass.Bass("TRN2", target_bir_lowering=False)



def emit_ln(S, x_tile, mu, rstd, sq, xnf_tile, eps=1e-5):
    v, s = "vector", "scalar"
    S.op(v, lambda E: E.reduce_sum(mu[:], x_tile, AX.X))
    S.op(v, lambda E: E.tensor_mul(sq[:], x_tile, x_tile), drain=False)
    S.op(v, lambda E: E.reduce_sum(rstd[:], sq[:], AX.X))
    S.op(v, lambda E: E.tensor_scalar_mul(mu[:], mu[:], 1.0 / 1024))
    S.op(v, lambda E: E.tensor_scalar_mul(rstd[:], rstd[:], 1.0 / 1024))
    S.op(v, lambda E: E.tensor_mul(sq[:, 0:1], mu[:], mu[:]))
    S.op(v, lambda E: E.tensor_sub(rstd[:], rstd[:], sq[:, 0:1]), inc="v_ln")
    S.wait(s, "v_ln")
    c_eps, c_mh = S.consts[eps], S.consts[-0.5]
    S.op(s, lambda E, c=c_eps: E.activation(rstd[:], rstd[:], AF.Ln, bias=c))
    S.op(s, lambda E, c=c_mh: E.activation(rstd[:], rstd[:], AF.Exp, scale=c), inc="s_ln")
    S.wait(v, "s_ln")
    S.op(v, lambda E: E.tensor_scalar(xnf_tile, x_tile, mu[:], rstd[:], ALU.subtract, ALU.mult), inc="v_xn")


def emit_quant(S, xnf, F, amax, recip, hi_scr, r_bf, m_f, xq_out, sgn_scr):
    v = "vector"
    nb = F // 16
    P = xnf.shape[0]
    xnf3 = xnf.rearrange("p (a b) -> p a b", b=16)
    mf = m_f[:, 0:F]
    mf3 = mf.rearrange("p (a b) -> p a b", b=16)
    S.op(v, lambda E: E.reduce_max(amax[:, 0:nb], xnf3, AX.X, apply_absolute_value=True))
    S.op(v, lambda E: E.reciprocal(recip[:, 0:nb], amax[:, 0:nb]))
    S.op(v, lambda E: E.tensor_scalar(recip[:, 0:nb], recip[:, 0:nb], 1e12, NUDGE6, ALU.min, ALU.mult))
    rb = recip[:, 0:nb].unsqueeze(2).broadcast_to([P, nb, 16])
    S.op(v, lambda E: E.tensor_tensor(mf3, xnf3, rb, ALU.mult), drain=False)
    xn_hi = mf.bitcast(I16).rearrange("p (a b) -> p a b", b=2)[:, :, 1:2].squeeze(2)
    hi_i = hi_scr.bitcast(I16)[:, 0:F]
    sg_i = sgn_scr.bitcast(I16)[:, 0:F]
    r_i = r_bf.bitcast(I16)[:, 0:F]
    S.op(v, lambda E: E.tensor_scalar(hi_i, xn_hi, 0x7FFF, None, ALU.bitwise_and), drain=False)
    S.op(v, lambda E: E.tensor_scalar(sg_i, xn_hi, -32768, 0x3F80, ALU.bitwise_and, ALU.bitwise_or), drain=False)
    S.op(v, lambda E: E.tensor_scalar(r_i, hi_i, 0x3F80, 0x0020, ALU.max, ALU.add), drain=False)
    S.op(v, lambda E: E.tensor_scalar(r_i, r_i, -64, None, ALU.bitwise_and), drain=False)
    S.op(v, lambda E: E.tensor_scalar(mf, hi_i, 0x3E80, None, ALU.is_ge), drain=False)
    S.op(v, lambda E: E.tensor_scalar(xq_out, hi_i, 0x3F40, None, ALU.is_ge), drain=False)
    S.op(v, lambda E: E.tensor_tensor(mf, mf, xq_out, ALU.add), drain=False)
    S.op(v, lambda E: E.scalar_tensor_tensor(mf, mf, 0.5, r_bf[:, 0:F], ALU.mult, ALU.mult), drain=False)
    S.op(v, lambda E: E.tensor_scalar_mul(amax[:, 0:nb], amax[:, 0:nb], 1.0 / 6.0))
    S.op(v, lambda E: E.tensor_mul(mf, mf, sgn_scr[:, 0:F]), drain=False)
    ab = amax[:, 0:nb].unsqueeze(2).broadcast_to([P, nb, 16])
    S.op(v, lambda E: E.tensor_tensor(xq_out.rearrange("p (a b) -> p a b", b=16), mf3, ab, ALU.mult), inc="q_done")


def emit_transposes(S, src_bf16, dst, tile_cols, ps_a, ps_b, ident, tag, evac="scalar"):
    t, v = "tensor", "vector"
    bufs = [ps_a, ps_b]
    n = len(tile_cols)
    groups = [tile_cols[i:i + 4] for i in range(0, n, 4)]
    start_v = S.val(tag + "_v")
    S.wait(t, tag + "_v", start_v)
    for gi, grp in enumerate(groups):
        ps = bufs[gi % 2]
        if gi >= 2:
            S.wait(t, tag + "_v", start_v + gi - 1)
        for j, (c0, dst_ap) in enumerate(grp):
            S.op(t, lambda E, c0=c0, j=j, ps=ps: E.transpose(ps[:, j * 128:(j + 1) * 128], src_bf16[:, c0:c0 + 128], ident[:]),
                 inc=(tag + "_t") if j == len(grp) - 1 else None)
        S.wait(evac, tag + "_t")
        for j, (c0, dst_ap) in enumerate(grp):
            last = j == len(grp) - 1
            if evac == "scalar":
                S.op("scalar", lambda E, j=j, ps=ps, dst_ap=dst_ap: E.activation(dst_ap, ps[:, j * 128:(j + 1) * 128], AF.Copy),
                     inc=(tag + "_v") if last else None, drain=last)
            else:
                S.op("vector", lambda E, j=j, ps=ps, dst_ap=dst_ap: E.tensor_copy(dst_ap, ps[:, j * 128:(j + 1) * 128]),
                     inc=(tag + "_v") if last else None, drain=last)


def emit_l1(S, nc, es, io):
    x_in = io['x_in']
    wgq = io['wgq']
    wuq = io['wuq']
    wdq = io['wdq']
    id_in = io['id_in']
    xn0_out = io['xn0_out']
    gacc_out = io['gacc_out']
    sb = lambda n_, s_, d_=F32: es.enter_context(nc.sbuf_tensor(S.pfx + n_, s_, d_))
    ps = lambda n_, s_, d_=F32: es.enter_context(nc.psum_tensor(S.pfx + n_, s_, d_))

    xtile = sb("xtile", [128, 2, D])
    xq = sb("xq", [128, D], BF16)
    xqT = sb("xqT", [128, DC, TOK], BF16)
    xn_f = sb("xn_f", [128, D])
    gacc = sb("gacc_sb", [128, NT, D])
    amax = sb("amax", [128, I_FFN // 16])
    recip = sb("recip", [128, I_FFN // 16])
    hi_scr = sb("hi_scr", [128, D], BF16)
    r_bf = sb("r_bf", [128, D], BF16)
    m_f = sb("m_f", [128, D])
    sgn = sb("sgn", [128, D], BF16)
    mu = sb("mu", [128, 1]); rstd = sb("rstd", [128, 1]); sq = sb("sq", [128, D])
    xnb = sb("xnb", [128, D], BF16)
    wg_s = sb("wg_s", [128, DC, 768], BF16)
    wu_s = sb("wu_s", [128, DC, 768], BF16)
    wd_s = sb("wd_s", [128, 6, D], BF16)
    hsil = sb("hsil", [128, 768])
    h_f = sb("h_f", [128, 768])
    hq = sb("hq", [128, 768], BF16)
    hqT = sb("hqT", [128, 6, 128], BF16)
    ident = sb("identb", [128, 128], BF16)
    ps_ga = ps("ps_ga", [128, 1024])
    ps_up = ps("ps_up", [128, 1024])
    ps_dn = ps("ps_dn", [128, D])
    ps_ta = ps("ps_ta", [128, 1024], BF16)
    ps_tb = ps("ps_tb", [128, 1024], BF16)

    S.setup_consts([1e-5, -0.5], sb)
    S.wait("scalar", "c_init")

    S.op("gpsimd", lambda E: E.dma_start(out=ident[:], in_=id_in), inc="ld_w", n=16, drain=False)
    S.dma("sync", xtile[:, 0, :], x_in[0:128, :], "ld_x")

    for t in range(NT):
        if t + 1 < NT:
            if t + 1 >= 2:
                S.wait("sync", "v_xn", t)
            S.dma("sync", xtile[:, (t + 1) % 2, :], x_in[(t + 1) * 128:(t + 2) * 128, :], "ld_x")
        S.wait("vector", "ld_x", 16 * (t + 1))
        xt = xtile[:, t % 2, :]
        emit_ln(S, xt, mu, rstd, sq, xn_f[:])
        S.wait("vector", "st_xn", S.val("st_xn"))
        S.op("vector", lambda E: E.tensor_copy(xnb[:], xn_f[:]), inc="v_xnb")
        S.wait("sync", "v_xnb")
        S.dma("sync", xn0_out[t * 128:(t + 1) * 128, :], xnb[:], "st_xn")
        S.wait("vector", "tp_v", S.val("tp_v"))
        emit_quant(S, xn_f[:], D, amax, recip, hi_scr, r_bf, m_f, xq[:], sgn)
        S.wait("tensor", "q_done")
        S.wait("tensor", "ld_w", 16)
        cols = [(k * 128, xqT[:, k, t * 128:(t + 1) * 128]) for k in range(DC)]
        emit_transposes(S, xq[:], None, cols, ps_ta, ps_tb, ident, "tp")

    for pi, (i0, iw) in enumerate(I_PHASES):
        nkd = iw // 128
        S.wait("sync", "t_gu", S.val("t_gu"))
        S.wait("sync", "t_dn", S.val("t_dn"))
        for k in range(DC):
            S.op("sync", lambda E, k=k, i0=i0, iw=iw: E.dma_start_transpose(
                wg_s[:, k, 0:iw], wgq[i0:i0 + iw, k * 128:(k + 1) * 128]), inc="ld_w", n=16, drain=False)
            S.op("sync", lambda E, k=k, i0=i0, iw=iw: E.dma_start_transpose(
                wu_s[:, k, 0:iw], wuq[i0:i0 + iw, k * 128:(k + 1) * 128]), inc="ld_w", n=16, drain=False)
        for k in range(nkd):
            S.op("sync", lambda E, k=k, i0=i0: E.dma_start_transpose(
                wd_s[:, k, :], wdq[:, i0 + k * 128:i0 + (k + 1) * 128]), inc="ld_w", n=16, drain=False)
        S.wait("tensor", "ld_w")
        for t in range(NT):
            nchunks = [(0, 512), (512, iw - 512)] if iw > 512 else [(0, iw)]
            S.wait("tensor", "v_h", S.val("v_h"))
            S.wait("tensor", "s_sil", S.val("s_sil"))
            for k in range(DC):
                lhsT = xqT[:, k, t * 128:(t + 1) * 128]
                for ci, (n0, nw) in enumerate(nchunks):
                    S.op("tensor", lambda E, lhsT=lhsT, k=k, n0=n0, nw=nw: E.matmul(
                        ps_ga[:, n0:n0 + nw], lhsT, wg_s[:, k, n0:n0 + nw], start=(k == 0), stop=(k == DC - 1)))
                for ci, (n0, nw) in enumerate(nchunks):
                    last = (k == DC - 1) and (ci == len(nchunks) - 1)
                    S.op("tensor", lambda E, lhsT=lhsT, k=k, n0=n0, nw=nw: E.matmul(
                        ps_up[:, n0:n0 + nw], lhsT, wu_s[:, k, n0:n0 + nw], start=(k == 0), stop=(k == DC - 1)),
                        inc="t_gu" if last else None)
            S.wait("scalar", "t_gu")
            S.wait("scalar", "v_h", S.val("v_h"))
            S.op("scalar", lambda E, iw=iw: E.activation(hsil[:, 0:iw], ps_ga[:, 0:iw], AF.Silu), inc="s_sil")
            S.wait("vector", "s_sil")
            S.wait("vector", "q_xn", S.val("q_xn"))
            S.op("vector", lambda E, iw=iw: E.tensor_tensor(h_f[:, 0:iw], hsil[:, 0:iw], ps_up[:, 0:iw], ALU.mult), inc="v_h")
            S.wait("vector", "tp_v", S.val("tp_v"))
            emit_quant(S, h_f[:, 0:iw], iw, amax, recip, hi_scr, r_bf, m_f, hq[:, 0:iw], sgn)
            S.wait("tensor", "q_done")
            cols = [(k * 128, hqT[:, k, :]) for k in range(nkd)]
            emit_transposes(S, hq[:], None, cols, ps_ta, ps_tb, ident, "tp")
            S.wait("tensor", "tp_v")
            S.wait("tensor", "v_acc", S.val("v_acc"))
            for k in range(nkd):
                for (n0, nw) in [(0, 512), (512, 512)]:
                    S.op("tensor", lambda E, k=k, n0=n0, nw=nw: E.matmul(
                        ps_dn[:, n0:n0 + nw], hqT[:, k, :], wd_s[:, k, n0:n0 + nw], start=(k == 0), stop=(k == nkd - 1)),
                        inc="t_dn" if (k == nkd - 1 and n0 == 512) else None)
            S.wait("vector", "t_dn")
            if pi == 0:
                S.op("vector", lambda E, t=t: E.tensor_copy(gacc[:, t, :], ps_dn[:]), inc="v_acc")
            else:
                S.op("vector", lambda E, t=t: E.tensor_add(gacc[:, t, :], gacc[:, t, :], ps_dn[:]), inc="v_acc")

    S.wait("sync", "v_acc")
    for t in range(NT):
        S.dma("sync", gacc_out[t * 128:(t + 1) * 128, :], gacc[:, t, :], "st_g")
    S.wait("sync", "st_g")
    S.wait("sync", "st_xn")


def build_launch1():
    nc = mkbass()
    x_in = nc.dram_tensor("x", [TOK, D], F32, kind="ExternalInput").ap()
    wgq = nc.dram_tensor("wgq", [I_FFN, D], BF16, kind="ExternalInput").ap()
    wuq = nc.dram_tensor("wuq", [I_FFN, D], BF16, kind="ExternalInput").ap()
    wdq = nc.dram_tensor("wdq", [D, I_FFN], BF16, kind="ExternalInput").ap()
    id_in = nc.dram_tensor("ident", [128, 128], F32, kind="ExternalInput").ap()
    xn0_out = nc.dram_tensor("xn0", [TOK, D], BF16, kind="ExternalOutput").ap()
    gacc_out = nc.dram_tensor("gacc", [TOK, D], F32, kind="ExternalOutput").ap()
    es = ExitStack()
    S = Sched(nc); S._es = es
    io = dict(x_in=x_in, wgq=wgq, wuq=wuq, wdq=wdq, id_in=id_in, xn0_out=xn0_out, gacc_out=gacc_out)
    emit_l1(S, nc, es, io)
    with es, nc.Block() as block:
        S.build(block, es)
    return nc


def emit_l0(S, nc, es, io):
    wg_in = io['wg_in']
    wu_in = io['wu_in']
    wd_in = io['wd_in']
    wg_out = io['wg_out']
    wu_out = io['wu_out']
    wd_out = io['wd_out']
    sb = lambda n_, s_, d_=F32: es.enter_context(nc.sbuf_tensor(S.pfx + n_, s_, d_))
    ps = lambda n_, s_, d_=F32: es.enter_context(nc.psum_tensor(S.pfx + n_, s_, d_))
    wf = sb("wf", [128, I_FFN])
    wq = sb("wq", [128, I_FFN], BF16)
    amax = sb("amax", [128, I_FFN // 16]); recip = sb("recip", [128, I_FFN // 16])
    hi_scr = sb("hi_scr", [128, I_FFN], BF16); r_bf = sb("r_bf", [128, I_FFN], BF16)
    m_f = sb("m_f", [128, I_FFN]); sgn = sb("sgn", [128, I_FFN], BF16)

    jobs = []
    for r0, rw in [(0, 128), (128, 128), (256, 96)]:
        jobs.append((wg_in[r0:r0 + rw, :], wg_out[r0:r0 + rw, :], rw, D))
        jobs.append((wu_in[r0:r0 + rw, :], wu_out[r0:r0 + rw, :], rw, D))
    jobs.append((wd_in, wd_out, 128, I_FFN))

    for src_ap, dst_ap, rw, F in jobs:
        S.dma("sync", wf[0:rw, 0:F], src_ap, "ld_x")
        S.wait("vector", "ld_x")
        emit_quant(S, wf[0:rw, 0:F], F, amax[0:rw], recip[0:rw], hi_scr[0:rw],
                   r_bf[0:rw], m_f[0:rw], wq[0:rw, 0:F], sgn[0:rw])
        S.wait("sync", "q_done")
        S.dma("sync", dst_ap, wq[0:rw, 0:F], "st_xn")
        S.wait("vector", "st_xn", S.val("st_xn"))
    S.wait("sync", "st_xn")


def build_launch0():
    nc = mkbass()
    wg_in = nc.dram_tensor("wg_sl", [352, D], F32, kind="ExternalInput").ap()
    wu_in = nc.dram_tensor("wu_sl", [352, D], F32, kind="ExternalInput").ap()
    wd_in = nc.dram_tensor("wd_sl", [128, I_FFN], F32, kind="ExternalInput").ap()
    wg_out = nc.dram_tensor("wgq_sl", [352, D], BF16, kind="ExternalOutput").ap()
    wu_out = nc.dram_tensor("wuq_sl", [352, D], BF16, kind="ExternalOutput").ap()
    wd_out = nc.dram_tensor("wdq_sl", [128, I_FFN], BF16, kind="ExternalOutput").ap()
    es = ExitStack()
    S = Sched(nc); S._es = es
    io = dict(wg_in=wg_in, wu_in=wu_in, wd_in=wd_in, wg_out=wg_out, wu_out=wu_out, wd_out=wd_out)
    emit_l0(S, nc, es, io)
    with es, nc.Block() as block:
        S.build(block, es)
    return nc


def emit_xnT(S, xn_dram, xnT, xnb_scr, ps_ta, ps_tb, ident, ld_sem="ld_xn"):
    for t in range(NT):
        S.dma("sync", xnb_scr[:, t % 2, :], xn_dram[t * 128:(t + 1) * 128, :], ld_sem)
    for t in range(NT):
        S.wait("tensor", ld_sem, 16 * (t + 1))
        cols = [(k * 128, xnT[:, k, t * 128:(t + 1) * 128]) for k in range(DC)]
        emit_transposes(S, xnb_scr[:, t % 2, :], None, cols, ps_ta, ps_tb, ident, "txn")


def emit_lam(S, beta_in, lnl_b, M_iota, Mp_iota, tmask, Tmat, pv2, lam128c, onesrow, cst, ps_misc, build_T=True, build_lamv=True, build_pv2=True):
    v, s, g, t = "vector", "scalar", "gpsimd", "tensor"
    beta_sb = lnl_b
    br = S.beta_row
    lamv_ap = S.lamv
    S.wait(s, "ld_beta")
    S.wait(s, "c_init")
    c_ln2 = S.consts[-LN2]
    lam_row_ap = S.lam_row
    S.op(s, lambda E, c=c_ln2, lr=lam_row_ap: E.activation(lr, br, AF.Exp, scale=c[0:1]), inc="s_lam")
    S.wait(v, "s_lam")
    S.op(v, lambda E, lr=lam_row_ap: E.tensor_scalar(lr, lr, -1.0, 1.0, ALU.mult, ALU.add))
    S.op(v, lambda E, lr=lam_row_ap: E.tensor_scalar(lr, lr, 1e-38, 1.0 - 1e-9, ALU.max, ALU.min), inc="v_lam")
    S.wait(s, "v_lam")
    S.op(s, lambda E, lr=lam_row_ap: E.activation(lr, lr, AF.Ln), inc="s_lnl")
    S.wait(t, "s_lnl")
    S.wait(t, "c_ones")
    S.op(t, lambda E, lr=lam_row_ap: E.matmul(ps_misc[:, 0:16], onesrow[:], lr, start=True, stop=True), inc="t_lnlb")
    S.wait(v, "t_lnlb")
    S.op(v, lambda E: E.tensor_copy(lnl_b[:, 0:16], ps_misc[:, 0:16]), inc="v_lnlb")
    S.wait(t, "v_lnlb")
    S.op(t, lambda E, lr=lam_row_ap: E.matmul(ps_misc[0:16, 16:17], lr, onesrow[0:1, 0:1], start=True, stop=True), inc="t_lnlc")
    S.wait(s, "t_lnlc")
    c128 = S.consts[128.0]
    S.op(s, lambda E, c=c128: E.activation(lam128c[:, 0:1], ps_misc[0:16, 16:17], AF.Exp, scale=c[0:16]), inc="s_l128")
    S.op(g, lambda E: E.iota(M_iota[:], [[1, 128]], channel_multiplier=0, allow_small_or_imprecise_dtypes=True))
    S.op(g, lambda E: E.iota(Mp_iota[:], [[0, 128]], channel_multiplier=1, allow_small_or_imprecise_dtypes=True))
    S.op(g, lambda E: E.tensor_tensor(M_iota[:], M_iota[:], Mp_iota[:], ALU.subtract), inc="g_M")
    S.wait(v, "g_M")
    S.op(v, lambda E: E.tensor_scalar(tmask[:], M_iota[:], 0.0, None, ALU.is_ge), inc="v_msk")
    if build_T:
        S.wait(s, "v_lnlb")
        S.wait(s, "g_M")
        for h in range(H):
            S.op(s, lambda E, h=h: E.activation(Tmat[:, h, :], M_iota[:], AF.Exp, scale=lnl_b[:, h:h + 1]),
                 inc="s_T" if h == H - 1 else None)
        S.wait(v, "s_T")
        S.wait(v, "v_msk")
        for h in range(H):
            S.op(v, lambda E, h=h: E.tensor_tensor(Tmat[:, h, :], Tmat[:, h, :], tmask[:], ALU.mult),
                 inc="v_T" if h == H - 1 else None)
    if build_pv2:
        S.op(g, lambda E: E.iota(Mp_iota[0:1, :], [[1, 128]], base=1, channel_multiplier=0,
                                 allow_small_or_imprecise_dtypes=True), inc="g_pv")
        S.wait(s, "g_pv")
        S.wait(s, "s_lnl")
        for h in range(H):
            S.op(s, lambda E, h=h, lr=lam_row_ap: E.activation(pv2[0:1, h * 128:(h + 1) * 128], Mp_iota[0:1, :], AF.Exp,
                                                scale=lr[0:1, h:h + 1]),
                 inc="s_pv" if h == H - 1 else None)
    if build_lamv:
        S.op(g, lambda E: E.iota(Mp_iota[:, 0:1], [[0, 1]], base=127, channel_multiplier=-1,
                                 allow_small_or_imprecise_dtypes=True), inc="g_lv")
        S.wait(s, "g_lv")
        for h in range(H):
            S.op(s, lambda E, h=h, lv=lamv_ap: E.activation(lv[:, h:h + 1], Mp_iota[:, 0:1], AF.Exp,
                                                scale=lnl_b[:, h:h + 1]),
                 inc="s_lv" if h == H - 1 else None)


def emit_l2(S, nc, es, io):
    xn_in = io['xn_in']
    wv_in = io['wv_in']
    beta_in = io['beta_in']
    id_in = io['id_in']
    st_out = io['st_out']
    sums_out = io['sums_out']
    sb = lambda n_, s_, d_=F32: es.enter_context(nc.sbuf_tensor(S.pfx + n_, s_, d_))
    ps = lambda n_, s_, d_=F32: es.enter_context(nc.psum_tensor(S.pfx + n_, s_, d_))
    xnb = sb("xnb", [128, 2, D], BF16)
    xnT = sb("xnT", [128, DC, TOK], BF16)
    wv_s = sb("wv_s", [128, DC, D], BF16)
    vbuf = sb("vbuf", [128, NT, D], BF16)
    stb = sb("stb", [128, D], BF16)
    sumrow = sb("sumrow", [1, NT * D])
    ident = sb("identb", [128, 128], BF16)
    onesrow = sb("onesrow", [1, 128])
    lnl_b = sb("lnl_b", [128, 16]); M_iota = sb("M_iota", [128, 128]); Mp_iota = sb("Mp_iota", [128, 128])
    tmask = sb("tmask", [128, 128]); Tmat = sb("Tmat", [128, H, 128], BF16)
    pv2 = sb("pv2", [1, H * 128], BF16); lam128c = sb("lam128c", [16, 1])
    lamv = sb("lamv", [128, H], BF16)
    beta_row = sb("beta_row", [1, H]); lam_row = sb("lam_row", [1, H])
    ps_ta = ps("ps_ta", [128, 1024], BF16)
    ps_tb = ps("ps_tb", [128, 1024], BF16)
    ps_v = ps("ps_v", [128, D])
    ps_sc = ps("ps_sc", [128, D])
    ps_sum = ps("ps_sum", [1, D])

    S.setup_consts([-LN2, 128.0], sb)
    S.beta_row = beta_row[:]; S.lam_row = lam_row[:]; S.lamv = lamv[:]

    S.op("gpsimd", lambda E: E.dma_start(out=ident[:], in_=id_in), inc="ld_w", n=16, drain=False)
    S.op("gpsimd", lambda E: E.memset(onesrow[:], 1.0), inc="c_ones")
    S.dma("sync", beta_row[:], beta_in, "ld_beta")
    S.dma("gpsimd", wv_s[:].rearrange("p c d -> (c p) d") if False else wv_s[:, 0, :], wv_in[0:128, :], "ld_w")
    for k in range(1, DC):
        S.dma("gpsimd", wv_s[:, k, :], wv_in[k * 128:(k + 1) * 128, :], "ld_w")
    emit_lam(S, beta_in, lnl_b, M_iota, Mp_iota, tmask, Tmat, pv2, lam128c, onesrow, None, ps_v)

    for t in range(NT):
        if t >= 2:
            S.wait("sync", "tp_t", 2 * (t - 1))
        S.dma("sync", xnb[:, t % 2, :], xn_in[t * 128:(t + 1) * 128, :], "ld_xn")
        S.wait("tensor", "ld_xn", 16 * (t + 1))
        S.wait("tensor", "ld_w", 16)
        cols = [(k * 128, xnT[:, k, t * 128:(t + 1) * 128]) for k in range(DC)]
        emit_transposes(S, xnb[:, t % 2, :], None, cols, ps_ta, ps_tb, ident, "tp")
        S.wait("tensor", "tp_v", 2 * (t + 1))
        S.wait("tensor", "ld_w", 16 * 9)
        S.wait("tensor", "s_l128", S.val("s_l128"))
        S.wait("tensor", "v_vev", S.val("v_vev"))
        for k in range(DC):
            for n0 in (0, 512):
                S.op("tensor", lambda E, k=k, t=t, n0=n0: E.matmul(
                    ps_v[:, n0:n0 + 512], xnT[:, k, t * 128:(t + 1) * 128], wv_s[:, k, n0:n0 + 512],
                    start=(k == 0), stop=(k == DC - 1)), inc="t_v" if (k == DC - 1 and n0 == 512) else None)
        S.wait("vector", "t_v")
        S.op("vector", lambda E, t=t: E.tensor_copy(vbuf[:, t, :], ps_v[:]), inc="v_vev")

    S.wait("tensor", "v_T")
    S.wait("tensor", "s_lv")
    for t in range(NT):
        S.wait("tensor", "v_vev", min(S.val("v_vev"), t + 1))
        S.wait("tensor", "v_stev", S.val("v_stev"))
        S.wait("tensor", "v_smev", S.val("v_smev"))
        for h in range(H):
            S.op("tensor", lambda E, t=t, h=h: E.matmul(
                ps_sc[:, h * DH:(h + 1) * DH], Tmat[:, h, :], vbuf[:, t, h * DH:(h + 1) * DH],
                start=True, stop=True), inc="t_sc" if h == H - 1 else None)
            S.op("tensor", lambda E, t=t, h=h: E.matmul(
                ps_sum[0:1, h * DH:(h + 1) * DH], lamv[:, h:h + 1], vbuf[:, t, h * DH:(h + 1) * DH],
                start=True, stop=True), inc="t_sm" if h == H - 1 else None)
        S.wait("vector", "t_sc")
        S.wait("vector", "st_st", S.val("st_st"))
        S.op("vector", lambda E, t=t: E.tensor_copy(stb[:], ps_sc[:]), inc="v_stev")
        S.wait("sync", "v_stev")
        S.dma("sync", st_out[t * 128:(t + 1) * 128, :], stb[:], "st_st")
        S.wait("vector", "t_sm")
        S.op("vector", lambda E, t=t: E.tensor_copy(sumrow[0:1, t * D:(t + 1) * D], ps_sum[0:1, :]), inc="v_smev")
    S.wait("sync", "v_smev")
    S.dma("sync", sums_out.rearrange("t d -> (t d)").unsqueeze(0), sumrow[:], "st_sm")
    S.wait("sync", "st_sm")
    S.wait("sync", "st_st")



def build_launch2():
    nc = mkbass()
    xn_in = nc.dram_tensor("xn0", [TOK, D], BF16, kind="ExternalInput").ap()
    wv_in = nc.dram_tensor("WvT", [D, D], F32, kind="ExternalInput").ap()
    beta_in = nc.dram_tensor("beta", [1, H], F32, kind="ExternalInput").ap()
    id_in = nc.dram_tensor("ident", [128, 128], F32, kind="ExternalInput").ap()
    st_out = nc.dram_tensor("state_local", [TOK, D], BF16, kind="ExternalOutput").ap()
    sums_out = nc.dram_tensor("sums", [NT, D], F32, kind="ExternalOutput").ap()
    es = ExitStack()
    S = Sched(nc); S._es = es
    io = dict(xn_in=xn_in, wv_in=wv_in, beta_in=beta_in, id_in=id_in, st_out=st_out, sums_out=sums_out)
    emit_l2(S, nc, es, io)
    with es, nc.Block() as block:
        S.build(block, es)
    return nc


def emit_l3(S, nc, es, io):
    xn_in = io['xn_in']
    stl_in = io['stl_in']
    sums_in = io['sums_in']
    x_in = io['x_in']
    gacc_in = io['gacc_in']
    wq_in = io['wq_in']
    wg_in = io['wg_in']
    wo_in = io['wo_in']
    beta_in = io['beta_in']
    gam_in = io['gam_in']
    bet_in = io['bet_in']
    id_in = io['id_in']
    cidx_in = io['cidx_in']
    x1_out = io['x1_out']
    carr_dram = io['carr_dram']
    carrmy_dram = io['carrmy_dram']
    sb = lambda n_, s_, d_=F32: es.enter_context(nc.sbuf_tensor(S.pfx + n_, s_, d_))
    ps = lambda n_, s_, d_=F32: es.enter_context(nc.psum_tensor(S.pfx + n_, s_, d_))
    xnb = sb("xnb", [128, 2, D], BF16)
    xnT_t = sb("xnT_t", [128, DC, 128], BF16)
    wq_s = sb("wq_s", [128, DC, D], BF16)
    wg_s = sb("wg_s", [128, DC, D], BF16)
    wo_s = sb("wo_s", [128, DC, D], BF16)
    ident = sb("identb", [128, 128], BF16)
    onesrow = sb("onesrow", [1, 128])
    lnl_b = sb("lnl_b", [128, 16]); M_iota = sb("M_iota", [128, 128]); Mp_iota = sb("Mp_iota", [128, 128])
    tmask = sb("tmask", [128, 128])
    pv2 = sb("pv2", [1, H * 128], BF16); lam128c = sb("lam128c", [16, 1])
    beta_row = sb("beta_row", [1, H]); lam_row = sb("lam_row", [1, H])
    sums_sb = sb("sums_sb", [16, 2 * NT * DH])
    carr_sb = sb("carr_sb", [16, 2 * NT * DH])
    carrg = sb("carrg", [NT, D])
    carr1p = sb("carr1p", [1, NT * D], BF16)
    gbrow = sb("gbrow", [1, 2 * D])
    cidx = sb("cidx", [NT, 1], I32)
    gb = sb("gb", [128, D]); bb = sb("bb", [128, D])
    stl_b = sb("stl_b", [128, D], BF16)
    stf = sb("stf", [128, D])
    ret_f = sb("ret_f", [128, D])
    y_f = sb("y_f", [128, D])
    g0_f = sb("g0_f", [128, D])
    u_b = sb("u_b", [128, D], BF16)
    uT = sb("uT", [128, DC, 128], BF16)
    xt = sb("xt", [128, D]); gat = sb("gat", [128, D]); gsil = sb("gsil", [128, D])
    x1_b = sb("x1_b", [128, D])
    mu = sb("mu", [128, 1]); rstd = sb("rstd", [128, 1]); sq = sb("sq", [128, D])
    ps_c = ps("ps_c", [128, D])
    ps_q = ps("ps_q", [128, D])
    ps_g = ps("ps_g", [128, D])
    ps_ta = ps("ps_ta", [128, 1024], BF16)
    ps_tb = ps("ps_tb", [128, 1024], BF16)

    S.setup_consts([-LN2, 128.0, 1e-5, -0.5], sb)
    S.wait("scalar", "c_init")
    S.beta_row = beta_row[:]; S.lam_row = lam_row[:]; S.lamv = None

    S.op("gpsimd", lambda E: E.dma_start(out=ident[:], in_=id_in), inc="ld_w", n=16, drain=False)
    S.op("gpsimd", lambda E: E.memset(onesrow[:], 1.0), inc="c_ones")
    S.dma("sync", beta_row[:], beta_in, "ld_beta")
    S.dma("sync", cidx[:], cidx_in, "ld_misc")
    for name, src_ap, dst in [("wq", wq_in, wq_s), ("wg", wg_in, wg_s), ("wo", wo_in, wo_s)]:
        for k in range(DC):
            S.dma("gpsimd", dst[:, k, :], src_ap[k * 128:(k + 1) * 128, :], "ld_w")
    emit_lam(S, beta_in, lnl_b, M_iota, Mp_iota, tmask, None, pv2, lam128c, onesrow, None, ps_q,
             build_T=False, build_lamv=False)
    S.dma("sync", gbrow[0:1, 0:D], gam_in, "ld_misc")
    S.dma("sync", gbrow[0:1, D:2 * D], bet_in, "ld_misc")
    S.wait("tensor", "ld_misc", 16 * 3)
    S.wait("tensor", "c_ones")
    for n0 in (0, 512):
        S.op("tensor", lambda E, n0=n0: E.matmul(ps_c[:, n0:n0 + 512], onesrow[:], gbrow[0:1, n0:n0 + 512], start=True, stop=True))
    for n0 in (0, 512):
        S.op("tensor", lambda E, n0=n0: E.matmul(ps_g[:, n0:n0 + 512], onesrow[:], gbrow[0:1, D + n0:D + n0 + 512], start=True, stop=True),
             inc="t_gbb" if n0 == 512 else None)
    S.wait("vector", "t_gbb")
    S.op("vector", lambda E: E.tensor_copy(gb[:], ps_c[:]))
    S.op("vector", lambda E: E.tensor_copy(bb[:], ps_g[:]), inc="v_gbb")

    S.dma("sync", sums_sb[:].rearrange("h (c d) -> h c d", d=DH),
          sums_in.rearrange("c (h d) -> h c d", d=DH), "ld_sums")
    S.wait("vector", "ld_sums")
    S.wait("vector", "s_l128")
    S.op("vector", lambda E: E.memset(carr_sb[:, 0:DH], 0.0))
    for c in range(1, 2 * NT):
        S.op("vector", lambda E, c=c: E.scalar_tensor_tensor(
            carr_sb[:, c * DH:(c + 1) * DH], carr_sb[:, (c - 1) * DH:c * DH], lam128c[:, 0:1],
            sums_sb[:, (c - 1) * DH:c * DH], ALU.mult, ALU.add),
            inc="v_chain" if c == 2 * NT - 1 else None)
    S.wait("sync", "v_chain")
    S.dma("sync", carr_dram.rearrange("c (h d) -> h c d", d=DH),
          carr_sb[:].rearrange("h (c d) -> h c d", d=DH), "st_carr")
    S.wait("gpsimd", "st_carr")
    S.wait("gpsimd", "ld_misc", 16)
    S.op("gpsimd", lambda E: E.indirect_dma_start(carrg[:], None, carr_dram,
                                                  bass.IndirectOffsetOnAxis(cidx[:], 0)),
         inc="g_carr", n=16, drain=False)
    S.wait("sync", "g_carr")
    S.dma("sync", carrmy_dram, carrg[:], "st_carr")
    S.wait("sync", "st_carr", 32)
    S.wait("gpsimd", "st_carr", 32)
    S.op("gpsimd", lambda E: E.dma_start(out=carr1p[:], in_=carrmy_dram.rearrange("c d -> (c d)").unsqueeze(0)),
         inc="ld_carr", n=16, drain=False)

    for t in range(NT):
        if t >= 2:
            S.wait("sync", "tp_t", 4 * (t - 2) + 2)
        S.dma("sync", xnb[:, t % 2, :], xn_in[t * 128:(t + 1) * 128, :], "ld_xn")
        S.wait("tensor", "ld_xn", 16 * (t + 1))
        S.wait("tensor", "ld_w", 16)
        S.wait("tensor", "s_l128", 1)
        cols = [(k * 128, xnT_t[:, k, :]) for k in range(DC)]
        S.wait("tensor", "t_q", S.val("t_q"))
        emit_transposes(S, xnb[:, t % 2, :], None, cols, ps_ta, ps_tb, ident, "tp")
        txn_v_done = S.val("tp_v")
        S.wait("tensor", "ld_carr")
        S.wait("tensor", "s_pv")
        S.wait("tensor", "v_ret", S.val("v_ret"))
        S.wait("tensor", "v_gbb")
        for h in range(H):
            S.op("tensor", lambda E, t=t, h=h: E.matmul(
                ps_c[:, h * DH:(h + 1) * DH], pv2[0:1, h * 128:(h + 1) * 128],
                carr1p[0:1, t * D + h * DH:t * D + (h + 1) * DH], start=True, stop=True),
                inc="t_corr" if h == H - 1 else None)
        S.wait("tensor", "tp_v", txn_v_done)
        S.wait("tensor", "ld_w", 16 * (1 + 3 * DC))
        S.wait("tensor", "v_ret", S.val("v_ret"))
        S.wait("tensor", "s_g0", S.val("s_g0"))
        for k in range(DC):
            for n0 in (0, 512):
                S.op("tensor", lambda E, k=k, n0=n0: E.matmul(
                    ps_q[:, n0:n0 + 512], xnT_t[:, k, :], wq_s[:, k, n0:n0 + 512],
                    start=(k == 0), stop=(k == DC - 1)))
        for k in range(DC):
            for n0 in (0, 512):
                S.op("tensor", lambda E, k=k, n0=n0: E.matmul(
                    ps_g[:, n0:n0 + 512], xnT_t[:, k, :], wg_s[:, k, n0:n0 + 512],
                    start=(k == 0), stop=(k == DC - 1)),
                    inc="t_q" if (k == DC - 1 and n0 == 512) else None)
        S.dma("sync", stl_b[:], stl_in[t * 128:(t + 1) * 128, :], "ld_stl")
        S.wait("vector", "ld_stl", 16 * (t + 1))
        S.wait("vector", "t_corr")
        S.op("vector", lambda E: E.tensor_tensor(stf[:], stl_b[:], ps_c[:], ALU.add))
        S.wait("vector", "t_q")
        S.op("vector", lambda E: E.tensor_tensor(ret_f[:], ps_q[:], stf[:], ALU.mult), inc="v_ret")
        emit_ln(S, ret_f[:], mu, rstd, sq, y_f[:])
        S.op("vector", lambda E: E.tensor_mul(y_f[:], y_f[:], gb[:]))
        S.op("vector", lambda E: E.tensor_add(y_f[:], y_f[:], bb[:]), inc="v_y")
        S.wait("scalar", "t_q")
        S.wait("scalar", "v_u", S.val("v_u"))
        S.op("scalar", lambda E: E.activation(g0_f[:], ps_g[:], AF.Silu), inc="s_g0")
        S.wait("vector", "s_g0")
        S.wait("vector", "tp_t", S.val("tp_t"))
        S.op("vector", lambda E: E.tensor_tensor(u_b[:], y_f[:], g0_f[:], ALU.mult), inc="v_u")
        S.wait("tensor", "v_u")
        colsu = [(k * 128, uT[:, k, :]) for k in range(DC)]
        emit_transposes(S, u_b[:], None, colsu, ps_ta, ps_tb, ident, "tp")
        S.wait("tensor", "tp_v")
        S.wait("tensor", "v_x1", S.val("v_x1"))
        for k in range(DC):
            for n0 in (0, 512):
                S.op("tensor", lambda E, k=k, n0=n0: E.matmul(
                    ps_q[:, n0:n0 + 512], uT[:, k, :], wo_s[:, k, n0:n0 + 512],
                    start=(k == 0), stop=(k == DC - 1)),
                    inc="t_s" if (k == DC - 1 and n0 == 512) else None)
        S.wait("sync", "v_x1", S.val("v_x1"))
        S.dma("sync", xt[:], x_in[t * 128:(t + 1) * 128, :], "ld_x")
        S.dma("sync", gat[:], gacc_in[t * 128:(t + 1) * 128, :], "ld_x")
        S.wait("scalar", "ld_x", 32 * (t + 1))
        S.op("scalar", lambda E: E.activation(gsil[:], gat[:], AF.Silu), inc="s_gs")
        S.wait("vector", "s_gs")
        S.wait("vector", "t_s")
        S.wait("vector", "st_x1", S.val("st_x1"))
        S.op("vector", lambda E: E.tensor_tensor(x1_b[:], ps_q[:], gsil[:], ALU.mult))
        S.op("vector", lambda E: E.tensor_add(x1_b[:], x1_b[:], xt[:]), inc="v_x1")
        S.wait("sync", "v_x1")
        S.dma("sync", x1_out[t * 128:(t + 1) * 128, :], x1_b[:], "st_x1")
    S.wait("sync", "st_x1")



def build_launch3():
    nc = mkbass()
    xn_in = nc.dram_tensor("xn0", [TOK, D], BF16, kind="ExternalInput").ap()
    stl_in = nc.dram_tensor("state_local", [TOK, D], BF16, kind="ExternalInput").ap()
    sums_in = nc.dram_tensor("sums_all", [2 * NT, D], F32, kind="ExternalInput").ap()
    x_in = nc.dram_tensor("x", [TOK, D], F32, kind="ExternalInput").ap()
    gacc_in = nc.dram_tensor("gacc", [TOK, D], F32, kind="ExternalInput").ap()
    wq_in = nc.dram_tensor("WqT", [D, D], F32, kind="ExternalInput").ap()
    wg_in = nc.dram_tensor("WgT", [D, D], F32, kind="ExternalInput").ap()
    wo_in = nc.dram_tensor("WoT", [D, D], F32, kind="ExternalInput").ap()
    beta_in = nc.dram_tensor("beta", [1, H], F32, kind="ExternalInput").ap()
    gam_in = nc.dram_tensor("ln_gamma", [1, D], F32, kind="ExternalInput").ap()
    bet_in = nc.dram_tensor("ln_beta", [1, D], F32, kind="ExternalInput").ap()
    id_in = nc.dram_tensor("ident", [128, 128], F32, kind="ExternalInput").ap()
    cidx_in = nc.dram_tensor("carry_idx", [NT, 1], I32, kind="ExternalInput").ap()
    x1_out = nc.dram_tensor("x1", [TOK, D], F32, kind="ExternalOutput").ap()
    carr_dram = nc.dram_tensor("carr_dram", [2 * NT, D], F32).ap()
    carrmy_dram = nc.dram_tensor("carrmy_dram", [NT, D], F32).ap()
    es = ExitStack()
    S = Sched(nc); S._es = es
    io = dict(xn_in=xn_in, stl_in=stl_in, sums_in=sums_in, x_in=x_in, gacc_in=gacc_in, wq_in=wq_in, wg_in=wg_in, wo_in=wo_in, beta_in=beta_in, gam_in=gam_in, bet_in=bet_in, id_in=id_in, cidx_in=cidx_in, x1_out=x1_out, carr_dram=carr_dram, carrmy_dram=carrmy_dram)
    emit_l3(S, nc, es, io)
    with es, nc.Block() as block:
        S.build(block, es)
    return nc


def emit_rms(S, x_tile, rstd, sq, wb_tile, out_bf, eps=1e-6):
    v, s = "vector", "scalar"
    S.op(v, lambda E: E.tensor_mul(sq[:], x_tile, x_tile), drain=False)
    S.op(v, lambda E: E.reduce_sum(rstd[:], sq[:], AX.X))
    S.op(v, lambda E: E.tensor_scalar_mul(rstd[:], rstd[:], 1.0 / 1024), inc="v_rms")
    S.wait(s, "v_rms")
    c_eps, c_mh = S.consts[eps], S.consts[-0.5]
    S.op(s, lambda E, c=c_eps: E.activation(rstd[:], rstd[:], AF.Ln, bias=c))
    S.op(s, lambda E, c=c_mh: E.activation(rstd[:], rstd[:], AF.Exp, scale=c), inc="s_rms")
    S.wait(v, "s_rms")
    S.op(v, lambda E: E.tensor_scalar(sq[:], x_tile, rstd[:], None, ALU.mult), drain=False)
    S.op(v, lambda E: E.tensor_tensor(out_bf, sq[:], wb_tile, ALU.mult), inc="v_xn")


def emit_bcast_row(S, row_ap, onesrow, ps_full, dst, tag):
    for n0 in (0, 512):
        S.op("tensor", lambda E, n0=n0: E.matmul(ps_full[:, n0:n0 + 512], onesrow[:], row_ap[0:1, n0:n0 + 512],
                                                 start=True, stop=True), inc=tag + "_t" if n0 == 512 else None)
    S.wait("vector", tag + "_t")
    S.op("vector", lambda E: E.tensor_copy(dst, ps_full[:]), inc=tag + "_v")


def emit_l4(S, nc, es, io):
    x1_in = io['x1_in']
    n1_in = io['n1_in']
    ws_in = io['ws_in']
    we_in = io['we_in']
    bias_in = io['bias_in']
    id_in = io['id_in']
    xn1_out = io['xn1_out']
    xnr1_out = io['xnr1_out']
    g1_out = io['g1_out']
    sb = lambda n_, s_, d_=F32: es.enter_context(nc.sbuf_tensor(S.pfx + n_, s_, d_))
    ps = lambda n_, s_, d_=F32: es.enter_context(nc.psum_tensor(S.pfx + n_, s_, d_))
    xt = sb("xt", [128, 2, D])
    wc = sb("wc", [128, DC, D], BF16)
    wscr = sb("wscr", [128, 2, D])
    ident = sb("identb", [128, 128], BF16)
    onesrow = sb("onesrow", [1, 128])
    n1b = sb("n1b", [128, D]); biasb = sb("biasb", [128, D])
    rowscr = sb("rowscr", [1, 2 * D])
    xnb = sb("xnb", [128, D], BF16)
    xnrb = sb("xnrb", [128, D], BF16)
    mu = sb("mu", [128, 1])
    xnT_t = sb("xnT_t", [128, DC, 128], BF16)
    gf = sb("gf", [128, D])
    rstd = sb("rstd", [128, 1]); sq = sb("sq", [128, D])
    ps_a = ps("ps_a", [128, 1024])
    ps_b = ps("ps_b", [128, 1024])
    ps_ta = ps("ps_ta", [128, 1024], BF16)
    ps_tb = ps("ps_tb", [128, 1024], BF16)

    S.setup_consts([1e-6, -0.5, 1e-5], sb)
    S.wait("scalar", "c_init")
    S.op("gpsimd", lambda E: E.dma_start(out=ident[:], in_=id_in), inc="ld_w", n=16, drain=False)
    S.op("gpsimd", lambda E: E.memset(onesrow[:], 1.0), inc="c_ones")
    for k in range(DC):
        S.dma("sync", wscr[:, 0, :], ws_in[k * 128:(k + 1) * 128, :], "ld_stl")
        S.dma("sync", wscr[:, 1, :], we_in[k * 128:(k + 1) * 128, :], "ld_stl")
        S.wait("vector", "ld_stl", 32 * (k + 1))
        S.op("vector", lambda E, k=k: E.tensor_add(wc[:, k, :], wscr[:, 0, :], wscr[:, 1, :]), inc="v_wc")
        if k < DC - 1:
            S.wait("sync", "v_wc")
    S.dma("sync", rowscr[0:1, 0:D], n1_in, "ld_misc")
    S.dma("sync", rowscr[0:1, D:2 * D], bias_in, "ld_misc")
    S.wait("tensor", "ld_misc")
    S.wait("tensor", "c_ones")
    emit_bcast_row(S, rowscr[0:1, 0:D], onesrow, ps_a, n1b[:], "bc1")
    S.wait("tensor", "bc1_v")
    emit_bcast_row(S, rowscr[0:1, D:2 * D], onesrow, ps_a, biasb[:], "bc2")

    for t in range(NT):
        if t >= 2:
            S.wait("sync", "v_xn", t - 1)
        S.dma("sync", xt[:, t % 2, :], x1_in[t * 128:(t + 1) * 128, :], "ld_x")
        S.wait("vector", "ld_x", 16 * (t + 1))
        S.wait("vector", "bc1_v"); S.wait("vector", "bc2_v")
        S.wait("vector", "tp_t", S.val("tp_t"))
        S.wait("vector", "st_xn", S.val("st_xn"))
        emit_ln(S, xt[:, t % 2, :], mu, rstd, sq, sq[:], eps=1e-5)
        S.op("vector", lambda E: E.tensor_copy(xnb[:], sq[:]), inc="v_xnb")
        emit_rms(S, xnb[:], rstd, sq, n1b[:], xnrb[:])
        S.wait("sync", "v_xn")
        S.dma("sync", xn1_out[t * 128:(t + 1) * 128, :], xnb[:], "st_xn")
        S.dma("sync", xnr1_out[t * 128:(t + 1) * 128, :], xnrb[:], "st_xn")
        S.wait("tensor", "v_xn", S.val("v_xn"))
        S.wait("tensor", "ld_w", 16)
        cols = [(k * 128, xnT_t[:, k, :]) for k in range(DC)]
        S.wait("tensor", "t_g", S.val("t_g"))
        emit_transposes(S, xnb[:], None, cols, ps_ta, ps_tb, ident, "tp")
        S.wait("tensor", "tp_v")
        S.wait("tensor", "v_wc", DC)
        S.wait("tensor", "v_g1", S.val("v_g1"))
        for k in range(DC):
            for n0 in (0, 512):
                S.op("tensor", lambda E, k=k, n0=n0: E.matmul(
                    ps_b[:, n0:n0 + 512], xnT_t[:, k, :], wc[:, k, n0:n0 + 512],
                    start=(k == 0), stop=(k == DC - 1)),
                    inc="t_g" if (k == DC - 1 and n0 == 512) else None)
        S.wait("vector", "t_g")
        S.wait("scalar", "v_g1", S.val("v_g1"))
        S.op("vector", lambda E: E.tensor_tensor(gf[:], ps_b[:], biasb[:], ALU.add), inc="v_gb")
        S.wait("scalar", "v_gb")
        S.op("scalar", lambda E: E.activation(gf[:], gf[:], AF.Silu), inc="s_g1")
        S.wait("sync", "s_g1")
        S.dma("sync", g1_out[t * 128:(t + 1) * 128, :], gf[:], "st_g1")
        S.wait("vector", "st_g1", S.val("st_g1"))
        S.op("vector", lambda E: E.engine_nop(), inc="v_g1", drain=False)
    S.wait("sync", "st_g1")
    S.wait("sync", "st_xn")



def build_launch4():
    nc = mkbass()
    x1_in = nc.dram_tensor("x1", [TOK, D], F32, kind="ExternalInput").ap()
    n1_in = nc.dram_tensor("norm1_w", [1, D], F32, kind="ExternalInput").ap()
    ws_in = nc.dram_tensor("WsT", [D, D], F32, kind="ExternalInput").ap()
    we_in = nc.dram_tensor("WeT", [D, D], F32, kind="ExternalInput").ap()
    bias_in = nc.dram_tensor("bias", [1, D], F32, kind="ExternalInput").ap()
    id_in = nc.dram_tensor("ident", [128, 128], F32, kind="ExternalInput").ap()
    xn1_out = nc.dram_tensor("xn1", [TOK, D], BF16, kind="ExternalOutput").ap()
    xnr1_out = nc.dram_tensor("xnr1", [TOK, D], BF16, kind="ExternalOutput").ap()
    g1_out = nc.dram_tensor("g1", [TOK, D], F32, kind="ExternalOutput").ap()
    es = ExitStack()
    S = Sched(nc); S._es = es
    io = dict(x1_in=x1_in, n1_in=n1_in, ws_in=ws_in, we_in=we_in, bias_in=bias_in, id_in=id_in, xn1_out=xn1_out, xnr1_out=xnr1_out, g1_out=g1_out)
    emit_l4(S, nc, es, io)
    with es, nc.Block() as block:
        S.build(block, es)
    return nc


NTF = 32


def emit_rope_qknorm(S, src_ps, nheads, ct, st, qf, qr, sqs, rstd8, out_bf, eps=1e-6):
    v, s = "vector", "scalar"
    W = nheads * DH
    S.op(v, lambda E: E.tensor_copy(qf[:, 0:W], src_ps), inc="v_qev")
    qe = qf[:, 0:W].rearrange("p (h f two) -> p h f two", h=nheads, two=2)[:, :, :, 0:1].squeeze(3)
    qo = qf[:, 0:W].rearrange("p (h f two) -> p h f two", h=nheads, two=2)[:, :, :, 1:2].squeeze(3)
    re = qr[:, 0:W].rearrange("p (h f two) -> p h f two", h=nheads, two=2)[:, :, :, 0:1].squeeze(3)
    im = qr[:, 0:W].rearrange("p (h f two) -> p h f two", h=nheads, two=2)[:, :, :, 1:2].squeeze(3)
    cb = ct.unsqueeze(1).broadcast_to([128, nheads, 32])
    sb_ = st.unsqueeze(1).broadcast_to([128, nheads, 32])
    s1 = sqs[:, 0:W // 2].rearrange("p (h f) -> p h f", h=nheads)
    S.op(v, lambda E: E.tensor_tensor(re, qe, cb, ALU.mult), drain=False)
    S.op(v, lambda E: E.tensor_tensor(s1, qo, sb_, ALU.mult), drain=False)
    S.op(v, lambda E: E.tensor_tensor(re, re, s1, ALU.subtract), drain=False)
    S.op(v, lambda E: E.tensor_tensor(im, qe, sb_, ALU.mult), drain=False)
    S.op(v, lambda E: E.tensor_tensor(s1, qo, cb, ALU.mult), drain=False)
    S.op(v, lambda E: E.tensor_tensor(im, im, s1, ALU.add), drain=False)
    S.op(v, lambda E: E.tensor_mul(sqs[:, 0:W], qr[:, 0:W], qr[:, 0:W]), drain=False)
    S.op(v, lambda E: E.tensor_reduce(rstd8[:, 0:nheads], sqs[:, 0:W].rearrange("p (h f) -> p h f", h=nheads),
                                      AX.X, ALU.add))
    S.op(v, lambda E: E.tensor_scalar_mul(rstd8[:, 0:nheads], rstd8[:, 0:nheads], 1.0 / DH), inc="v_qn")
    S.wait(s, "v_qn")
    c_eps, c_mh = S.consts[eps], S.consts[-0.5]
    S.op(s, lambda E, c=c_eps: E.activation(rstd8[:, 0:nheads], rstd8[:, 0:nheads], AF.Ln, bias=c))
    S.op(s, lambda E, c=c_mh: E.activation(rstd8[:, 0:nheads], rstd8[:, 0:nheads], AF.Exp, scale=c), inc="s_qn")
    S.wait(v, "s_qn")
    rb = rstd8[:, 0:nheads].unsqueeze(2).broadcast_to([128, nheads, DH])
    S.op(v, lambda E: E.tensor_tensor(out_bf.rearrange("p (h f) -> p h f", h=nheads),
                                      qr[:, 0:W].rearrange("p (h f) -> p h f", h=nheads), rb, ALU.mult),
         inc="v_qb")


def emit_l5(S, nc, es, io, local):
    nmask = 6 if local else 4
    xn_in = io['xn_in']; wq_in = io['wq_in']; cos_in = io['cos_in']; sin_in = io['sin_in']
    mask_in = io['mask_in']; id_in = io['id_in']; mg_out = io['mg_out']
    if not local:
        kvd_in = io['kvd_in']; ku_in = io['ku_in']; vu_in = io['vu_in']
    else:
        wk_in = io['wk_in']; wv_in = io['wv_in']
    xn_idx = io.get('xn_idx')
    sb = lambda n_, s_, d_=F32: es.enter_context(nc.sbuf_tensor(S.pfx + n_, s_, d_))
    ps = lambda n_, s_, d_=F32: es.enter_context(nc.psum_tensor(S.pfx + n_, s_, d_))
    xnb = sb("xnb", [128, 2, D], BF16)
    xnT_t = sb("xnT_t", [128, DC, 128], BF16)
    wq_s = sb("wq_s", [128, DC, 512], BF16)
    if not local:
        kvd_s = sb("kvd_s", [128, DC, 256], BF16)
        ku_s = sb("ku_s", [128, 2, 128], BF16)
        vu_s = sb("vu_s", [128, 2, 128], BF16)
    else:
        wk_s = sb("wk_s", [128, DC, 128], BF16)
        wv_s = sb("wv_s", [128, DC, 128], BF16)
    qT = sb("qT", [128, 4, 2 * TOK], BF16)
    kT = sb("kT", [128, 2 * TOK], BF16)
    vb = sb("vb", [128, NTF, 128], BF16)
    merged = sb("merged", [128, 4, 2 * TOK], BF16)
    mask_s = sb("mask_s", [128, nmask, 512], BF16)
    ident = sb("identb", [128, 128], BF16)
    onesrow = sb("onesrow", [1, 128])
    onesp = sb("onesp", [64, 128])
    onescol_b = sb("onescol_b", [128, 1], BF16)
    ctile = sb("ctile", [128, 2, 32]); stile = sb("stile", [128, 2, 32])
    qf = sb("qf", [128, 512]); qr = sb("qr", [128, 512]); sqs = sb("sqs", [128, 512])
    rstd8 = sb("rstd8", [128, 8])
    latb = sb("latb", [128, 256], BF16)
    latT_t = sb("latT_t", [128, 2, 128], BF16)
    kb = sb("kb", [128, 128], BF16)
    pAB = sb("pAB", [128, 3, 1024], BF16)
    rs_sb = sb("rs_sb", [128, 512])
    rs2 = sb("rs2", [128, 512])
    P1 = ps("P1", [128, 1024])
    P2 = ps("P2", [128, 1024])
    P3 = ps("P3", [128, 1024])
    es_proj = ExitStack()
    psp = lambda n_, s_, d_=F32: es_proj.enter_context(nc.psum_tensor(S.pfx + n_, s_, d_))
    ps_ta = psp("ps_ta", [128, 1024], BF16)
    ps_tb = psp("ps_tb", [128, 1024], BF16)

    S.setup_consts([1e-6, -0.5], sb)
    S.wait("scalar", "c_init")
    S.op("gpsimd", lambda E: E.dma_start(out=ident[:], in_=id_in), inc="ld_w", n=16, drain=False)
    S.op("gpsimd", lambda E: E.memset(onesrow[:], 1.0), inc="c_ones")
    S.op("gpsimd", lambda E: E.memset(onesp[:], 1.0), inc="c_ones")
    S.op("gpsimd", lambda E: E.memset(onescol_b[:], 1.0), inc="c_ones")
    for k in range(DC):
        S.dma("gpsimd", wq_s[:, k, :], wq_in[k * 128:(k + 1) * 128, :], "ld_w")
    if not local:
        for k in range(DC):
            S.dma("gpsimd", kvd_s[:, k, :], kvd_in[k * 128:(k + 1) * 128, :], "ld_w")
        for j in range(2):
            S.dma("gpsimd", ku_s[:, j, :], ku_in[j * 128:(j + 1) * 128, :], "ld_w")
            S.dma("gpsimd", vu_s[:, j, :], vu_in[j * 128:(j + 1) * 128, :], "ld_w")
        n_wdma = DC * 2 + 4
    else:
        for k in range(DC):
            S.dma("gpsimd", wk_s[:, k, :], wk_in[k * 128:(k + 1) * 128, :], "ld_w")
            S.dma("gpsimd", wv_s[:, k, :], wv_in[k * 128:(k + 1) * 128, :], "ld_w")
        n_wdma = DC * 3
    for j in range(nmask):
        S.dma("sync", mask_s[:, j, :], mask_in[j], "ld_m")

    if xn_idx is not None:
        idxb5 = sb("idxb5", [128, NTF], I32)
        S.dma("sync", idxb5[:], xn_idx, "ld_ix")
        S.wait("gpsimd", "ld_ix")
    for t in range(NTF):
        if t >= 2:
            S.wait("sync", "tp_t", S.val("tp_t") - 2)
            if xn_idx is not None:
                S.wait("gpsimd", "tp_t", S.val("tp_t") - 2)
        if t >= 2:
            S.wait("sync", "v_qb", 2 * (t - 1))
        if xn_idx is None:
            S.dma("sync", xnb[:, t % 2, :], xn_in[t * 128:(t + 1) * 128, :], "ld_xb")
        else:
            S.op("gpsimd", lambda E, t=t: E.indirect_dma_start(
                xnb[:, t % 2, :], None, xn_in, bass.IndirectOffsetOnAxis(idxb5[:, t:t + 1], 0)),
                inc="ld_xb", n=16, drain=False)
        S.dma("sync", ctile[:, t % 2, :], cos_in[t * 128:(t + 1) * 128, :], "ld_xn")
        S.dma("sync", stile[:, t % 2, :], sin_in[t * 128:(t + 1) * 128, :], "ld_xn")
        S.wait("tensor", "ld_xb", 16 * (t + 1))
        S.wait("tensor", "ld_xn", 32 * (t + 1))
        S.wait("vector", "ld_xn", 32 * (t + 1))
        S.wait("vector", "tp_t", S.val("tp_t"))
        S.wait("tensor", "ld_w", 16)
        cols = [(k * 128, xnT_t[:, k, :]) for k in range(DC)]
        S.wait("tensor", "t_proj", S.val("t_proj"))
        emit_transposes(S, xnb[:, t % 2, :], None, cols, ps_ta, ps_tb, ident, "tp")
        S.wait("tensor", "tp_v")
        if t == 0 and 'dbg_xnT' in io:
            S.wait("sync", "tp_v", S.val("tp_v"))
            S.dma("sync", io['dbg_xnT'], xnT_t[:].rearrange("p c f -> p (c f)"), "st_mg")
        S.wait("tensor", "ld_w", 16 * (1 + n_wdma))
        S.wait("tensor", "v_qev", S.val("v_qev"))
        S.wait("tensor", "v_lat", S.val("v_lat"))
        for k in range(DC):
            S.op("tensor", lambda E, k=k: E.matmul(P1[:, 0:512], xnT_t[:, k, :], wq_s[:, k, :],
                                                   start=(k == 0), stop=(k == DC - 1)))
        if not local:
            for k in range(DC):
                S.op("tensor", lambda E, k=k: E.matmul(P1[:, 512:768], xnT_t[:, k, :], kvd_s[:, k, :],
                                                       start=(k == 0), stop=(k == DC - 1)),
                     inc="t_proj" if k == DC - 1 else None)
        else:
            S.wait("tensor", "v_kev", S.val("v_kev"))
            S.wait("tensor", "v_vev", S.val("v_vev"))
            for k in range(DC):
                S.op("tensor", lambda E, k=k: E.matmul(P2[:, 0:128], xnT_t[:, k, :], wk_s[:, k, :],
                                                       start=(k == 0), stop=(k == DC - 1)))
            for k in range(DC):
                S.op("tensor", lambda E, k=k: E.matmul(P2[:, 128:256], xnT_t[:, k, :], wv_s[:, k, :],
                                                       start=(k == 0), stop=(k == DC - 1)),
                     inc="t_proj" if k == DC - 1 else None)
        S.wait("vector", "t_proj")
        emit_rope_qknorm(S, P1[:, 0:512], 8, ctile[:, t % 2, :], stile[:, t % 2, :],
                         qf, qr, sqs, rstd8, sqs[:, 0:512].bitcast(BF16)[:, 0:512])
        if t == 0 and 'dbg_qf' in io:
            S.wait("sync", "v_qev", S.val("v_qev"))
            S.dma("sync", io['dbg_qf'], qf[:, 0:512], "st_mg")
        qbv = sqs[:, 0:512].bitcast(BF16)[:, 0:512]
        S.wait("tensor", "v_qb")
        colsq = [(j * 128, qT[:, j, t * 128:(t + 1) * 128]) for j in range(4)]
        emit_transposes(S, qbv, None, colsq, ps_ta, ps_tb, ident, "tp")
        if t == 0 and 'dbg_qT0' in io:
            S.wait("sync", "tp_v", S.val("tp_v"))
            for j in range(4):
                S.dma("sync", io['dbg_qT0'][:, j * 128:(j + 1) * 128], qT[:, j, 0:128], "st_mg")
        if not local:
            S.op("vector", lambda E: E.tensor_copy(latb[:], P1[:, 512:768]), inc="v_lat")
            S.wait("tensor", "v_lat")
            colsl = [(j * 128, latT_t[:, j, :]) for j in range(2)]
            S.wait("tensor", "t_kv", S.val("t_kv"))
            emit_transposes(S, latb[:].bitcast(BF16) if False else latb[:], None, colsl, ps_ta, ps_tb, ident, "tp")
            S.wait("tensor", "tp_v")
            S.wait("tensor", "v_kev", S.val("v_kev"))
            S.wait("tensor", "v_vev", S.val("v_vev"))
            for j in range(2):
                S.op("tensor", lambda E, j=j: E.matmul(P2[:, 0:128], latT_t[:, j, :], ku_s[:, j, :],
                                                       start=(j == 0), stop=(j == 1)))
            for j in range(2):
                S.op("tensor", lambda E, j=j: E.matmul(P2[:, 128:256], latT_t[:, j, :], vu_s[:, j, :],
                                                       start=(j == 0), stop=(j == 1)), inc="t_kv" if j == 1 else None)
            S.wait("vector", "t_kv")
        S.wait("vector", "tp_t", S.val("tp_t"))
        emit_rope_qknorm(S, P2[:, 0:128], 2, ctile[:, t % 2, :], stile[:, t % 2, :],
                         qf, qr, sqs, rstd8, kb[:])
        S.op("vector", lambda E, t=t: E.tensor_copy(vb[:, t, :], P2[:, 128:256]), inc="v_vev")
        if t == 0 and 'dbg_qf' in io:
            S.wait("sync", "v_vev", S.val("v_vev"))
            S.dma("sync", io['dbg_v0'], vb[:, 0, :], "st_mg")
        S.op("vector", lambda E: E.engine_nop(), inc="v_kev", drain=False)
        S.wait("tensor", "v_qb")
        colsk = [(0, kT[:, t * 128:(t + 1) * 128])]
        emit_transposes(S, kb[:], None, colsk, ps_ta, ps_tb, ident, "tp")

    es_proj.close()
    P4 = ps("P4", [128, 1024])
    S.wait("tensor", "ld_m")
    proj_done_v = S.val("tp_v")
    S.wait("tensor", "tp_v", proj_done_v)
    S.wait("tensor", "v_qev", S.val("v_qev"))
    S.wait("tensor", "v_lat", S.val("v_lat"))
    S.wait("tensor", "v_kev", S.val("v_kev"))
    S.wait("tensor", "v_vev", S.val("v_vev"))
    Sbufs = [P1, P2, P4]
    gi = [0]

    def emit_S(kt, qg, hp):
        j = gi[0]
        buf = Sbufs[j % 3]
        ksl = slice(kt * 128, (kt + 1) * 128)
        qsl = slice(qg * 512, (qg + 1) * 512)
        S.wait("tensor", "s_exp", max(0, S.val("s_exp") - 2))
        S.op("tensor", lambda E, buf=buf, ksl=ksl, qsl=qsl, hp=hp: E.matmul(
            buf[:, 0:512], kT[0:64, ksl], qT[0:64, hp, qsl],
            start=True, stop=True, tile_position=(0, 0)))
        S.op("tensor", lambda E, buf=buf, ksl=ksl, qsl=qsl, hp=hp: E.matmul(
            buf[:, 512:1024], kT[64:128, ksl], qT[64:128, hp, qsl],
            start=True, stop=True, tile_position=(64, 0)), inc="t_S")

    def emit_exp():
        j = gi[0]
        buf = Sbufs[j % 3]
        S.wait("scalar", "t_S", S.val("t_S"))
        S.wait("scalar", "t_av", max(0, S.val("t_av") - 1))
        S.op("scalar", lambda E, buf=buf, s=j % 3: E.activation(pAB[:, s, :], buf[:, 0:1024], AF.Exp, scale=0.125),
             inc="s_exp")

    def emit_mult(kt, qg, lag):
        j = gi[0]
        diag_j = kt - 4 * qg + (2 if local else 0)
        if not (0 <= diag_j < nmask):
            return 0
        S.wait("vector", "s_exp", S.val("s_exp") - lag)
        mb = mask_s[:, diag_j, :].unsqueeze(1).broadcast_to([128, 2, 512])
        S.op("vector", lambda E, s=j % 3, mb=mb: E.tensor_tensor(
            pAB[:, s, :].rearrange("p (two q) -> p two q", two=2),
            pAB[:, s, :].rearrange("p (two q) -> p two q", two=2), mb, ALU.mult), inc="v_msk")
        return 1

    def emit_av(kt, first, last, masked, lag):
        s = gi[0] % 3
        S.wait("tensor", "s_exp", S.val("s_exp") - lag)
        if masked:
            S.wait("tensor", "v_msk", S.val("v_msk"))
        if first:
            S.wait("tensor", "v_mg", S.val("v_mg"))
        S.op("tensor", lambda E, s=s, first=first, last=last: E.matmul(
            P3[0:1, 512:1024], onescol_b[:], pAB[:, s, 0:512], start=first, stop=last, tile_position=(0, 0)))
        S.op("tensor", lambda E, s=s, first=first, last=last: E.matmul(
            P3[32:33, 512:1024], onescol_b[:], pAB[:, s, 512:1024], start=first, stop=last, tile_position=(0, 32)))
        S.op("tensor", lambda E, s=s, kt=kt, first=first, last=last: E.matmul(
            P3[0:64, 0:512], vb[:, kt, 0:64], pAB[:, s, 0:512], start=first, stop=last, tile_position=(0, 0)))
        S.op("tensor", lambda E, s=s, kt=kt, first=first, last=last: E.matmul(
            P3[64:128, 0:512], vb[:, kt, 64:128], pAB[:, s, 512:1024], start=first, stop=last, tile_position=(0, 64)),
            inc="t_av")

    for hp in range(4):
        for qg in range(8):
            kt_lo = max(0, 4 * qg - 2) if local else 0
            kts = list(range(kt_lo, 4 * qg + 4))
            qsl = slice(qg * 512, (qg + 1) * 512)
            n = len(kts)
            for i, kt in enumerate(kts):
                emit_S(kt, qg, hp)
                emit_exp()
                if i >= 1:
                    gi[0] -= 1
                    m = emit_mult(kts[i - 1], qg, 1)
                    emit_av(kts[i - 1], i - 1 == 0, False, m, 1)
                    gi[0] += 1
                gi[0] += 1
            gi[0] -= 1
            m = emit_mult(kts[n - 1], qg, 0)
            emit_av(kts[n - 1], n == 1, True, m, 0)
            gi[0] += 1
            S.wait("scalar", "t_av", S.val("t_av"))
            S.op("scalar", lambda E: E.activation(rs_sb[:], P3[:, 512:1024], AF.Ln))
            S.op("scalar", lambda E: E.activation(rs2[:], rs_sb[:], AF.Exp, scale=-1.0), inc="s_rs")
            S.wait("tensor", "s_rs", S.val("s_rs"))
            S.op("tensor", lambda E: E.matmul(P3[0:64, 512:1024], onesp[0:1, 0:64], rs2[0:1, :],
                                              start=True, stop=True, tile_position=(0, 0)))
            S.op("tensor", lambda E: E.matmul(P3[64:128, 512:1024], onesp[32:33, 0:64], rs2[32:33, :],
                                              start=True, stop=True, tile_position=(32, 64)), inc="t_bc")
            S.wait("vector", "t_av", S.val("t_av"))
            S.op("vector", lambda E, hp=hp, qsl=qsl: E.tensor_copy(merged[:, hp, qsl], P3[:, 0:512]), drain=False)
            S.wait("vector", "t_bc", S.val("t_bc"))
            S.op("vector", lambda E, hp=hp, qsl=qsl: E.tensor_tensor(
                merged[:, hp, qsl], merged[:, hp, qsl], P3[:, 512:1024], ALU.mult), inc="v_mg")
    S.wait("sync", "v_mg")
    for j in range(4):
        S.dma("sync", mg_out[j * 128:(j + 1) * 128, :], merged[:, j, :], "st_mg")
    if 'dbg_qT' in io:
        for j in range(4):
            S.dma("sync", io['dbg_qT'][j * 128:(j + 1) * 128, :], qT[:, j, :], "st_mg")
        S.dma("sync", io['dbg_kT'], kT[:], "st_mg")
        S.dma("sync", io['dbg_vb'], vb[:].rearrange("p t d -> p (t d)"), "st_mg")
        if xn_idx is not None:
            S.dma("sync", io['dbg_idx'], idxb5[:], "st_mg")
    S.wait("sync", "st_mg")



def build_launch5(local=False):
    nmask = 6 if local else 4
    nc = mkbass()
    xn_in = nc.dram_tensor("xn_full", [2 * TOK, D], BF16, kind="ExternalInput").ap()
    wq_in = nc.dram_tensor("wq_h", [D, 512], F32, kind="ExternalInput").ap()
    if not local:
        kvd_in = nc.dram_tensor("kvd", [D, 256], F32, kind="ExternalInput").ap()
        ku_in = nc.dram_tensor("ku_h", [256, 128], F32, kind="ExternalInput").ap()
        vu_in = nc.dram_tensor("vu_h", [256, 128], F32, kind="ExternalInput").ap()
    else:
        wk_in = nc.dram_tensor("wk_h", [D, 128], F32, kind="ExternalInput").ap()
        wv_in = nc.dram_tensor("wv_h", [D, 128], F32, kind="ExternalInput").ap()
    cos_in = nc.dram_tensor("cosr", [2 * TOK, 32], F32, kind="ExternalInput").ap()
    sin_in = nc.dram_tensor("sinr", [2 * TOK, 32], F32, kind="ExternalInput").ap()
    mask_in = nc.dram_tensor("maskadd", [nmask, 128, 512], BF16, kind="ExternalInput").ap()
    id_in = nc.dram_tensor("ident", [128, 128], F32, kind="ExternalInput").ap()
    mg_out = nc.dram_tensor("mergedT", [512, 2 * TOK], BF16, kind="ExternalOutput").ap()
    es = ExitStack()
    S = Sched(nc); S._es = es
    io = dict(xn_in=xn_in, wq_in=wq_in, cos_in=cos_in, sin_in=sin_in, mask_in=mask_in,
              id_in=id_in, mg_out=mg_out)
    if not local:
        io.update(kvd_in=kvd_in, ku_in=ku_in, vu_in=vu_in)
    else:
        io.update(wk_in=wk_in, wv_in=wv_in)
    emit_l5(S, nc, es, io, local)
    with es, nc.Block() as block:
        S.build(block, es)
    return nc


def emit_l6(S, nc, es, io, final):
    mg_in = io['mg_in']; wo_in = io['wo_in']
    if not final:
        xn1_in = io['xn1_in']; n2_in = io['n2_in']; id_in = io['id_in']
        xa_out = io['xa_out']; xn2_out = io['xn2_out']
    else:
        xa_in = io['xa_in']; x1_in = io['x1_in']; g1_in = io['g1_in']; out_out = io['out_out']
    mg_idx = io.get('mg_idx')
    sb = lambda n_, s_, d_=F32: es.enter_context(nc.sbuf_tensor(S.pfx + n_, s_, d_))
    ps = lambda n_, s_, d_=F32: es.enter_context(nc.psum_tensor(S.pfx + n_, s_, d_))
    mg_s = sb("mg_s", [128, DC, TOK], BF16)
    wo_s = sb("wo_s", [128, DC, D], BF16)
    aux = sb("aux", [128, 2, D])
    auxb = sb("auxb", [128, 2, D], BF16)
    xab = sb("xab", [128, D])
    outb = sb("outb", [128, D])
    rstd = sb("rstd", [128, 1]); sq = sb("sq", [128, D])
    xnb2 = sb("xnb2", [128, D], BF16)
    n2b = sb("n2b", [128, D]); rowscr = sb("rowscr", [1, D])
    onesrow = sb("onesrow", [1, 128])
    P1 = ps("P1", [128, 1024])
    P2 = ps("P2", [128, 1024])

    S.setup_consts([1e-6, -0.5], sb)
    S.wait("scalar", "c_init")
    if mg_idx is not None:
        idxb6 = sb("idxb6", [128, DC], I32)
        S.dma("sync", idxb6[:], mg_idx, "ld_ix")
        S.wait("gpsimd", "ld_ix")
    for k in range(DC):
        S.dma("gpsimd", wo_s[:, k, :], wo_in[k * 128:(k + 1) * 128, :], "ld_w")
        if mg_idx is None:
            S.dma("sync", mg_s[:, k, :], mg_in[k * 128:(k + 1) * 128, :], "ld_m")
        else:
            S.op("gpsimd", lambda E, k=k: E.indirect_dma_start(
                mg_s[:, k, :], None, mg_in, bass.IndirectOffsetOnAxis(idxb6[:, k:k + 1], 0)),
                inc="ld_m", n=16, drain=False)
    if not final:
        S.op("gpsimd", lambda E: E.memset(onesrow[:], 1.0), inc="c_ones")
        S.dma("sync", rowscr[0:1, :], n2_in, "ld_misc")
        S.wait("tensor", "ld_misc")
        S.wait("tensor", "c_ones")
        emit_bcast_row(S, rowscr[0:1, :], onesrow, P2, n2b[:], "bc1")

    for t in range(NT):
        if not final:
            S.wait("sync", "v_xa", S.val("v_xa"))
            S.dma("sync", auxb[:, 0, :], xn1_in[t * 128:(t + 1) * 128, :], "ld_a")
        else:
            S.wait("sync", "v_out", S.val("v_out"))
            S.wait("sync", "st_out", S.val("st_out"))
            S.dma("sync", aux[:, 0, :], xa_in[t * 128:(t + 1) * 128, :], "ld_a")
            S.dma("sync", aux[:, 1, :], g1_in[t * 128:(t + 1) * 128, :], "ld_a")
            S.dma("sync", outb[:], x1_in[t * 128:(t + 1) * 128, :], "ld_a")
        S.wait("tensor", "ld_w")
        S.wait("tensor", "ld_m")
        S.wait("tensor", "v_wo", S.val("v_wo"))
        for k in range(DC):
            for n0 in (0, 512):
                S.op("tensor", lambda E, k=k, t=t, n0=n0: E.matmul(
                    P1[:, n0:n0 + 512], mg_s[:, k, t * 128:(t + 1) * 128], wo_s[:, k, n0:n0 + 512],
                    start=(k == 0), stop=(k == DC - 1)),
                    inc="t_wo" if (k == DC - 1 and n0 == 512) else None)
        na = 16 * (t + 1) * (1 if not final else 3)
        S.wait("vector", "ld_a", na)
        S.wait("vector", "t_wo")
        if not final:
            S.wait("vector", "st_xa", S.val("st_xa"))
            S.op("vector", lambda E: E.tensor_tensor(xab[:], P1[:], auxb[:, 0, :], ALU.add), inc="v_wo")
            S.wait("sync", "v_wo")
            S.dma("sync", xa_out[t * 128:(t + 1) * 128, :], xab[:], "st_xa")
            S.wait("vector", "bc1_v")
            emit_rms(S, xab[:], rstd, sq, n2b[:], xnb2[:])
            S.wait("sync", "v_xn")
            S.dma("sync", xn2_out[t * 128:(t + 1) * 128, :], xnb2[:], "st_xa")
            S.op("vector", lambda E: E.engine_nop(), inc="v_xa", drain=False)
        else:
            S.wait("vector", "st_out", S.val("st_out"))
            S.op("vector", lambda E: E.tensor_tensor(xab[:], P1[:], aux[:, 0, :], ALU.add), inc="v_wo")
            S.op("vector", lambda E: E.tensor_mul(xab[:], xab[:], aux[:, 1, :]), drain=False)
            S.op("vector", lambda E: E.tensor_add(outb[:], outb[:], xab[:]), inc="v_out")
            S.wait("sync", "v_out")
            S.dma("sync", out_out[t * 128:(t + 1) * 128, :], outb[:], "st_out")
    S.wait("sync", "st_xa" if not final else "st_out")



def build_launch6(final=False):
    nc = mkbass()
    mg_in = nc.dram_tensor("mergedT_full", [D, TOK], BF16, kind="ExternalInput").ap()
    wo_in = nc.dram_tensor("wo", [D, D], F32, kind="ExternalInput").ap()
    if not final:
        xn1_in = nc.dram_tensor("xn1", [TOK, D], BF16, kind="ExternalInput").ap()
        n2_in = nc.dram_tensor("norm2_w", [1, D], F32, kind="ExternalInput").ap()
        id_in = nc.dram_tensor("ident", [128, 128], F32, kind="ExternalInput").ap()
        xa_out = nc.dram_tensor("x_a", [TOK, D], F32, kind="ExternalOutput").ap()
        xn2_out = nc.dram_tensor("xn2", [TOK, D], BF16, kind="ExternalOutput").ap()
    else:
        xa_in = nc.dram_tensor("x_a", [TOK, D], F32, kind="ExternalInput").ap()
        x1_in = nc.dram_tensor("x1", [TOK, D], F32, kind="ExternalInput").ap()
        g1_in = nc.dram_tensor("g1", [TOK, D], F32, kind="ExternalInput").ap()
        out_out = nc.dram_tensor("out", [TOK, D], F32, kind="ExternalOutput").ap()
    es = ExitStack()
    S = Sched(nc); S._es = es
    io = dict(mg_in=mg_in, wo_in=wo_in)
    if not final:
        io.update(xn1_in=xn1_in, n2_in=n2_in, id_in=id_in, xa_out=xa_out, xn2_out=xn2_out)
    else:
        io.update(xa_in=xa_in, x1_in=x1_in, g1_in=g1_in, out_out=out_out)
    emit_l6(S, nc, es, io, final)
    with es, nc.Block() as block:
        S.build(block, es)
    return nc


def build_fused(upto=99):
    nc = mkbass()
    EI, EO = "ExternalInput", "ExternalOutput"
    dt_ = lambda n, s, d=F32, k="Internal": nc.dram_tensor(n, s, d, kind=k).ap()
    x_in = dt_("x", [TOK, D], F32, EI)
    wg_sl = dt_("wg_sl", [352, D], F32, EI)
    wu_sl = dt_("wu_sl", [352, D], F32, EI)
    wd_sl = dt_("wd_sl", [128, I_FFN], F32, EI)
    WvT = dt_("WvT", [D, D], F32, EI)
    WqT = dt_("WqT", [D, D], F32, EI)
    WgT = dt_("WgT", [D, D], F32, EI)
    WoT = dt_("WoT", [D, D], F32, EI)
    beta_in = dt_("beta", [1, H], F32, EI)
    gam_in = dt_("ln_gamma", [1, D], F32, EI)
    bet_in = dt_("ln_beta", [1, D], F32, EI)
    id_in = dt_("ident", [128, 128], F32, EI)
    cidx_in = dt_("carry_idx", [NT, 1], I32, EI)
    n1_in = dt_("norm1_w", [1, D], F32, EI)
    n2_in = dt_("norm2_w", [1, D], F32, EI)
    ws_in = dt_("WsT", [D, D], F32, EI)
    we_in = dt_("WeT", [D, D], F32, EI)
    bias_in = dt_("bias", [1, D], F32, EI)
    gwq_in = dt_("g_wq_h", [D, 512], F32, EI)
    kvd_in = dt_("g_kvd", [D, 256], F32, EI)
    ku_in = dt_("g_ku_h", [256, 128], F32, EI)
    vu_in = dt_("g_vu_h", [256, 128], F32, EI)
    gwo_in = dt_("g_wo", [D, D], F32, EI)
    lwq_in = dt_("l_wq_h", [D, 512], F32, EI)
    lwk_in = dt_("l_wk_h", [D, 128], F32, EI)
    lwv_in = dt_("l_wv_h", [D, 128], F32, EI)
    lwo_in = dt_("l_wo", [D, D], F32, EI)
    cosg_in = dt_("cosg", [2 * TOK, 32], F32, EI)
    sing_in = dt_("sing", [2 * TOK, 32], F32, EI)
    cosl_in = dt_("cosl", [2 * TOK, 32], F32, EI)
    sinl_in = dt_("sinl", [2 * TOK, 32], F32, EI)
    maskG_in = dt_("maskG", [4, 128, 512], BF16, EI)
    maskL_in = dt_("maskL", [6, 128, 512], BF16, EI)
    xnidx_g = dt_("xnidx_g", [128, NTF], I32, EI)
    xnidx_l = dt_("xnidx_l", [128, NTF], I32, EI)
    mgidx_g = dt_("mgidx_g", [128, DC], I32, EI)
    mgidx_l = dt_("mgidx_l", [128, DC], I32, EI)
    sums_idx = dt_("sums_idx", [2 * NT, 1], I32, EI)
    out_out = dt_("out", [TOK, D], F32, EO)
    wgq_sl = dt_("wgq_sl_i", [352, D], BF16)
    wuq_sl = dt_("wuq_sl_i", [352, D], BF16)
    wdq_sl = dt_("wdq_sl_i", [128, I_FFN], BF16)
    wgq_all = dt_("wgq_all", [I_FFN, D], BF16)
    wuq_all = dt_("wuq_all", [I_FFN, D], BF16)
    wdq_all = dt_("wdq_all", [D, I_FFN], BF16)
    xn0_i = dt_("xn0_i", [TOK, D], BF16)
    gacc_i = dt_("gacc_i", [TOK, D], F32)
    stl_i = dt_("stl_i", [TOK, D], BF16)
    sums_i = dt_("sums_i", [NT, D], F32)
    sums_ag = dt_("sums_ag", [8 * NT, D], F32)
    sums_st = dt_("sums_st", [2 * NT, D], F32)
    carr_dram = dt_("carr_dram", [2 * NT, D], F32)
    carrmy_dram = dt_("carrmy_dram", [NT, D], F32)
    x1_i = dt_("x1_i", [TOK, D], F32)
    xn1_i = dt_("xn1_i", [TOK, D], BF16)
    xnr1_i = dt_("xnr1_i", [TOK, D], BF16)
    g1_i = dt_("g1_i", [TOK, D], F32)
    xnr1_ag = dt_("xnr1_ag", [8 * TOK, D], BF16)
    mgG_i = dt_("mgG_i", [512, 2 * TOK], BF16)
    mgG_ag = dt_("mgG_ag", [8 * 512, 2 * TOK], BF16)
    xa_i = dt_("xa_i", [TOK, D], F32)
    xn2_i = dt_("xn2_i", [TOK, D], BF16)
    xn2_ag = dt_("xn2_ag", [8 * TOK, D], BF16)
    mgL_i = dt_("mgL_i", [512, 2 * TOK], BF16)
    mgL_ag = dt_("mgL_ag", [8 * 512, 2 * TOK], BF16)

    es_g = ExitStack()
    S = Sched(nc); S._es = es_g
    sbg = lambda n_, s_, d_=F32: es_g.enter_context(nc.sbuf_tensor(n_, s_, d_))
    sums_gsb = sbg("sums_gsb", [2 * NT, D])
    sidxb = sbg("sidxb", [2 * NT, 1], I32)

    RG = [[0, 1, 2, 3, 4, 5, 6, 7]]

    def ag(cc_in, cc_out):
        S.op("gpsimd", lambda E, i=cc_in, o=cc_out: E.collective_compute(
            "AllGather", ALU.bypass, replica_groups=RG, ins=[i.opt()], outs=[o.opt()]),
            inc="!cc", n=1, drain=False)

    PH = [0]
    def phase(pfx, fn):
        if PH[0] > upto:
            return
        PH[0] += 1
        S.pfx = pfx
        S.rebase()
        es_p = ExitStack()
        fn(es_p)
        S.pfx = ""
        S.barrier()
        es_p.close()

    def glue(fn):
        if PH[0] > upto:
            return
        fn()

    phase("p0_", lambda es: emit_l0(S, nc, es, dict(
        wg_in=wg_sl, wu_in=wu_sl, wd_in=wd_sl,
        wg_out=wgq_sl.tensor.ap() if False else wgq_sl, wu_out=wuq_sl, wd_out=wdq_sl)))
    def g0():
        ag(wgq_sl, wgq_all); ag(wuq_sl, wuq_all); ag(wdq_sl, wdq_all)
        S.wait("sync", "!cc", 3); S.wait("gpsimd", "!cc", 3)
    glue(g0)
    phase("p1_", lambda es: emit_l1(S, nc, es, dict(
        x_in=x_in, wgq=wgq_all, wuq=wuq_all, wdq=wdq_all, id_in=id_in,
        xn0_out=xn0_i, gacc_out=gacc_i)))
    phase("p2_", lambda es: emit_l2(S, nc, es, dict(
        xn_in=xn0_i, wv_in=WvT, beta_in=beta_in, id_in=id_in,
        st_out=stl_i, sums_out=sums_i)))
    def g2():
        ag(sums_i, sums_ag)
        S.pfx = "g2_"
        S.rebase()
        S.dma("sync", sidxb[:], sums_idx, "ld_carr")
        S.wait("gpsimd", "!cc", 4)
        S.wait("gpsimd", "ld_carr")
        S.op("gpsimd", lambda E: E.indirect_dma_start(
            sums_gsb[:], None, sums_ag, bass.IndirectOffsetOnAxis(sidxb[:], 0)), inc="g_carr", n=16, drain=False)
        S.wait("gpsimd", "g_carr")
        S.op("gpsimd", lambda E: E.dma_start(out=sums_st, in_=sums_gsb[:]), inc="g_carr", n=16, drain=False)
        S.wait("sync", "g_carr", 32)
        S.pfx = ""
    glue(g2)
    phase("p3_", lambda es: emit_l3(S, nc, es, dict(
        xn_in=xn0_i, stl_in=stl_i, sums_in=sums_st, x_in=x_in, gacc_in=gacc_i,
        wq_in=WqT, wg_in=WgT, wo_in=WoT, beta_in=beta_in, gam_in=gam_in, bet_in=bet_in,
        id_in=id_in, cidx_in=cidx_in, x1_out=x1_i, carr_dram=carr_dram, carrmy_dram=carrmy_dram)))
    phase("p4_", lambda es: emit_l4(S, nc, es, dict(
        x1_in=x1_i, n1_in=n1_in, ws_in=ws_in, we_in=we_in, bias_in=bias_in, id_in=id_in,
        xn1_out=xn1_i, xnr1_out=xnr1_i, g1_out=g1_i)))
    def g4():
        ag(xnr1_i, xnr1_ag)
        S.wait("gpsimd", "!cc", 5)
    glue(g4)
    p5io = dict(
        xn_in=xnr1_ag, xn_idx=xnidx_g, wq_in=gwq_in, kvd_in=kvd_in, ku_in=ku_in, vu_in=vu_in,
        cos_in=cosg_in, sin_in=sing_in, mask_in=maskG_in, id_in=id_in, mg_out=mgG_i)
    if 90 <= upto < 99:
        p5io['dbg_qT'] = nc.dram_tensor("d_qT", [512, 4096], BF16, kind="ExternalOutput").ap()
        p5io['dbg_kT'] = nc.dram_tensor("d_kT", [128, 4096], BF16, kind="ExternalOutput").ap()
        p5io['dbg_vb'] = nc.dram_tensor("d_vb", [128, 4096], BF16, kind="ExternalOutput").ap()
        p5io['dbg_idx'] = nc.dram_tensor("d_idx", [128, NTF], I32, kind="ExternalOutput").ap()
        p5io['dbg_xnb'] = nc.dram_tensor("d_xnb", [128, D], BF16, kind="ExternalOutput").ap()
        p5io['dbg_xnT'] = nc.dram_tensor("d_xnT", [128, D], BF16, kind="ExternalOutput").ap()
        p5io['dbg_qf'] = nc.dram_tensor("d_qf", [128, 512], F32, kind="ExternalOutput").ap()
        p5io['dbg_v0'] = nc.dram_tensor("d_v0", [128, 128], BF16, kind="ExternalOutput").ap()
        p5io['dbg_qT0'] = nc.dram_tensor("d_qT0", [128, 512], BF16, kind="ExternalOutput").ap()
    phase("p5_", lambda es: emit_l5(S, nc, es, p5io, False))
    def g5():
        ag(mgG_i, mgG_ag)
        S.wait("gpsimd", "!cc", 6)
    glue(g5)
    phase("p6_", lambda es: emit_l6(S, nc, es, dict(
        mg_in=mgG_ag.rearrange("(r hh) t -> (r hh) t") if False else mgG_ag.rearrange("r (hh t) -> (r hh) t", hh=2),
        mg_idx=mgidx_g, wo_in=gwo_in, xn1_in=xn1_i, n2_in=n2_in, id_in=id_in,
        xa_out=xa_i, xn2_out=xn2_i), False))
    def g6():
        ag(xn2_i, xn2_ag)
        S.wait("gpsimd", "!cc", 7)
    glue(g6)
    p7io = dict(
        xn_in=xn2_ag, xn_idx=xnidx_l, wq_in=lwq_in, wk_in=lwk_in, wv_in=lwv_in,
        cos_in=cosl_in, sin_in=sinl_in, mask_in=maskL_in, id_in=id_in, mg_out=mgL_i)
    phase("p7_", lambda es: emit_l5(S, nc, es, p7io, True))
    def g7():
        ag(mgL_i, mgL_ag)
        S.wait("gpsimd", "!cc", 8)
    glue(g7)
    phase("p8_", lambda es: emit_l6(S, nc, es, dict(
        mg_in=mgL_ag.rearrange("r (hh t) -> (r hh) t", hh=2),
        mg_idx=mgidx_l, wo_in=lwo_in, xa_in=xa_i, x1_in=x1_i, g1_in=g1_i, out_out=out_out), True))

    if upto < 99:
        S.pfx = "gd_"
        S.rebase()
        srcs = {0: None, 1: gacc_i, 2: gacc_i, 3: x1_i, 4: g1_i, 5: g1_i, 6: xa_i, 7: xa_i}
        sd = srcs.get(min(upto, 7), x1_i)
        if upto >= 90:
            for nm, t_, d_ in [("d_xnrag", xnr1_ag, BF16), ("d_sumsag", sums_ag, F32),
                               ("d_xn2ag", xn2_ag, BF16), ("d_mgL", mgL_i, BF16), ("d_mgLag", mgL_ag, BF16),
                               ("d_xn0", xn0_i, BF16), ("d_gacc", gacc_i, F32), ("d_x1", x1_i, F32),
                               ("d_g1", g1_i, F32), ("d_xa", xa_i, F32), ("d_xnr1", xnr1_i, BF16),
                               ("d_mgG", mgG_i, BF16), ("d_xn2", xn2_i, BF16), ("d_stl", stl_i, BF16)]:
                shp = list(t_.shape)
                dbg = nc.dram_tensor(nm, shp, d_, kind="ExternalOutput").ap()
                S.dma("sync", dbg, t_, "st_dbg")
            S.wait("sync", "st_dbg")
        elif sd is not None:
            S.dma("sync", out_out, sd, "st_dbg")
            S.wait("sync", "st_dbg")

    with es_g, nc.Block() as block:
        S.build(block, es_g)
    return nc



HEAD_ORDER = [0, 4, 1, 5, 2, 6, 3, 7]

def _rope_tables(base, S_=4096, hd=64):
    inv = (1.0 / base ** (np.arange(0, hd, 2, dtype=np.float32) / np.float32(hd))).astype(np.float32)
    fr = np.arange(S_, dtype=np.float32)[:, None] * inv[None, :]
    return np.cos(fr).astype(np.float32), np.sin(fr).astype(np.float32)

def _build_masks(local):
    n = 6 if local else 4
    out = np.zeros((n, 128, 512), np.float32)
    k = np.arange(128)[:, None]
    qf = np.arange(512)[None, :]
    for j in range(n):
        rel = (j - 2) if local else j
        d = qf - (rel * 128 + k)
        vis = (d >= 0) & (d < 256) if local else (d >= 0)
        out[j] = np.where(vis, 1.0, 0.0)
    return out.astype(BF)

def _head_cols(half):
    return np.concatenate([np.arange((half * 8 + h) * 64, (half * 8 + h) * 64 + 64) for h in HEAD_ORDER])

WO_ROWS = np.concatenate([np.arange(h * 64, h * 64 + 64) for h in
                          [0, 4, 1, 5, 2, 6, 3, 7, 8, 12, 9, 13, 10, 14, 11, 15]])

_BUILDERS = {}

def _get(name, fn):
    if name not in _BUILDERS:
        _BUILDERS[name] = fn()
    return _BUILDERS[name]


def kernel(**inputs):
    g = {k: np.asarray(v, np.float32) for k, v in inputs.items()}
    C = np.ascontiguousarray
    ident = np.eye(128, dtype=np.float32)
    cosg, sing = _rope_tables(1e6)
    cosl, sinl = _rope_tables(1e4)
    maskG = _build_masks(False)
    maskL = _build_masks(True)
    wg, wu, wd = g['L0_ffn_w_gate'], g['L0_ffn_w_up'], g['L0_ffn_w_down']
    p = np.arange(128, dtype=np.int32)[:, None]
    t = np.arange(NTF, dtype=np.int32)[None, :]
    k = np.arange(DC, dtype=np.int32)[None, :]
    maps = []
    for c in range(8):
        b, half = c // 2, c % 2
        xnidx = (b * 4096 + t * 128 + p).astype(np.int32)
        mgidx = (2 * ((2 * b + k // 4) * 512 + (k % 4) * 128 + p) + half).astype(np.int32)
        maps.append(dict(
            x=C(g['L0_attn_x'][b, half*2048:(half+1)*2048]),
            wg_sl=C(wg[c*352:(c+1)*352]), wu_sl=C(wu[c*352:(c+1)*352]), wd_sl=C(wd[c*128:(c+1)*128]),
            WvT=C(g['L0_attn_Wv'].T), WqT=C(g['L0_attn_Wq'].T), WgT=C(g['L0_attn_Wg'].T),
            WoT=C(g['L0_attn_Wo'].T),
            beta=g['L0_attn_beta'].reshape(1, 16),
            ln_gamma=g['L0_attn_ln_gamma'].reshape(1, 1024), ln_beta=g['L0_attn_ln_beta'].reshape(1, 1024),
            ident=ident,
            carry_idx=(np.arange(16, dtype=np.int32) + 16 * half).reshape(16, 1),
            norm1_w=g['L1_attn_norm1_w'].reshape(1, 1024), norm2_w=g['L1_attn_norm2_w'].reshape(1, 1024),
            WsT=C(g['L1_ffn_W_shared'].T), WeT=C(g['L1_ffn_W_expert'].T),
            bias=g['L1_ffn_bias'].reshape(1, 1024),
            g_wq_h=C(g['L1_attn_g_wq'][:, _head_cols(half)]),
            g_kvd=C(g['L1_attn_g_kv_down']),
            g_ku_h=C(g['L1_attn_g_k_up'][:, half*128:(half+1)*128]),
            g_vu_h=C(g['L1_attn_g_v_up'][:, half*128:(half+1)*128]),
            g_wo=C(g['L1_attn_g_wo'][WO_ROWS, :]),
            l_wq_h=C(g['L1_attn_l_wq'][:, _head_cols(half)]),
            l_wk_h=C(g['L1_attn_l_wk'][:, half*128:(half+1)*128]),
            l_wv_h=C(g['L1_attn_l_wv'][:, half*128:(half+1)*128]),
            l_wo=C(g['L1_attn_l_wo'][WO_ROWS, :]),
            cosg=cosg, sing=sing, cosl=cosl, sinl=sinl,
            maskG=maskG, maskL=maskL,
            xnidx_g=xnidx, xnidx_l=xnidx, mgidx_g=mgidx, mgidx_l=mgidx,
            sums_idx=(np.arange(32, dtype=np.int32) + 32 * b).reshape(32, 1),
        ))
    res = run_bass_kernel_spmd(_get('fused', build_fused), maps, core_ids=list(range(8)))
    out = np.zeros((4, 4096, 1024), np.float32)
    for c in range(8):
        out[c//2, (c%2)*2048:(c%2+1)*2048] = res.results[c]['out']
    return out


# version 09
# speedup vs baseline: 1.0078x; 1.0078x over previous
import numpy as np
import ml_dtypes
import concourse.bass as bass
import concourse.mybir as mybir
from concourse.bass_utils import run_bass_kernel_spmd
from contextlib import ExitStack

F32 = mybir.dt.float32
BF16 = mybir.dt.bfloat16
I32 = mybir.dt.int32
I16 = mybir.dt.int16
U8 = mybir.dt.uint8
AF = mybir.ActivationFunctionType
ALU = mybir.AluOpType
AX = mybir.AxisListType
BF = ml_dtypes.bfloat16

NCORES = 8
TOK = 2048
NT = TOK // 128
D = 1024
DC = D // 128
I_FFN = 2816
I_PHASES = [(0, 768), (768, 768), (1536, 768), (2304, 512)]
H = 16
DH = 64
LN2 = float(np.log(2.0))
NUDGE6 = float(np.float32(6.0 * (1 - 2**-20)))

ENGS = ("sync", "gpsimd", "vector", "scalar", "tensor")


class Sched:

    def __init__(self, nc):
        self.nc = nc
        self.q = {e: [] for e in ENGS}
        self.cnt = {}
        self.sems = {}
        self._es = None
        self.pfx = ""
        self.base = {}
        self.name_map = {}
        self.trace = {e: [] for e in ENGS}

    def _n(self, name):
        if name.startswith("!"):
            return name
        return self.name_map.setdefault(name, "s%d" % len(self.name_map))

    def _sem_raw(self, name):
        if name not in self.sems:
            self.sems[name] = self._es.enter_context(self.nc.semaphore(name))
            self.cnt[name] = 0
        return self.sems[name]

    def sem(self, name):
        return self._sem_raw(self._n(name))

    def op(self, eng, fn, inc=None, n=1, drain=None):
        if drain is None:
            drain = eng in ("vector", "gpsimd", "scalar")
        if inc is not None:
            inc = self._n(inc)
            self._sem_raw(inc)
            self.cnt[inc] += n
        self.trace[eng].append(("inc", inc, n) if inc else ("op", None, 0))
        def run(E, fn=fn, inc=inc, n=n, drain=drain):
            ins = fn(E)
            if drain:
                d = E.drain()
                if inc is not None:
                    d.then_inc(self.sems[inc], n)
            elif inc is not None:
                ins.then_inc(self.sems[inc], n)
        self.q[eng].append(run)

    def dma(self, eng, out, in_, inc, **kw):
        self.op(eng, lambda E, o=out, i=in_: E.dma_start(out=o, in_=i, **kw), inc=inc, n=16, drain=False)

    def rebase(self):
        self.base = dict(self.cnt)
        self.name_map = {}

    def _b(self, name):
        return 0 if name.startswith("!") else self.base.get(name, 0)

    def wait(self, eng, semname, val=None):
        semname = self._n(semname)
        self._sem_raw(semname)
        v = self.cnt[semname] if val is None else self._b(semname) + val
        if v == 0:
            return
        self.trace[eng].append(("wait", semname, v))
        self.q[eng].append(lambda E, s=semname, v=v: E.wait_ge(self.sems[s], v))

    def val(self, semname):
        n = self._n(semname)
        return self.cnt.get(n, 0) - self._b(n)

    def barrier(self):
        for e in ENGS:
            def mk(e=e):
                def run(E):
                    d = E.drain()
                    d.then_inc(self.sems["!bar"], 1)
                return run
            self._sem_raw("!bar")
            self.cnt["!bar"] += 1
            self.trace[e].append(("inc", "!bar", 1))
            self.q[e].append(mk())
        tot = self.cnt["!bar"]
        for e in ENGS:
            self.trace[e].append(("wait", "!bar", tot))
            self.q[e].append(lambda E, v=tot: E.wait_ge(self.sems["!bar"], v))

    def setup_consts(self, values, sb):
        cst = sb("cst", [128, len(values)])
        self.consts = {}
        self.consts = {}
        for i, v in enumerate(values):
            ap = cst[:, i:i + 1]
            self.consts[v] = ap
            self.op("gpsimd", lambda E, ap=ap, v=v: E.memset(ap, float(v)),
                    inc="c_init" if i == len(values) - 1 else None)
        return self.consts

    def check(self):
        pos = {e: 0 for e in ENGS}
        cnt = {}
        progress = True
        while progress:
            progress = False
            for e in ENGS:
                tr = self.trace[e]
                while pos[e] < len(tr):
                    kind, sem, v = tr[pos[e]]
                    if kind == "wait" and cnt.get(sem, 0) < v:
                        break
                    if kind == "inc":
                        cnt[sem] = cnt.get(sem, 0) + v
                    pos[e] += 1
                    progress = True
        stuck = {e: (pos[e], len(self.trace[e]), self.trace[e][pos[e]] if pos[e] < len(self.trace[e]) else None)
                 for e in ENGS if pos[e] < len(self.trace[e])}
        if stuck:
            for e, (p, n, item) in stuck.items():
                print(f"DEADLOCK {e}: op {p}/{n} waiting {item}; sem at {cnt.get(item[1], 0)}")
            raise RuntimeError(f"schedule deadlock: {stuck}")

    def build(self, block, es):
        self.check()
        for e in ENGS:
            ops = self.q[e]
            if not ops:
                continue
            def emit(E, ops=ops):
                for f in ops:
                    f(E)
            getattr(block, e)(emit)


def mkbass():
    return bass.Bass("TRN2", target_bir_lowering=False)



def emit_ln(S, x_tile, mu, rstd, sq, xnf_tile, eps=1e-5):
    v, s = "vector", "scalar"
    S.op(v, lambda E: E.reduce_sum(mu[:], x_tile, AX.X))
    S.op(v, lambda E: E.tensor_mul(sq[:], x_tile, x_tile), drain=False)
    S.op(v, lambda E: E.reduce_sum(rstd[:], sq[:], AX.X))
    S.op(v, lambda E: E.tensor_scalar_mul(mu[:], mu[:], 1.0 / 1024))
    S.op(v, lambda E: E.tensor_scalar_mul(rstd[:], rstd[:], 1.0 / 1024))
    S.op(v, lambda E: E.tensor_mul(sq[:, 0:1], mu[:], mu[:]))
    S.op(v, lambda E: E.tensor_sub(rstd[:], rstd[:], sq[:, 0:1]), inc="v_ln")
    S.wait(s, "v_ln")
    c_eps, c_mh = S.consts[eps], S.consts[-0.5]
    S.op(s, lambda E, c=c_eps: E.activation(rstd[:], rstd[:], AF.Ln, bias=c))
    S.op(s, lambda E, c=c_mh: E.activation(rstd[:], rstd[:], AF.Exp, scale=c), inc="s_ln")
    S.wait(v, "s_ln")
    S.op(v, lambda E: E.tensor_scalar(xnf_tile, x_tile, mu[:], rstd[:], ALU.subtract, ALU.mult), inc="v_xn")


def emit_quant(S, xnf, F, amax, recip, hi_scr, r_bf, m_f, xq_out, sgn_scr):
    v = "vector"
    nb = F // 16
    P = xnf.shape[0]
    xnf3 = xnf.rearrange("p (a b) -> p a b", b=16)
    mf = m_f[:, 0:F]
    mf3 = mf.rearrange("p (a b) -> p a b", b=16)
    S.op(v, lambda E: E.reduce_max(amax[:, 0:nb], xnf3, AX.X, apply_absolute_value=True))
    S.op(v, lambda E: E.reciprocal(recip[:, 0:nb], amax[:, 0:nb]))
    S.op(v, lambda E: E.tensor_scalar(recip[:, 0:nb], recip[:, 0:nb], 1e12, NUDGE6, ALU.min, ALU.mult))
    rb = recip[:, 0:nb].unsqueeze(2).broadcast_to([P, nb, 16])
    S.op(v, lambda E: E.tensor_tensor(mf3, xnf3, rb, ALU.mult), drain=False)
    xn_hi = mf.bitcast(I16).rearrange("p (a b) -> p a b", b=2)[:, :, 1:2].squeeze(2)
    hi_i = hi_scr.bitcast(I16)[:, 0:F]
    sg_i = sgn_scr.bitcast(I16)[:, 0:F]
    r_i = r_bf.bitcast(I16)[:, 0:F]
    S.op(v, lambda E: E.tensor_scalar(hi_i, xn_hi, 0x7FFF, None, ALU.bitwise_and), drain=False)
    S.op(v, lambda E: E.tensor_scalar(sg_i, xn_hi, -32768, 0x3F80, ALU.bitwise_and, ALU.bitwise_or), drain=False)
    S.op(v, lambda E: E.tensor_scalar(r_i, hi_i, 0x3F80, 0x0020, ALU.max, ALU.add), drain=False)
    S.op(v, lambda E: E.tensor_scalar(r_i, r_i, -64, None, ALU.bitwise_and), drain=False)
    S.op(v, lambda E: E.tensor_scalar(mf, hi_i, 0x3E80, None, ALU.is_ge), drain=False)
    S.op(v, lambda E: E.tensor_scalar(xq_out, hi_i, 0x3F40, None, ALU.is_ge), drain=False)
    S.op(v, lambda E: E.tensor_tensor(mf, mf, xq_out, ALU.add), drain=False)
    S.op(v, lambda E: E.scalar_tensor_tensor(mf, mf, 0.5, r_bf[:, 0:F], ALU.mult, ALU.mult), drain=False)
    S.op(v, lambda E: E.tensor_scalar_mul(amax[:, 0:nb], amax[:, 0:nb], 1.0 / 6.0))
    S.op(v, lambda E: E.tensor_mul(mf, mf, sgn_scr[:, 0:F]), drain=False)
    ab = amax[:, 0:nb].unsqueeze(2).broadcast_to([P, nb, 16])
    S.op(v, lambda E: E.tensor_tensor(xq_out.rearrange("p (a b) -> p a b", b=16), mf3, ab, ALU.mult), inc="q_done")


def emit_transposes(S, src_bf16, dst, tile_cols, ps_a, ps_b, ident, tag, evac="scalar"):
    t, v = "tensor", "vector"
    bufs = [ps_a, ps_b]
    n = len(tile_cols)
    groups = [tile_cols[i:i + 4] for i in range(0, n, 4)]
    start_v = S.val(tag + "_v")
    S.wait(t, tag + "_v", start_v)
    for gi, grp in enumerate(groups):
        ps = bufs[gi % 2]
        if gi >= 2:
            S.wait(t, tag + "_v", start_v + gi - 1)
        for j, (c0, dst_ap) in enumerate(grp):
            S.op(t, lambda E, c0=c0, j=j, ps=ps: E.transpose(ps[:, j * 128:(j + 1) * 128], src_bf16[:, c0:c0 + 128], ident[:]),
                 inc=(tag + "_t") if j == len(grp) - 1 else None)
        S.wait(evac, tag + "_t")
        for j, (c0, dst_ap) in enumerate(grp):
            last = j == len(grp) - 1
            if evac == "scalar":
                S.op("scalar", lambda E, j=j, ps=ps, dst_ap=dst_ap: E.activation(dst_ap, ps[:, j * 128:(j + 1) * 128], AF.Copy),
                     inc=(tag + "_v") if last else None, drain=last)
            else:
                S.op("vector", lambda E, j=j, ps=ps, dst_ap=dst_ap: E.tensor_copy(dst_ap, ps[:, j * 128:(j + 1) * 128]),
                     inc=(tag + "_v") if last else None, drain=last)


def emit_l1(S, nc, es, io):
    x_in = io['x_in']
    wgq = io['wgq']
    wuq = io['wuq']
    wdq = io['wdq']
    id_in = io['id_in']
    xn0_out = io['xn0_out']
    gacc_out = io['gacc_out']
    sb = lambda n_, s_, d_=F32: es.enter_context(nc.sbuf_tensor(S.pfx + n_, s_, d_))
    ps = lambda n_, s_, d_=F32: es.enter_context(nc.psum_tensor(S.pfx + n_, s_, d_))

    xtile = sb("xtile", [128, 2, D])
    xq = sb("xq", [128, D], BF16)
    xqT = sb("xqT", [128, DC, TOK], BF16)
    xn_f = sb("xn_f", [128, D])
    gacc = sb("gacc_sb", [128, NT, D])
    amax = sb("amax", [128, I_FFN // 16])
    recip = sb("recip", [128, I_FFN // 16])
    hi_scr = sb("hi_scr", [128, D], BF16)
    r_bf = sb("r_bf", [128, D], BF16)
    m_f = sb("m_f", [128, D])
    sgn = sb("sgn", [128, D], BF16)
    mu = sb("mu", [128, 1]); rstd = sb("rstd", [128, 1]); sq = sb("sq", [128, D])
    xnb = sb("xnb", [128, D], BF16)
    wg_s = sb("wg_s", [128, DC, 768], BF16)
    wu_s = sb("wu_s", [128, DC, 768], BF16)
    wd_s = sb("wd_s", [128, 6, D], BF16)
    hsil = sb("hsil", [128, 768])
    h_f = sb("h_f", [128, 768])
    hq = sb("hq", [128, 768], BF16)
    hqT = sb("hqT", [128, 6, 128], BF16)
    ident = sb("identb", [128, 128], BF16)
    ps_ga = ps("ps_ga", [128, 1024])
    ps_up = ps("ps_up", [128, 1024])
    ps_dn = ps("ps_dn", [128, D])
    ps_ta = ps("ps_ta", [128, 1024], BF16)
    ps_tb = ps("ps_tb", [128, 1024], BF16)

    S.setup_consts([1e-5, -0.5], sb)
    S.wait("scalar", "c_init")

    S.op("gpsimd", lambda E: E.dma_start(out=ident[:], in_=id_in), inc="ld_w", n=16, drain=False)
    S.dma("sync", xtile[:, 0, :], x_in[0:128, :], "ld_x")

    for t in range(NT):
        if t + 1 < NT:
            if t + 1 >= 2:
                S.wait("sync", "v_xn", t)
            S.dma("sync", xtile[:, (t + 1) % 2, :], x_in[(t + 1) * 128:(t + 2) * 128, :], "ld_x")
        S.wait("vector", "ld_x", 16 * (t + 1))
        xt = xtile[:, t % 2, :]
        emit_ln(S, xt, mu, rstd, sq, xn_f[:])
        S.wait("vector", "st_xn", S.val("st_xn"))
        S.op("vector", lambda E: E.tensor_copy(xnb[:], xn_f[:]), inc="v_xnb")
        S.wait("sync", "v_xnb")
        S.dma("sync", xn0_out[t * 128:(t + 1) * 128, :], xnb[:], "st_xn")
        S.wait("vector", "tp_v", S.val("tp_v"))
        emit_quant(S, xn_f[:], D, amax, recip, hi_scr, r_bf, m_f, xq[:], sgn)
        S.wait("tensor", "q_done")
        S.wait("tensor", "ld_w", 16)
        cols = [(k * 128, xqT[:, k, t * 128:(t + 1) * 128]) for k in range(DC)]
        emit_transposes(S, xq[:], None, cols, ps_ta, ps_tb, ident, "tp")

    for pi, (i0, iw) in enumerate(I_PHASES):
        nkd = iw // 128
        S.wait("sync", "t_gu", S.val("t_gu"))
        S.wait("sync", "t_dn", S.val("t_dn"))
        for k in range(DC):
            S.op("sync", lambda E, k=k, i0=i0, iw=iw: E.dma_start_transpose(
                wg_s[:, k, 0:iw], wgq[i0:i0 + iw, k * 128:(k + 1) * 128]), inc="ld_w", n=16, drain=False)
            S.op("sync", lambda E, k=k, i0=i0, iw=iw: E.dma_start_transpose(
                wu_s[:, k, 0:iw], wuq[i0:i0 + iw, k * 128:(k + 1) * 128]), inc="ld_w", n=16, drain=False)
        for k in range(nkd):
            S.op("sync", lambda E, k=k, i0=i0: E.dma_start_transpose(
                wd_s[:, k, :], wdq[:, i0 + k * 128:i0 + (k + 1) * 128]), inc="ld_w", n=16, drain=False)
        S.wait("tensor", "ld_w")
        for t in range(NT):
            nchunks = [(0, 512), (512, iw - 512)] if iw > 512 else [(0, iw)]
            S.wait("tensor", "v_h", S.val("v_h"))
            S.wait("tensor", "s_sil", S.val("s_sil"))
            for k in range(DC):
                lhsT = xqT[:, k, t * 128:(t + 1) * 128]
                for ci, (n0, nw) in enumerate(nchunks):
                    S.op("tensor", lambda E, lhsT=lhsT, k=k, n0=n0, nw=nw: E.matmul(
                        ps_ga[:, n0:n0 + nw], lhsT, wg_s[:, k, n0:n0 + nw], start=(k == 0), stop=(k == DC - 1)))
                for ci, (n0, nw) in enumerate(nchunks):
                    last = (k == DC - 1) and (ci == len(nchunks) - 1)
                    S.op("tensor", lambda E, lhsT=lhsT, k=k, n0=n0, nw=nw: E.matmul(
                        ps_up[:, n0:n0 + nw], lhsT, wu_s[:, k, n0:n0 + nw], start=(k == 0), stop=(k == DC - 1)),
                        inc="t_gu" if last else None)
            S.wait("scalar", "t_gu")
            S.wait("scalar", "v_h", S.val("v_h"))
            S.op("scalar", lambda E, iw=iw: E.activation(hsil[:, 0:iw], ps_ga[:, 0:iw], AF.Silu), inc="s_sil")
            S.wait("vector", "s_sil")
            S.wait("vector", "q_xn", S.val("q_xn"))
            S.op("vector", lambda E, iw=iw: E.tensor_tensor(h_f[:, 0:iw], hsil[:, 0:iw], ps_up[:, 0:iw], ALU.mult), inc="v_h")
            S.wait("vector", "tp_v", S.val("tp_v"))
            emit_quant(S, h_f[:, 0:iw], iw, amax, recip, hi_scr, r_bf, m_f, hq[:, 0:iw], sgn)
            S.wait("tensor", "q_done")
            cols = [(k * 128, hqT[:, k, :]) for k in range(nkd)]
            emit_transposes(S, hq[:], None, cols, ps_ta, ps_tb, ident, "tp")
            S.wait("tensor", "tp_v")
            S.wait("tensor", "v_acc", S.val("v_acc"))
            for k in range(nkd):
                for (n0, nw) in [(0, 512), (512, 512)]:
                    S.op("tensor", lambda E, k=k, n0=n0, nw=nw: E.matmul(
                        ps_dn[:, n0:n0 + nw], hqT[:, k, :], wd_s[:, k, n0:n0 + nw], start=(k == 0), stop=(k == nkd - 1)),
                        inc="t_dn" if (k == nkd - 1 and n0 == 512) else None)
            S.wait("vector", "t_dn")
            if pi == 0:
                S.op("vector", lambda E, t=t: E.tensor_copy(gacc[:, t, :], ps_dn[:]), inc="v_acc")
            else:
                S.op("vector", lambda E, t=t: E.tensor_add(gacc[:, t, :], gacc[:, t, :], ps_dn[:]), inc="v_acc")

    S.wait("sync", "v_acc")
    for t in range(NT):
        S.dma("sync", gacc_out[t * 128:(t + 1) * 128, :], gacc[:, t, :], "st_g")
    S.wait("sync", "st_g")
    S.wait("sync", "st_xn")


def build_launch1():
    nc = mkbass()
    x_in = nc.dram_tensor("x", [TOK, D], F32, kind="ExternalInput").ap()
    wgq = nc.dram_tensor("wgq", [I_FFN, D], BF16, kind="ExternalInput").ap()
    wuq = nc.dram_tensor("wuq", [I_FFN, D], BF16, kind="ExternalInput").ap()
    wdq = nc.dram_tensor("wdq", [D, I_FFN], BF16, kind="ExternalInput").ap()
    id_in = nc.dram_tensor("ident", [128, 128], F32, kind="ExternalInput").ap()
    xn0_out = nc.dram_tensor("xn0", [TOK, D], BF16, kind="ExternalOutput").ap()
    gacc_out = nc.dram_tensor("gacc", [TOK, D], F32, kind="ExternalOutput").ap()
    es = ExitStack()
    S = Sched(nc); S._es = es
    io = dict(x_in=x_in, wgq=wgq, wuq=wuq, wdq=wdq, id_in=id_in, xn0_out=xn0_out, gacc_out=gacc_out)
    emit_l1(S, nc, es, io)
    with es, nc.Block() as block:
        S.build(block, es)
    return nc


def emit_l0(S, nc, es, io):
    wg_in = io['wg_in']
    wu_in = io['wu_in']
    wd_in = io['wd_in']
    wg_out = io['wg_out']
    wu_out = io['wu_out']
    wd_out = io['wd_out']
    sb = lambda n_, s_, d_=F32: es.enter_context(nc.sbuf_tensor(S.pfx + n_, s_, d_))
    ps = lambda n_, s_, d_=F32: es.enter_context(nc.psum_tensor(S.pfx + n_, s_, d_))
    wf = sb("wf", [128, I_FFN])
    wq = sb("wq", [128, I_FFN], BF16)
    amax = sb("amax", [128, I_FFN // 16]); recip = sb("recip", [128, I_FFN // 16])
    hi_scr = sb("hi_scr", [128, I_FFN], BF16); r_bf = sb("r_bf", [128, I_FFN], BF16)
    m_f = sb("m_f", [128, I_FFN]); sgn = sb("sgn", [128, I_FFN], BF16)

    jobs = []
    for r0, rw in [(0, 128), (128, 128), (256, 96)]:
        jobs.append((wg_in[r0:r0 + rw, :], wg_out[r0:r0 + rw, :], rw, D))
        jobs.append((wu_in[r0:r0 + rw, :], wu_out[r0:r0 + rw, :], rw, D))
    jobs.append((wd_in, wd_out, 128, I_FFN))

    for src_ap, dst_ap, rw, F in jobs:
        S.dma("sync", wf[0:rw, 0:F], src_ap, "ld_x")
        S.wait("vector", "ld_x")
        emit_quant(S, wf[0:rw, 0:F], F, amax[0:rw], recip[0:rw], hi_scr[0:rw],
                   r_bf[0:rw], m_f[0:rw], wq[0:rw, 0:F], sgn[0:rw])
        S.wait("sync", "q_done")
        S.dma("sync", dst_ap, wq[0:rw, 0:F], "st_xn")
        S.wait("vector", "st_xn", S.val("st_xn"))
    S.wait("sync", "st_xn")


def build_launch0():
    nc = mkbass()
    wg_in = nc.dram_tensor("wg_sl", [352, D], F32, kind="ExternalInput").ap()
    wu_in = nc.dram_tensor("wu_sl", [352, D], F32, kind="ExternalInput").ap()
    wd_in = nc.dram_tensor("wd_sl", [128, I_FFN], F32, kind="ExternalInput").ap()
    wg_out = nc.dram_tensor("wgq_sl", [352, D], BF16, kind="ExternalOutput").ap()
    wu_out = nc.dram_tensor("wuq_sl", [352, D], BF16, kind="ExternalOutput").ap()
    wd_out = nc.dram_tensor("wdq_sl", [128, I_FFN], BF16, kind="ExternalOutput").ap()
    es = ExitStack()
    S = Sched(nc); S._es = es
    io = dict(wg_in=wg_in, wu_in=wu_in, wd_in=wd_in, wg_out=wg_out, wu_out=wu_out, wd_out=wd_out)
    emit_l0(S, nc, es, io)
    with es, nc.Block() as block:
        S.build(block, es)
    return nc


def emit_xnT(S, xn_dram, xnT, xnb_scr, ps_ta, ps_tb, ident, ld_sem="ld_xn"):
    for t in range(NT):
        S.dma("sync", xnb_scr[:, t % 2, :], xn_dram[t * 128:(t + 1) * 128, :], ld_sem)
    for t in range(NT):
        S.wait("tensor", ld_sem, 16 * (t + 1))
        cols = [(k * 128, xnT[:, k, t * 128:(t + 1) * 128]) for k in range(DC)]
        emit_transposes(S, xnb_scr[:, t % 2, :], None, cols, ps_ta, ps_tb, ident, "txn")


def emit_lam(S, beta_in, lnl_b, M_iota, Mp_iota, tmask, Tmat, pv2, lam128c, onesrow, cst, ps_misc, build_T=True, build_lamv=True, build_pv2=True):
    v, s, g, t = "vector", "scalar", "gpsimd", "tensor"
    beta_sb = lnl_b
    br = S.beta_row
    lamv_ap = S.lamv
    S.wait(s, "ld_beta")
    S.wait(s, "c_init")
    c_ln2 = S.consts[-LN2]
    lam_row_ap = S.lam_row
    S.op(s, lambda E, c=c_ln2, lr=lam_row_ap: E.activation(lr, br, AF.Exp, scale=c[0:1]), inc="s_lam")
    S.wait(v, "s_lam")
    S.op(v, lambda E, lr=lam_row_ap: E.tensor_scalar(lr, lr, -1.0, 1.0, ALU.mult, ALU.add))
    S.op(v, lambda E, lr=lam_row_ap: E.tensor_scalar(lr, lr, 1e-38, 1.0 - 1e-9, ALU.max, ALU.min), inc="v_lam")
    S.wait(s, "v_lam")
    S.op(s, lambda E, lr=lam_row_ap: E.activation(lr, lr, AF.Ln), inc="s_lnl")
    S.wait(t, "s_lnl")
    S.wait(t, "c_ones")
    S.op(t, lambda E, lr=lam_row_ap: E.matmul(ps_misc[:, 0:16], onesrow[:], lr, start=True, stop=True), inc="t_lnlb")
    S.wait(v, "t_lnlb")
    S.op(v, lambda E: E.tensor_copy(lnl_b[:, 0:16], ps_misc[:, 0:16]), inc="v_lnlb")
    S.wait(t, "v_lnlb")
    S.op(t, lambda E, lr=lam_row_ap: E.matmul(ps_misc[0:16, 16:17], lr, onesrow[0:1, 0:1], start=True, stop=True), inc="t_lnlc")
    S.wait(s, "t_lnlc")
    c128 = S.consts[128.0]
    S.op(s, lambda E, c=c128: E.activation(lam128c[:, 0:1], ps_misc[0:16, 16:17], AF.Exp, scale=c[0:16]), inc="s_l128")
    S.op(g, lambda E: E.iota(M_iota[:], [[1, 128]], channel_multiplier=0, allow_small_or_imprecise_dtypes=True))
    S.op(g, lambda E: E.iota(Mp_iota[:], [[0, 128]], channel_multiplier=1, allow_small_or_imprecise_dtypes=True))
    S.op(g, lambda E: E.tensor_tensor(M_iota[:], M_iota[:], Mp_iota[:], ALU.subtract), inc="g_M")
    S.wait(v, "g_M")
    S.op(v, lambda E: E.tensor_scalar(tmask[:], M_iota[:], 0.0, None, ALU.is_ge), inc="v_msk")
    if build_T:
        S.wait(s, "v_lnlb")
        S.wait(s, "g_M")
        for h in range(H):
            S.op(s, lambda E, h=h: E.activation(Tmat[:, h, :], M_iota[:], AF.Exp, scale=lnl_b[:, h:h + 1]),
                 inc="s_T" if h == H - 1 else None)
        S.wait(v, "s_T")
        S.wait(v, "v_msk")
        for h in range(H):
            S.op(v, lambda E, h=h: E.tensor_tensor(Tmat[:, h, :], Tmat[:, h, :], tmask[:], ALU.mult),
                 inc="v_T" if h == H - 1 else None)
    if build_pv2:
        S.op(g, lambda E: E.iota(Mp_iota[0:1, :], [[1, 128]], base=1, channel_multiplier=0,
                                 allow_small_or_imprecise_dtypes=True), inc="g_pv")
        S.wait(s, "g_pv")
        S.wait(s, "s_lnl")
        for h in range(H):
            S.op(s, lambda E, h=h, lr=lam_row_ap: E.activation(pv2[0:1, h * 128:(h + 1) * 128], Mp_iota[0:1, :], AF.Exp,
                                                scale=lr[0:1, h:h + 1]),
                 inc="s_pv" if h == H - 1 else None)
    if build_lamv:
        S.op(g, lambda E: E.iota(Mp_iota[:, 0:1], [[0, 1]], base=127, channel_multiplier=-1,
                                 allow_small_or_imprecise_dtypes=True), inc="g_lv")
        S.wait(s, "g_lv")
        for h in range(H):
            S.op(s, lambda E, h=h, lv=lamv_ap: E.activation(lv[:, h:h + 1], Mp_iota[:, 0:1], AF.Exp,
                                                scale=lnl_b[:, h:h + 1]),
                 inc="s_lv" if h == H - 1 else None)


def emit_l2(S, nc, es, io):
    xn_in = io['xn_in']
    wv_in = io['wv_in']
    beta_in = io['beta_in']
    id_in = io['id_in']
    st_out = io['st_out']
    sums_out = io['sums_out']
    sb = lambda n_, s_, d_=F32: es.enter_context(nc.sbuf_tensor(S.pfx + n_, s_, d_))
    ps = lambda n_, s_, d_=F32: es.enter_context(nc.psum_tensor(S.pfx + n_, s_, d_))
    xnb = sb("xnb", [128, 2, D], BF16)
    xnT = sb("xnT", [128, DC, TOK], BF16)
    wv_s = sb("wv_s", [128, DC, D], BF16)
    vbuf = sb("vbuf", [128, NT, D], BF16)
    stb = sb("stb", [128, D], BF16)
    sumrow = sb("sumrow", [1, NT * D])
    ident = sb("identb", [128, 128], BF16)
    onesrow = sb("onesrow", [1, 128])
    lnl_b = sb("lnl_b", [128, 16]); M_iota = sb("M_iota", [128, 128]); Mp_iota = sb("Mp_iota", [128, 128])
    tmask = sb("tmask", [128, 128]); Tmat = sb("Tmat", [128, H, 128], BF16)
    pv2 = sb("pv2", [1, H * 128], BF16); lam128c = sb("lam128c", [16, 1])
    lamv = sb("lamv", [128, H], BF16)
    beta_row = sb("beta_row", [1, H]); lam_row = sb("lam_row", [1, H])
    ps_ta = ps("ps_ta", [128, 1024], BF16)
    ps_tb = ps("ps_tb", [128, 1024], BF16)
    ps_v = ps("ps_v", [128, D])
    ps_sc = ps("ps_sc", [128, D])
    ps_sum = ps("ps_sum", [1, D])

    S.setup_consts([-LN2, 128.0], sb)
    S.beta_row = beta_row[:]; S.lam_row = lam_row[:]; S.lamv = lamv[:]

    S.op("gpsimd", lambda E: E.dma_start(out=ident[:], in_=id_in), inc="ld_w", n=16, drain=False)
    S.op("gpsimd", lambda E: E.memset(onesrow[:], 1.0), inc="c_ones")
    S.dma("sync", beta_row[:], beta_in, "ld_beta")
    S.dma("gpsimd", wv_s[:].rearrange("p c d -> (c p) d") if False else wv_s[:, 0, :], wv_in[0:128, :], "ld_w")
    for k in range(1, DC):
        S.dma("gpsimd", wv_s[:, k, :], wv_in[k * 128:(k + 1) * 128, :], "ld_w")
    emit_lam(S, beta_in, lnl_b, M_iota, Mp_iota, tmask, Tmat, pv2, lam128c, onesrow, None, ps_v)

    for t in range(NT):
        if t >= 2:
            S.wait("sync", "tp_t", 2 * (t - 1))
        S.dma("sync", xnb[:, t % 2, :], xn_in[t * 128:(t + 1) * 128, :], "ld_xn")
        S.wait("tensor", "ld_xn", 16 * (t + 1))
        S.wait("tensor", "ld_w", 16)
        cols = [(k * 128, xnT[:, k, t * 128:(t + 1) * 128]) for k in range(DC)]
        emit_transposes(S, xnb[:, t % 2, :], None, cols, ps_ta, ps_tb, ident, "tp")
        S.wait("tensor", "tp_v", 2 * (t + 1))
        S.wait("tensor", "ld_w", 16 * 9)
        S.wait("tensor", "s_l128", S.val("s_l128"))
        S.wait("tensor", "v_vev", S.val("v_vev"))
        for k in range(DC):
            for n0 in (0, 512):
                S.op("tensor", lambda E, k=k, t=t, n0=n0: E.matmul(
                    ps_v[:, n0:n0 + 512], xnT[:, k, t * 128:(t + 1) * 128], wv_s[:, k, n0:n0 + 512],
                    start=(k == 0), stop=(k == DC - 1)), inc="t_v" if (k == DC - 1 and n0 == 512) else None)
        S.wait("vector", "t_v")
        S.op("vector", lambda E, t=t: E.tensor_copy(vbuf[:, t, :], ps_v[:]), inc="v_vev")

    S.wait("tensor", "v_T")
    S.wait("tensor", "s_lv")
    for t in range(NT):
        S.wait("tensor", "v_vev", min(S.val("v_vev"), t + 1))
        S.wait("tensor", "v_stev", S.val("v_stev"))
        S.wait("tensor", "v_smev", S.val("v_smev"))
        for h in range(H):
            S.op("tensor", lambda E, t=t, h=h: E.matmul(
                ps_sc[:, h * DH:(h + 1) * DH], Tmat[:, h, :], vbuf[:, t, h * DH:(h + 1) * DH],
                start=True, stop=True), inc="t_sc" if h == H - 1 else None)
            S.op("tensor", lambda E, t=t, h=h: E.matmul(
                ps_sum[0:1, h * DH:(h + 1) * DH], lamv[:, h:h + 1], vbuf[:, t, h * DH:(h + 1) * DH],
                start=True, stop=True), inc="t_sm" if h == H - 1 else None)
        S.wait("vector", "t_sc")
        S.wait("vector", "st_st", S.val("st_st"))
        S.op("vector", lambda E, t=t: E.tensor_copy(stb[:], ps_sc[:]), inc="v_stev")
        S.wait("sync", "v_stev")
        S.dma("sync", st_out[t * 128:(t + 1) * 128, :], stb[:], "st_st")
        S.wait("vector", "t_sm")
        S.op("vector", lambda E, t=t: E.tensor_copy(sumrow[0:1, t * D:(t + 1) * D], ps_sum[0:1, :]), inc="v_smev")
    S.wait("sync", "v_smev")
    S.dma("sync", sums_out.rearrange("t d -> (t d)").unsqueeze(0), sumrow[:], "st_sm")
    S.wait("sync", "st_sm")
    S.wait("sync", "st_st")



def build_launch2():
    nc = mkbass()
    xn_in = nc.dram_tensor("xn0", [TOK, D], BF16, kind="ExternalInput").ap()
    wv_in = nc.dram_tensor("WvT", [D, D], F32, kind="ExternalInput").ap()
    beta_in = nc.dram_tensor("beta", [1, H], F32, kind="ExternalInput").ap()
    id_in = nc.dram_tensor("ident", [128, 128], F32, kind="ExternalInput").ap()
    st_out = nc.dram_tensor("state_local", [TOK, D], BF16, kind="ExternalOutput").ap()
    sums_out = nc.dram_tensor("sums", [NT, D], F32, kind="ExternalOutput").ap()
    es = ExitStack()
    S = Sched(nc); S._es = es
    io = dict(xn_in=xn_in, wv_in=wv_in, beta_in=beta_in, id_in=id_in, st_out=st_out, sums_out=sums_out)
    emit_l2(S, nc, es, io)
    with es, nc.Block() as block:
        S.build(block, es)
    return nc


def emit_l3(S, nc, es, io):
    xn_in = io['xn_in']
    stl_in = io['stl_in']
    sums_in = io['sums_in']
    x_in = io['x_in']
    gacc_in = io['gacc_in']
    wq_in = io['wq_in']
    wg_in = io['wg_in']
    wo_in = io['wo_in']
    beta_in = io['beta_in']
    gam_in = io['gam_in']
    bet_in = io['bet_in']
    id_in = io['id_in']
    cidx_in = io['cidx_in']
    x1_out = io['x1_out']
    carr_dram = io['carr_dram']
    carrmy_dram = io['carrmy_dram']
    sb = lambda n_, s_, d_=F32: es.enter_context(nc.sbuf_tensor(S.pfx + n_, s_, d_))
    ps = lambda n_, s_, d_=F32: es.enter_context(nc.psum_tensor(S.pfx + n_, s_, d_))
    xnb = sb("xnb", [128, 2, D], BF16)
    xnT_t = sb("xnT_t", [128, DC, 128], BF16)
    wq_s = sb("wq_s", [128, DC, D], BF16)
    wg_s = sb("wg_s", [128, DC, D], BF16)
    wo_s = sb("wo_s", [128, DC, D], BF16)
    ident = sb("identb", [128, 128], BF16)
    onesrow = sb("onesrow", [1, 128])
    lnl_b = sb("lnl_b", [128, 16]); M_iota = sb("M_iota", [128, 128]); Mp_iota = sb("Mp_iota", [128, 128])
    tmask = sb("tmask", [128, 128])
    pv2 = sb("pv2", [1, H * 128], BF16); lam128c = sb("lam128c", [16, 1])
    beta_row = sb("beta_row", [1, H]); lam_row = sb("lam_row", [1, H])
    sums_sb = sb("sums_sb", [16, 2 * NT * DH])
    carr_sb = sb("carr_sb", [16, 2 * NT * DH])
    carrg = sb("carrg", [NT, D])
    carr1p = sb("carr1p", [1, NT * D], BF16)
    gbrow = sb("gbrow", [1, 2 * D])
    cidx = sb("cidx", [NT, 1], I32)
    gb = sb("gb", [128, D]); bb = sb("bb", [128, D])
    stl_b = sb("stl_b", [128, D], BF16)
    stf = sb("stf", [128, D])
    ret_f = sb("ret_f", [128, D])
    y_f = sb("y_f", [128, D])
    g0_f = sb("g0_f", [128, D])
    u_b = sb("u_b", [128, D], BF16)
    uT = sb("uT", [128, DC, 128], BF16)
    xt = sb("xt", [128, D]); gat = sb("gat", [128, D]); gsil = sb("gsil", [128, D])
    x1_b = sb("x1_b", [128, D])
    mu = sb("mu", [128, 1]); rstd = sb("rstd", [128, 1]); sq = sb("sq", [128, D])
    ps_c = ps("ps_c", [128, D])
    ps_q = ps("ps_q", [128, D])
    ps_g = ps("ps_g", [128, D])
    ps_ta = ps("ps_ta", [128, 1024], BF16)
    ps_tb = ps("ps_tb", [128, 1024], BF16)

    S.setup_consts([-LN2, 128.0, 1e-5, -0.5], sb)
    S.wait("scalar", "c_init")
    S.beta_row = beta_row[:]; S.lam_row = lam_row[:]; S.lamv = None

    S.op("gpsimd", lambda E: E.dma_start(out=ident[:], in_=id_in), inc="ld_w", n=16, drain=False)
    S.op("gpsimd", lambda E: E.memset(onesrow[:], 1.0), inc="c_ones")
    S.dma("sync", beta_row[:], beta_in, "ld_beta")
    S.dma("sync", cidx[:], cidx_in, "ld_misc")
    for name, src_ap, dst in [("wq", wq_in, wq_s), ("wg", wg_in, wg_s), ("wo", wo_in, wo_s)]:
        for k in range(DC):
            S.dma("gpsimd", dst[:, k, :], src_ap[k * 128:(k + 1) * 128, :], "ld_w")
    emit_lam(S, beta_in, lnl_b, M_iota, Mp_iota, tmask, None, pv2, lam128c, onesrow, None, ps_q,
             build_T=False, build_lamv=False)
    S.dma("sync", gbrow[0:1, 0:D], gam_in, "ld_misc")
    S.dma("sync", gbrow[0:1, D:2 * D], bet_in, "ld_misc")
    S.wait("tensor", "ld_misc", 16 * 3)
    S.wait("tensor", "c_ones")
    for n0 in (0, 512):
        S.op("tensor", lambda E, n0=n0: E.matmul(ps_c[:, n0:n0 + 512], onesrow[:], gbrow[0:1, n0:n0 + 512], start=True, stop=True))
    for n0 in (0, 512):
        S.op("tensor", lambda E, n0=n0: E.matmul(ps_g[:, n0:n0 + 512], onesrow[:], gbrow[0:1, D + n0:D + n0 + 512], start=True, stop=True),
             inc="t_gbb" if n0 == 512 else None)
    S.wait("vector", "t_gbb")
    S.op("vector", lambda E: E.tensor_copy(gb[:], ps_c[:]))
    S.op("vector", lambda E: E.tensor_copy(bb[:], ps_g[:]), inc="v_gbb")

    S.dma("sync", sums_sb[:].rearrange("h (c d) -> h c d", d=DH),
          sums_in.rearrange("c (h d) -> h c d", d=DH), "ld_sums")
    S.wait("vector", "ld_sums")
    S.wait("vector", "s_l128")
    S.op("vector", lambda E: E.memset(carr_sb[:, 0:DH], 0.0))
    for c in range(1, 2 * NT):
        S.op("vector", lambda E, c=c: E.scalar_tensor_tensor(
            carr_sb[:, c * DH:(c + 1) * DH], carr_sb[:, (c - 1) * DH:c * DH], lam128c[:, 0:1],
            sums_sb[:, (c - 1) * DH:c * DH], ALU.mult, ALU.add),
            inc="v_chain" if c == 2 * NT - 1 else None)
    S.wait("sync", "v_chain")
    S.dma("sync", carr_dram.rearrange("c (h d) -> h c d", d=DH),
          carr_sb[:].rearrange("h (c d) -> h c d", d=DH), "st_carr")
    S.wait("gpsimd", "st_carr")
    S.wait("gpsimd", "ld_misc", 16)
    S.op("gpsimd", lambda E: E.indirect_dma_start(carrg[:], None, carr_dram,
                                                  bass.IndirectOffsetOnAxis(cidx[:], 0)),
         inc="g_carr", n=16, drain=False)
    S.wait("sync", "g_carr")
    S.dma("sync", carrmy_dram, carrg[:], "st_carr")
    S.wait("sync", "st_carr", 32)
    S.wait("gpsimd", "st_carr", 32)
    S.op("gpsimd", lambda E: E.dma_start(out=carr1p[:], in_=carrmy_dram.rearrange("c d -> (c d)").unsqueeze(0)),
         inc="ld_carr", n=16, drain=False)

    for t in range(NT):
        if t >= 2:
            S.wait("sync", "tp_t", 4 * (t - 2) + 2)
        S.dma("sync", xnb[:, t % 2, :], xn_in[t * 128:(t + 1) * 128, :], "ld_xn")
        S.wait("tensor", "ld_xn", 16 * (t + 1))
        S.wait("tensor", "ld_w", 16)
        S.wait("tensor", "s_l128", 1)
        cols = [(k * 128, xnT_t[:, k, :]) for k in range(DC)]
        S.wait("tensor", "t_q", S.val("t_q"))
        emit_transposes(S, xnb[:, t % 2, :], None, cols, ps_ta, ps_tb, ident, "tp")
        txn_v_done = S.val("tp_v")
        S.wait("tensor", "ld_carr")
        S.wait("tensor", "s_pv")
        S.wait("tensor", "v_ret", S.val("v_ret"))
        S.wait("tensor", "v_gbb")
        for h in range(H):
            S.op("tensor", lambda E, t=t, h=h: E.matmul(
                ps_c[:, h * DH:(h + 1) * DH], pv2[0:1, h * 128:(h + 1) * 128],
                carr1p[0:1, t * D + h * DH:t * D + (h + 1) * DH], start=True, stop=True),
                inc="t_corr" if h == H - 1 else None)
        S.wait("tensor", "tp_v", txn_v_done)
        S.wait("tensor", "ld_w", 16 * (1 + 3 * DC))
        S.wait("tensor", "v_ret", S.val("v_ret"))
        S.wait("tensor", "s_g0", S.val("s_g0"))
        for k in range(DC):
            for n0 in (0, 512):
                S.op("tensor", lambda E, k=k, n0=n0: E.matmul(
                    ps_q[:, n0:n0 + 512], xnT_t[:, k, :], wq_s[:, k, n0:n0 + 512],
                    start=(k == 0), stop=(k == DC - 1)))
        for k in range(DC):
            for n0 in (0, 512):
                S.op("tensor", lambda E, k=k, n0=n0: E.matmul(
                    ps_g[:, n0:n0 + 512], xnT_t[:, k, :], wg_s[:, k, n0:n0 + 512],
                    start=(k == 0), stop=(k == DC - 1)),
                    inc="t_q" if (k == DC - 1 and n0 == 512) else None)
        S.dma("sync", stl_b[:], stl_in[t * 128:(t + 1) * 128, :], "ld_stl")
        S.wait("vector", "ld_stl", 16 * (t + 1))
        S.wait("vector", "t_corr")
        S.op("vector", lambda E: E.tensor_tensor(stf[:], stl_b[:], ps_c[:], ALU.add))
        S.wait("vector", "t_q")
        S.op("vector", lambda E: E.tensor_tensor(ret_f[:], ps_q[:], stf[:], ALU.mult), inc="v_ret")
        emit_ln(S, ret_f[:], mu, rstd, sq, y_f[:])
        S.op("vector", lambda E: E.tensor_mul(y_f[:], y_f[:], gb[:]))
        S.op("vector", lambda E: E.tensor_add(y_f[:], y_f[:], bb[:]), inc="v_y")
        S.wait("scalar", "t_q")
        S.wait("scalar", "v_u", S.val("v_u"))
        S.op("scalar", lambda E: E.activation(g0_f[:], ps_g[:], AF.Silu), inc="s_g0")
        S.wait("vector", "s_g0")
        S.wait("vector", "tp_t", S.val("tp_t"))
        S.op("vector", lambda E: E.tensor_tensor(u_b[:], y_f[:], g0_f[:], ALU.mult), inc="v_u")
        S.wait("tensor", "v_u")
        colsu = [(k * 128, uT[:, k, :]) for k in range(DC)]
        emit_transposes(S, u_b[:], None, colsu, ps_ta, ps_tb, ident, "tp")
        S.wait("tensor", "tp_v")
        S.wait("tensor", "v_x1", S.val("v_x1"))
        for k in range(DC):
            for n0 in (0, 512):
                S.op("tensor", lambda E, k=k, n0=n0: E.matmul(
                    ps_q[:, n0:n0 + 512], uT[:, k, :], wo_s[:, k, n0:n0 + 512],
                    start=(k == 0), stop=(k == DC - 1)),
                    inc="t_s" if (k == DC - 1 and n0 == 512) else None)
        S.wait("sync", "v_x1", S.val("v_x1"))
        S.dma("sync", xt[:], x_in[t * 128:(t + 1) * 128, :], "ld_x")
        S.dma("sync", gat[:], gacc_in[t * 128:(t + 1) * 128, :], "ld_x")
        S.wait("scalar", "ld_x", 32 * (t + 1))
        S.op("scalar", lambda E: E.activation(gsil[:], gat[:], AF.Silu), inc="s_gs")
        S.wait("vector", "s_gs")
        S.wait("vector", "t_s")
        S.wait("vector", "st_x1", S.val("st_x1"))
        S.op("vector", lambda E: E.tensor_tensor(x1_b[:], ps_q[:], gsil[:], ALU.mult))
        S.op("vector", lambda E: E.tensor_add(x1_b[:], x1_b[:], xt[:]), inc="v_x1")
        S.wait("sync", "v_x1")
        S.dma("sync", x1_out[t * 128:(t + 1) * 128, :], x1_b[:], "st_x1")
    S.wait("sync", "st_x1")



def build_launch3():
    nc = mkbass()
    xn_in = nc.dram_tensor("xn0", [TOK, D], BF16, kind="ExternalInput").ap()
    stl_in = nc.dram_tensor("state_local", [TOK, D], BF16, kind="ExternalInput").ap()
    sums_in = nc.dram_tensor("sums_all", [2 * NT, D], F32, kind="ExternalInput").ap()
    x_in = nc.dram_tensor("x", [TOK, D], F32, kind="ExternalInput").ap()
    gacc_in = nc.dram_tensor("gacc", [TOK, D], F32, kind="ExternalInput").ap()
    wq_in = nc.dram_tensor("WqT", [D, D], F32, kind="ExternalInput").ap()
    wg_in = nc.dram_tensor("WgT", [D, D], F32, kind="ExternalInput").ap()
    wo_in = nc.dram_tensor("WoT", [D, D], F32, kind="ExternalInput").ap()
    beta_in = nc.dram_tensor("beta", [1, H], F32, kind="ExternalInput").ap()
    gam_in = nc.dram_tensor("ln_gamma", [1, D], F32, kind="ExternalInput").ap()
    bet_in = nc.dram_tensor("ln_beta", [1, D], F32, kind="ExternalInput").ap()
    id_in = nc.dram_tensor("ident", [128, 128], F32, kind="ExternalInput").ap()
    cidx_in = nc.dram_tensor("carry_idx", [NT, 1], I32, kind="ExternalInput").ap()
    x1_out = nc.dram_tensor("x1", [TOK, D], F32, kind="ExternalOutput").ap()
    carr_dram = nc.dram_tensor("carr_dram", [2 * NT, D], F32).ap()
    carrmy_dram = nc.dram_tensor("carrmy_dram", [NT, D], F32).ap()
    es = ExitStack()
    S = Sched(nc); S._es = es
    io = dict(xn_in=xn_in, stl_in=stl_in, sums_in=sums_in, x_in=x_in, gacc_in=gacc_in, wq_in=wq_in, wg_in=wg_in, wo_in=wo_in, beta_in=beta_in, gam_in=gam_in, bet_in=bet_in, id_in=id_in, cidx_in=cidx_in, x1_out=x1_out, carr_dram=carr_dram, carrmy_dram=carrmy_dram)
    emit_l3(S, nc, es, io)
    with es, nc.Block() as block:
        S.build(block, es)
    return nc


def emit_rms(S, x_tile, rstd, sq, wb_tile, out_bf, eps=1e-6):
    v, s = "vector", "scalar"
    S.op(v, lambda E: E.tensor_mul(sq[:], x_tile, x_tile), drain=False)
    S.op(v, lambda E: E.reduce_sum(rstd[:], sq[:], AX.X))
    S.op(v, lambda E: E.tensor_scalar_mul(rstd[:], rstd[:], 1.0 / 1024), inc="v_rms")
    S.wait(s, "v_rms")
    c_eps, c_mh = S.consts[eps], S.consts[-0.5]
    S.op(s, lambda E, c=c_eps: E.activation(rstd[:], rstd[:], AF.Ln, bias=c))
    S.op(s, lambda E, c=c_mh: E.activation(rstd[:], rstd[:], AF.Exp, scale=c), inc="s_rms")
    S.wait(v, "s_rms")
    S.op(v, lambda E: E.tensor_scalar(sq[:], x_tile, rstd[:], None, ALU.mult), drain=False)
    S.op(v, lambda E: E.tensor_tensor(out_bf, sq[:], wb_tile, ALU.mult), inc="v_xn")


def emit_bcast_row(S, row_ap, onesrow, ps_full, dst, tag):
    for n0 in (0, 512):
        S.op("tensor", lambda E, n0=n0: E.matmul(ps_full[:, n0:n0 + 512], onesrow[:], row_ap[0:1, n0:n0 + 512],
                                                 start=True, stop=True), inc=tag + "_t" if n0 == 512 else None)
    S.wait("vector", tag + "_t")
    S.op("vector", lambda E: E.tensor_copy(dst, ps_full[:]), inc=tag + "_v")


def emit_l4(S, nc, es, io):
    x1_in = io['x1_in']
    n1_in = io['n1_in']
    ws_in = io['ws_in']
    we_in = io['we_in']
    bias_in = io['bias_in']
    id_in = io['id_in']
    xn1_out = io['xn1_out']
    xnr1_out = io['xnr1_out']
    g1_out = io['g1_out']
    sb = lambda n_, s_, d_=F32: es.enter_context(nc.sbuf_tensor(S.pfx + n_, s_, d_))
    ps = lambda n_, s_, d_=F32: es.enter_context(nc.psum_tensor(S.pfx + n_, s_, d_))
    xt = sb("xt", [128, 2, D])
    wc = sb("wc", [128, DC, D], BF16)
    wscr = sb("wscr", [128, 2, D])
    ident = sb("identb", [128, 128], BF16)
    onesrow = sb("onesrow", [1, 128])
    n1b = sb("n1b", [128, D]); biasb = sb("biasb", [128, D])
    rowscr = sb("rowscr", [1, 2 * D])
    xnb = sb("xnb", [128, D], BF16)
    xnrb = sb("xnrb", [128, D], BF16)
    mu = sb("mu", [128, 1])
    xnT_t = sb("xnT_t", [128, DC, 128], BF16)
    gf = sb("gf", [128, D])
    rstd = sb("rstd", [128, 1]); sq = sb("sq", [128, D])
    ps_a = ps("ps_a", [128, 1024])
    ps_b = ps("ps_b", [128, 1024])
    ps_ta = ps("ps_ta", [128, 1024], BF16)
    ps_tb = ps("ps_tb", [128, 1024], BF16)

    S.setup_consts([1e-6, -0.5, 1e-5], sb)
    S.wait("scalar", "c_init")
    S.op("gpsimd", lambda E: E.dma_start(out=ident[:], in_=id_in), inc="ld_w", n=16, drain=False)
    S.op("gpsimd", lambda E: E.memset(onesrow[:], 1.0), inc="c_ones")
    for k in range(DC):
        S.dma("sync", wscr[:, 0, :], ws_in[k * 128:(k + 1) * 128, :], "ld_stl")
        S.dma("sync", wscr[:, 1, :], we_in[k * 128:(k + 1) * 128, :], "ld_stl")
        S.wait("vector", "ld_stl", 32 * (k + 1))
        S.op("vector", lambda E, k=k: E.tensor_add(wc[:, k, :], wscr[:, 0, :], wscr[:, 1, :]), inc="v_wc")
        if k < DC - 1:
            S.wait("sync", "v_wc")
    S.dma("sync", rowscr[0:1, 0:D], n1_in, "ld_misc")
    S.dma("sync", rowscr[0:1, D:2 * D], bias_in, "ld_misc")
    S.wait("tensor", "ld_misc")
    S.wait("tensor", "c_ones")
    emit_bcast_row(S, rowscr[0:1, 0:D], onesrow, ps_a, n1b[:], "bc1")
    S.wait("tensor", "bc1_v")
    emit_bcast_row(S, rowscr[0:1, D:2 * D], onesrow, ps_a, biasb[:], "bc2")

    for t in range(NT):
        if t >= 2:
            S.wait("sync", "v_xn", t - 1)
        S.dma("sync", xt[:, t % 2, :], x1_in[t * 128:(t + 1) * 128, :], "ld_x")
        S.wait("vector", "ld_x", 16 * (t + 1))
        S.wait("vector", "bc1_v"); S.wait("vector", "bc2_v")
        S.wait("vector", "tp_t", S.val("tp_t"))
        S.wait("vector", "st_xn", S.val("st_xn"))
        emit_ln(S, xt[:, t % 2, :], mu, rstd, sq, sq[:], eps=1e-5)
        S.op("vector", lambda E: E.tensor_copy(xnb[:], sq[:]), inc="v_xnb")
        emit_rms(S, xnb[:], rstd, sq, n1b[:], xnrb[:])
        S.wait("sync", "v_xn")
        S.dma("sync", xn1_out[t * 128:(t + 1) * 128, :], xnb[:], "st_xn")
        S.dma("sync", xnr1_out[t * 128:(t + 1) * 128, :], xnrb[:], "st_xn")
        S.wait("tensor", "v_xn", S.val("v_xn"))
        S.wait("tensor", "ld_w", 16)
        cols = [(k * 128, xnT_t[:, k, :]) for k in range(DC)]
        S.wait("tensor", "t_g", S.val("t_g"))
        emit_transposes(S, xnb[:], None, cols, ps_ta, ps_tb, ident, "tp")
        S.wait("tensor", "tp_v")
        S.wait("tensor", "v_wc", DC)
        S.wait("tensor", "v_g1", S.val("v_g1"))
        for k in range(DC):
            for n0 in (0, 512):
                S.op("tensor", lambda E, k=k, n0=n0: E.matmul(
                    ps_b[:, n0:n0 + 512], xnT_t[:, k, :], wc[:, k, n0:n0 + 512],
                    start=(k == 0), stop=(k == DC - 1)),
                    inc="t_g" if (k == DC - 1 and n0 == 512) else None)
        S.wait("vector", "t_g")
        S.wait("scalar", "v_g1", S.val("v_g1"))
        S.op("vector", lambda E: E.tensor_tensor(gf[:], ps_b[:], biasb[:], ALU.add), inc="v_gb")
        S.wait("scalar", "v_gb")
        S.op("scalar", lambda E: E.activation(gf[:], gf[:], AF.Silu), inc="s_g1")
        S.wait("sync", "s_g1")
        S.dma("sync", g1_out[t * 128:(t + 1) * 128, :], gf[:], "st_g1")
        S.wait("vector", "st_g1", S.val("st_g1"))
        S.op("vector", lambda E: E.engine_nop(), inc="v_g1", drain=False)
    S.wait("sync", "st_g1")
    S.wait("sync", "st_xn")



def build_launch4():
    nc = mkbass()
    x1_in = nc.dram_tensor("x1", [TOK, D], F32, kind="ExternalInput").ap()
    n1_in = nc.dram_tensor("norm1_w", [1, D], F32, kind="ExternalInput").ap()
    ws_in = nc.dram_tensor("WsT", [D, D], F32, kind="ExternalInput").ap()
    we_in = nc.dram_tensor("WeT", [D, D], F32, kind="ExternalInput").ap()
    bias_in = nc.dram_tensor("bias", [1, D], F32, kind="ExternalInput").ap()
    id_in = nc.dram_tensor("ident", [128, 128], F32, kind="ExternalInput").ap()
    xn1_out = nc.dram_tensor("xn1", [TOK, D], BF16, kind="ExternalOutput").ap()
    xnr1_out = nc.dram_tensor("xnr1", [TOK, D], BF16, kind="ExternalOutput").ap()
    g1_out = nc.dram_tensor("g1", [TOK, D], F32, kind="ExternalOutput").ap()
    es = ExitStack()
    S = Sched(nc); S._es = es
    io = dict(x1_in=x1_in, n1_in=n1_in, ws_in=ws_in, we_in=we_in, bias_in=bias_in, id_in=id_in, xn1_out=xn1_out, xnr1_out=xnr1_out, g1_out=g1_out)
    emit_l4(S, nc, es, io)
    with es, nc.Block() as block:
        S.build(block, es)
    return nc


NTF = 32


def emit_rope_qknorm(S, src_ps, nheads, ct, st, qf, qr, sqs, rstd8, out_bf, eps=1e-6):
    v, s = "vector", "scalar"
    W = nheads * DH
    S.op(v, lambda E: E.tensor_copy(qf[:, 0:W], src_ps), inc="v_qev")
    qe = qf[:, 0:W].rearrange("p (h f two) -> p h f two", h=nheads, two=2)[:, :, :, 0:1].squeeze(3)
    qo = qf[:, 0:W].rearrange("p (h f two) -> p h f two", h=nheads, two=2)[:, :, :, 1:2].squeeze(3)
    re = qr[:, 0:W].rearrange("p (h f two) -> p h f two", h=nheads, two=2)[:, :, :, 0:1].squeeze(3)
    im = qr[:, 0:W].rearrange("p (h f two) -> p h f two", h=nheads, two=2)[:, :, :, 1:2].squeeze(3)
    cb = ct.unsqueeze(1).broadcast_to([128, nheads, 32])
    sb_ = st.unsqueeze(1).broadcast_to([128, nheads, 32])
    s1 = sqs[:, 0:W // 2].rearrange("p (h f) -> p h f", h=nheads)
    S.op(v, lambda E: E.tensor_tensor(re, qe, cb, ALU.mult), drain=False)
    S.op(v, lambda E: E.tensor_tensor(s1, qo, sb_, ALU.mult), drain=False)
    S.op(v, lambda E: E.tensor_tensor(re, re, s1, ALU.subtract), drain=False)
    S.op(v, lambda E: E.tensor_tensor(im, qe, sb_, ALU.mult), drain=False)
    S.op(v, lambda E: E.tensor_tensor(s1, qo, cb, ALU.mult), drain=False)
    S.op(v, lambda E: E.tensor_tensor(im, im, s1, ALU.add), drain=False)
    S.op(v, lambda E: E.tensor_mul(sqs[:, 0:W], qr[:, 0:W], qr[:, 0:W]), drain=False)
    S.op(v, lambda E: E.tensor_reduce(rstd8[:, 0:nheads], sqs[:, 0:W].rearrange("p (h f) -> p h f", h=nheads),
                                      AX.X, ALU.add))
    S.op(v, lambda E: E.tensor_scalar_mul(rstd8[:, 0:nheads], rstd8[:, 0:nheads], 1.0 / DH), inc="v_qn")
    S.wait(s, "v_qn")
    c_eps, c_mh = S.consts[eps], S.consts[-0.5]
    S.op(s, lambda E, c=c_eps: E.activation(rstd8[:, 0:nheads], rstd8[:, 0:nheads], AF.Ln, bias=c))
    S.op(s, lambda E, c=c_mh: E.activation(rstd8[:, 0:nheads], rstd8[:, 0:nheads], AF.Exp, scale=c), inc="s_qn")
    S.wait(v, "s_qn")
    rb = rstd8[:, 0:nheads].unsqueeze(2).broadcast_to([128, nheads, DH])
    S.op(v, lambda E: E.tensor_tensor(out_bf.rearrange("p (h f) -> p h f", h=nheads),
                                      qr[:, 0:W].rearrange("p (h f) -> p h f", h=nheads), rb, ALU.mult),
         inc="v_qb")


def emit_l5(S, nc, es, io, local):
    nmask = 6 if local else 4
    xn_in = io['xn_in']; wq_in = io['wq_in']; cos_in = io['cos_in']; sin_in = io['sin_in']
    mask_in = io['mask_in']; id_in = io['id_in']; mg_out = io['mg_out']
    if not local:
        kvd_in = io['kvd_in']; ku_in = io['ku_in']; vu_in = io['vu_in']
    else:
        wk_in = io['wk_in']; wv_in = io['wv_in']
    xn_idx = io.get('xn_idx')
    sb = lambda n_, s_, d_=F32: es.enter_context(nc.sbuf_tensor(S.pfx + n_, s_, d_))
    ps = lambda n_, s_, d_=F32: es.enter_context(nc.psum_tensor(S.pfx + n_, s_, d_))
    xnb = sb("xnb", [128, 2, D], BF16)
    xnT_t = sb("xnT_t", [128, DC, 128], BF16)
    wq_s = sb("wq_s", [128, DC, 512], BF16)
    if not local:
        kvd_s = sb("kvd_s", [128, DC, 256], BF16)
        ku_s = sb("ku_s", [128, 2, 128], BF16)
        vu_s = sb("vu_s", [128, 2, 128], BF16)
    else:
        wk_s = sb("wk_s", [128, DC, 128], BF16)
        wv_s = sb("wv_s", [128, DC, 128], BF16)
    qT = sb("qT", [128, 4, 2 * TOK], BF16)
    kT = sb("kT", [128, 2 * TOK], BF16)
    vb = sb("vb", [128, NTF, 128], BF16)
    merged = sb("merged", [128, 4, 2 * TOK], BF16)
    mask_s = sb("mask_s", [128, nmask, 512], BF16)
    ident = sb("identb", [128, 128], BF16)
    onesrow = sb("onesrow", [1, 128])
    onesp = sb("onesp", [64, 128])
    onescol_b = sb("onescol_b", [128, 1], BF16)
    ctile = sb("ctile", [128, 2, 32]); stile = sb("stile", [128, 2, 32])
    qf = sb("qf", [128, 512]); qr = sb("qr", [128, 512]); sqs = sb("sqs", [128, 512])
    rstd8 = sb("rstd8", [128, 8])
    latb = sb("latb", [128, 256], BF16)
    latT_t = sb("latT_t", [128, 2, 128], BF16)
    kb = sb("kb", [128, 128], BF16)
    pAB = sb("pAB", [128, 3, 1024], BF16)
    rs_sb = sb("rs_sb", [128, 512])
    rs2 = sb("rs2", [128, 512])
    P1 = ps("P1", [128, 1024])
    P2 = ps("P2", [128, 1024])
    P3 = ps("P3", [128, 1024])
    es_proj = ExitStack()
    psp = lambda n_, s_, d_=F32: es_proj.enter_context(nc.psum_tensor(S.pfx + n_, s_, d_))
    ps_ta = psp("ps_ta", [128, 1024], BF16)
    ps_tb = psp("ps_tb", [128, 1024], BF16)

    S.setup_consts([1e-6, -0.5], sb)
    S.wait("scalar", "c_init")
    S.op("gpsimd", lambda E: E.dma_start(out=ident[:], in_=id_in), inc="ld_w", n=16, drain=False)
    S.op("gpsimd", lambda E: E.memset(onesrow[:], 1.0), inc="c_ones")
    S.op("gpsimd", lambda E: E.memset(onesp[:], 1.0), inc="c_ones")
    S.op("gpsimd", lambda E: E.memset(onescol_b[:], 1.0), inc="c_ones")
    for k in range(DC):
        S.dma("gpsimd", wq_s[:, k, :], wq_in[k * 128:(k + 1) * 128, :], "ld_w")
    if not local:
        for k in range(DC):
            S.dma("gpsimd", kvd_s[:, k, :], kvd_in[k * 128:(k + 1) * 128, :], "ld_w")
        for j in range(2):
            S.dma("gpsimd", ku_s[:, j, :], ku_in[j * 128:(j + 1) * 128, :], "ld_w")
            S.dma("gpsimd", vu_s[:, j, :], vu_in[j * 128:(j + 1) * 128, :], "ld_w")
        n_wdma = DC * 2 + 4
    else:
        for k in range(DC):
            S.dma("gpsimd", wk_s[:, k, :], wk_in[k * 128:(k + 1) * 128, :], "ld_w")
            S.dma("gpsimd", wv_s[:, k, :], wv_in[k * 128:(k + 1) * 128, :], "ld_w")
        n_wdma = DC * 3
    for j in range(nmask):
        S.dma("sync", mask_s[:, j, :], mask_in[j], "ld_m")

    if xn_idx is not None:
        idxb5 = sb("idxb5", [128, NTF], I32)
        S.dma("sync", idxb5[:], xn_idx, "ld_ix")
        S.wait("gpsimd", "ld_ix")
    for t in range(NTF):
        if t >= 2:
            S.wait("sync", "tp_t", S.val("tp_t") - 2)
            if xn_idx is not None:
                S.wait("gpsimd", "tp_t", S.val("tp_t") - 2)
        if t >= 2:
            S.wait("sync", "v_qb", 2 * (t - 1))
        if xn_idx is None:
            S.dma("sync", xnb[:, t % 2, :], xn_in[t * 128:(t + 1) * 128, :], "ld_xb")
        else:
            S.op("gpsimd", lambda E, t=t: E.indirect_dma_start(
                xnb[:, t % 2, :], None, xn_in, bass.IndirectOffsetOnAxis(idxb5[:, t:t + 1], 0)),
                inc="ld_xb", n=16, drain=False)
        S.dma("sync", ctile[:, t % 2, :], cos_in[t * 128:(t + 1) * 128, :], "ld_xn")
        S.dma("sync", stile[:, t % 2, :], sin_in[t * 128:(t + 1) * 128, :], "ld_xn")
        S.wait("tensor", "ld_xb", 16 * (t + 1))
        S.wait("tensor", "ld_xn", 32 * (t + 1))
        S.wait("vector", "ld_xn", 32 * (t + 1))
        S.wait("vector", "tp_t", S.val("tp_t"))
        S.wait("tensor", "ld_w", 16)
        cols = [(k * 128, xnT_t[:, k, :]) for k in range(DC)]
        S.wait("tensor", "t_proj", S.val("t_proj"))
        emit_transposes(S, xnb[:, t % 2, :], None, cols, ps_ta, ps_tb, ident, "tp")
        S.wait("tensor", "tp_v")
        if t == 0 and 'dbg_xnT' in io:
            S.wait("sync", "tp_v", S.val("tp_v"))
            S.dma("sync", io['dbg_xnT'], xnT_t[:].rearrange("p c f -> p (c f)"), "st_mg")
        S.wait("tensor", "ld_w", 16 * (1 + n_wdma))
        S.wait("tensor", "v_qev", S.val("v_qev"))
        S.wait("tensor", "v_lat", S.val("v_lat"))
        for k in range(DC):
            S.op("tensor", lambda E, k=k: E.matmul(P1[:, 0:512], xnT_t[:, k, :], wq_s[:, k, :],
                                                   start=(k == 0), stop=(k == DC - 1)))
        if not local:
            for k in range(DC):
                S.op("tensor", lambda E, k=k: E.matmul(P1[:, 512:768], xnT_t[:, k, :], kvd_s[:, k, :],
                                                       start=(k == 0), stop=(k == DC - 1)),
                     inc="t_proj" if k == DC - 1 else None)
        else:
            S.wait("tensor", "v_kev", S.val("v_kev"))
            S.wait("tensor", "v_vev", S.val("v_vev"))
            for k in range(DC):
                S.op("tensor", lambda E, k=k: E.matmul(P2[:, 0:128], xnT_t[:, k, :], wk_s[:, k, :],
                                                       start=(k == 0), stop=(k == DC - 1)))
            for k in range(DC):
                S.op("tensor", lambda E, k=k: E.matmul(P2[:, 128:256], xnT_t[:, k, :], wv_s[:, k, :],
                                                       start=(k == 0), stop=(k == DC - 1)),
                     inc="t_proj" if k == DC - 1 else None)
        S.wait("vector", "t_proj")
        emit_rope_qknorm(S, P1[:, 0:512], 8, ctile[:, t % 2, :], stile[:, t % 2, :],
                         qf, qr, sqs, rstd8, sqs[:, 0:512].bitcast(BF16)[:, 0:512])
        if t == 0 and 'dbg_qf' in io:
            S.wait("sync", "v_qev", S.val("v_qev"))
            S.dma("sync", io['dbg_qf'], qf[:, 0:512], "st_mg")
        qbv = sqs[:, 0:512].bitcast(BF16)[:, 0:512]
        S.wait("tensor", "v_qb")
        colsq = [(j * 128, qT[:, j, t * 128:(t + 1) * 128]) for j in range(4)]
        emit_transposes(S, qbv, None, colsq, ps_ta, ps_tb, ident, "tp")
        if t == 0 and 'dbg_qT0' in io:
            S.wait("sync", "tp_v", S.val("tp_v"))
            for j in range(4):
                S.dma("sync", io['dbg_qT0'][:, j * 128:(j + 1) * 128], qT[:, j, 0:128], "st_mg")
        if not local:
            S.op("vector", lambda E: E.tensor_copy(latb[:], P1[:, 512:768]), inc="v_lat")
            S.wait("tensor", "v_lat")
            colsl = [(j * 128, latT_t[:, j, :]) for j in range(2)]
            S.wait("tensor", "t_kv", S.val("t_kv"))
            emit_transposes(S, latb[:].bitcast(BF16) if False else latb[:], None, colsl, ps_ta, ps_tb, ident, "tp")
            S.wait("tensor", "tp_v")
            S.wait("tensor", "v_kev", S.val("v_kev"))
            S.wait("tensor", "v_vev", S.val("v_vev"))
            for j in range(2):
                S.op("tensor", lambda E, j=j: E.matmul(P2[:, 0:128], latT_t[:, j, :], ku_s[:, j, :],
                                                       start=(j == 0), stop=(j == 1)))
            for j in range(2):
                S.op("tensor", lambda E, j=j: E.matmul(P2[:, 128:256], latT_t[:, j, :], vu_s[:, j, :],
                                                       start=(j == 0), stop=(j == 1)), inc="t_kv" if j == 1 else None)
            S.wait("vector", "t_kv")
        S.wait("vector", "tp_t", S.val("tp_t"))
        emit_rope_qknorm(S, P2[:, 0:128], 2, ctile[:, t % 2, :], stile[:, t % 2, :],
                         qf, qr, sqs, rstd8, kb[:])
        S.op("vector", lambda E, t=t: E.tensor_copy(vb[:, t, :], P2[:, 128:256]), inc="v_vev")
        if t == 0 and 'dbg_qf' in io:
            S.wait("sync", "v_vev", S.val("v_vev"))
            S.dma("sync", io['dbg_v0'], vb[:, 0, :], "st_mg")
        S.op("vector", lambda E: E.engine_nop(), inc="v_kev", drain=False)
        S.wait("tensor", "v_qb")
        colsk = [(0, kT[:, t * 128:(t + 1) * 128])]
        emit_transposes(S, kb[:], None, colsk, ps_ta, ps_tb, ident, "tp")

    es_proj.close()
    P4 = ps("P4", [128, 1024])
    S.wait("tensor", "ld_m")
    proj_done_v = S.val("tp_v")
    S.wait("tensor", "tp_v", proj_done_v)
    S.wait("tensor", "v_qev", S.val("v_qev"))
    S.wait("tensor", "v_lat", S.val("v_lat"))
    S.wait("tensor", "v_kev", S.val("v_kev"))
    S.wait("tensor", "v_vev", S.val("v_vev"))
    Sbufs = [P1, P2, P4]
    gi = [0]

    def emit_S(kt, qg, hp):
        j = gi[0]
        buf = Sbufs[j % 3]
        ksl = slice(kt * 128, (kt + 1) * 128)
        qsl = slice(qg * 512, (qg + 1) * 512)
        diag_j = kt - 4 * qg + (2 if local else 0)
        masked = (0 <= diag_j < nmask)
        S.wait("tensor", "s_exp", max(0, S.val("s_exp") - 2))
        S.op("tensor", lambda E, buf=buf, ksl=ksl, qsl=qsl, hp=hp, masked=masked: E.matmul(
            buf[:, 0:512], kT[0:64, ksl], qT[0:64, hp, qsl],
            start=True, stop=not masked, tile_position=(0, 0)))
        S.op("tensor", lambda E, buf=buf, ksl=ksl, qsl=qsl, hp=hp, masked=masked: E.matmul(
            buf[:, 512:1024], kT[64:128, ksl], qT[64:128, hp, qsl],
            start=True, stop=not masked, tile_position=(64, 0)),
            inc=None if masked else "t_S")
        if masked:
            S.op("tensor", lambda E, buf=buf, j=diag_j: E.matmul(
                buf[:, 0:512], ident[:], mask_s[:, j, :], start=False, stop=True))
            S.op("tensor", lambda E, buf=buf, j=diag_j: E.matmul(
                buf[:, 512:1024], ident[:], mask_s[:, j, :], start=False, stop=True), inc="t_S")

    def emit_exp():
        j = gi[0]
        buf = Sbufs[j % 3]
        S.wait("scalar", "t_S", S.val("t_S"))
        S.wait("scalar", "t_av", max(0, S.val("t_av") - 1))
        S.op("scalar", lambda E, buf=buf, s=j % 3: E.activation(pAB[:, s, :], buf[:, 0:1024], AF.Exp, scale=0.125),
             inc="s_exp")

    def emit_av(kt, first, last, lag):
        s = gi[0] % 3
        S.wait("tensor", "s_exp", S.val("s_exp") - lag)
        if first:
            S.wait("tensor", "v_mg", S.val("v_mg"))
        S.op("tensor", lambda E, s=s, first=first, last=last: E.matmul(
            P3[0:1, 512:1024], onescol_b[:], pAB[:, s, 0:512], start=first, stop=last, tile_position=(0, 0)))
        S.op("tensor", lambda E, s=s, first=first, last=last: E.matmul(
            P3[32:33, 512:1024], onescol_b[:], pAB[:, s, 512:1024], start=first, stop=last, tile_position=(0, 32)))
        S.op("tensor", lambda E, s=s, kt=kt, first=first, last=last: E.matmul(
            P3[0:64, 0:512], vb[:, kt, 0:64], pAB[:, s, 0:512], start=first, stop=last, tile_position=(0, 0)))
        S.op("tensor", lambda E, s=s, kt=kt, first=first, last=last: E.matmul(
            P3[64:128, 0:512], vb[:, kt, 64:128], pAB[:, s, 512:1024], start=first, stop=last, tile_position=(0, 64)),
            inc="t_av")

    for hp in range(4):
        for qg in range(8):
            kt_lo = max(0, 4 * qg - 2) if local else 0
            kts = list(range(kt_lo, 4 * qg + 4))
            qsl = slice(qg * 512, (qg + 1) * 512)
            n = len(kts)
            for i, kt in enumerate(kts):
                emit_S(kt, qg, hp)
                emit_exp()
                if i >= 1:
                    gi[0] -= 1
                    emit_av(kts[i - 1], i - 1 == 0, False, 1)
                    gi[0] += 1
                gi[0] += 1
            gi[0] -= 1
            emit_av(kts[n - 1], n == 1, True, 0)
            gi[0] += 1
            S.wait("scalar", "t_av", S.val("t_av"))
            S.op("scalar", lambda E: E.activation(rs_sb[:], P3[:, 512:1024], AF.Ln))
            S.op("scalar", lambda E: E.activation(rs2[:], rs_sb[:], AF.Exp, scale=-1.0), inc="s_rs")
            S.wait("tensor", "s_rs", S.val("s_rs"))
            S.op("tensor", lambda E: E.matmul(P3[0:64, 512:1024], onesp[0:1, 0:64], rs2[0:1, :],
                                              start=True, stop=True, tile_position=(0, 0)))
            S.op("tensor", lambda E: E.matmul(P3[64:128, 512:1024], onesp[32:33, 0:64], rs2[32:33, :],
                                              start=True, stop=True, tile_position=(32, 64)), inc="t_bc")
            S.wait("vector", "t_av", S.val("t_av"))
            S.op("vector", lambda E, hp=hp, qsl=qsl: E.tensor_copy(merged[:, hp, qsl], P3[:, 0:512]), drain=False)
            S.wait("vector", "t_bc", S.val("t_bc"))
            S.op("vector", lambda E, hp=hp, qsl=qsl: E.tensor_tensor(
                merged[:, hp, qsl], merged[:, hp, qsl], P3[:, 512:1024], ALU.mult), inc="v_mg")
    S.wait("sync", "v_mg")
    for j in range(4):
        S.dma("sync", mg_out[j * 128:(j + 1) * 128, :], merged[:, j, :], "st_mg")
    if 'dbg_qT' in io:
        for j in range(4):
            S.dma("sync", io['dbg_qT'][j * 128:(j + 1) * 128, :], qT[:, j, :], "st_mg")
        S.dma("sync", io['dbg_kT'], kT[:], "st_mg")
        S.dma("sync", io['dbg_vb'], vb[:].rearrange("p t d -> p (t d)"), "st_mg")
        if xn_idx is not None:
            S.dma("sync", io['dbg_idx'], idxb5[:], "st_mg")
    S.wait("sync", "st_mg")



def build_launch5(local=False):
    nmask = 6 if local else 4
    nc = mkbass()
    xn_in = nc.dram_tensor("xn_full", [2 * TOK, D], BF16, kind="ExternalInput").ap()
    wq_in = nc.dram_tensor("wq_h", [D, 512], F32, kind="ExternalInput").ap()
    if not local:
        kvd_in = nc.dram_tensor("kvd", [D, 256], F32, kind="ExternalInput").ap()
        ku_in = nc.dram_tensor("ku_h", [256, 128], F32, kind="ExternalInput").ap()
        vu_in = nc.dram_tensor("vu_h", [256, 128], F32, kind="ExternalInput").ap()
    else:
        wk_in = nc.dram_tensor("wk_h", [D, 128], F32, kind="ExternalInput").ap()
        wv_in = nc.dram_tensor("wv_h", [D, 128], F32, kind="ExternalInput").ap()
    cos_in = nc.dram_tensor("cosr", [2 * TOK, 32], F32, kind="ExternalInput").ap()
    sin_in = nc.dram_tensor("sinr", [2 * TOK, 32], F32, kind="ExternalInput").ap()
    mask_in = nc.dram_tensor("maskadd", [nmask, 128, 512], BF16, kind="ExternalInput").ap()
    id_in = nc.dram_tensor("ident", [128, 128], F32, kind="ExternalInput").ap()
    mg_out = nc.dram_tensor("mergedT", [512, 2 * TOK], BF16, kind="ExternalOutput").ap()
    es = ExitStack()
    S = Sched(nc); S._es = es
    io = dict(xn_in=xn_in, wq_in=wq_in, cos_in=cos_in, sin_in=sin_in, mask_in=mask_in,
              id_in=id_in, mg_out=mg_out)
    if not local:
        io.update(kvd_in=kvd_in, ku_in=ku_in, vu_in=vu_in)
    else:
        io.update(wk_in=wk_in, wv_in=wv_in)
    emit_l5(S, nc, es, io, local)
    with es, nc.Block() as block:
        S.build(block, es)
    return nc


def emit_l6(S, nc, es, io, final):
    mg_in = io['mg_in']; wo_in = io['wo_in']
    if not final:
        xn1_in = io['xn1_in']; n2_in = io['n2_in']; id_in = io['id_in']
        xa_out = io['xa_out']; xn2_out = io['xn2_out']
    else:
        xa_in = io['xa_in']; x1_in = io['x1_in']; g1_in = io['g1_in']; out_out = io['out_out']
    mg_idx = io.get('mg_idx')
    sb = lambda n_, s_, d_=F32: es.enter_context(nc.sbuf_tensor(S.pfx + n_, s_, d_))
    ps = lambda n_, s_, d_=F32: es.enter_context(nc.psum_tensor(S.pfx + n_, s_, d_))
    mg_s = sb("mg_s", [128, DC, TOK], BF16)
    wo_s = sb("wo_s", [128, DC, D], BF16)
    aux = sb("aux", [128, 2, D])
    auxb = sb("auxb", [128, 2, D], BF16)
    xab = sb("xab", [128, D])
    outb = sb("outb", [128, D])
    rstd = sb("rstd", [128, 1]); sq = sb("sq", [128, D])
    xnb2 = sb("xnb2", [128, D], BF16)
    n2b = sb("n2b", [128, D]); rowscr = sb("rowscr", [1, D])
    onesrow = sb("onesrow", [1, 128])
    P1 = ps("P1", [128, 1024])
    P2 = ps("P2", [128, 1024])

    S.setup_consts([1e-6, -0.5], sb)
    S.wait("scalar", "c_init")
    if mg_idx is not None:
        idxb6 = sb("idxb6", [128, DC], I32)
        S.dma("sync", idxb6[:], mg_idx, "ld_ix")
        S.wait("gpsimd", "ld_ix")
    for k in range(DC):
        S.dma("gpsimd", wo_s[:, k, :], wo_in[k * 128:(k + 1) * 128, :], "ld_w")
        if mg_idx is None:
            S.dma("sync", mg_s[:, k, :], mg_in[k * 128:(k + 1) * 128, :], "ld_m")
        else:
            S.op("gpsimd", lambda E, k=k: E.indirect_dma_start(
                mg_s[:, k, :], None, mg_in, bass.IndirectOffsetOnAxis(idxb6[:, k:k + 1], 0)),
                inc="ld_m", n=16, drain=False)
    if not final:
        S.op("gpsimd", lambda E: E.memset(onesrow[:], 1.0), inc="c_ones")
        S.dma("sync", rowscr[0:1, :], n2_in, "ld_misc")
        S.wait("tensor", "ld_misc")
        S.wait("tensor", "c_ones")
        emit_bcast_row(S, rowscr[0:1, :], onesrow, P2, n2b[:], "bc1")

    for t in range(NT):
        if not final:
            S.wait("sync", "v_xa", S.val("v_xa"))
            S.dma("sync", auxb[:, 0, :], xn1_in[t * 128:(t + 1) * 128, :], "ld_a")
        else:
            S.wait("sync", "v_out", S.val("v_out"))
            S.wait("sync", "st_out", S.val("st_out"))
            S.dma("sync", aux[:, 0, :], xa_in[t * 128:(t + 1) * 128, :], "ld_a")
            S.dma("sync", aux[:, 1, :], g1_in[t * 128:(t + 1) * 128, :], "ld_a")
            S.dma("sync", outb[:], x1_in[t * 128:(t + 1) * 128, :], "ld_a")
        S.wait("tensor", "ld_w")
        S.wait("tensor", "ld_m")
        S.wait("tensor", "v_wo", S.val("v_wo"))
        for k in range(DC):
            for n0 in (0, 512):
                S.op("tensor", lambda E, k=k, t=t, n0=n0: E.matmul(
                    P1[:, n0:n0 + 512], mg_s[:, k, t * 128:(t + 1) * 128], wo_s[:, k, n0:n0 + 512],
                    start=(k == 0), stop=(k == DC - 1)),
                    inc="t_wo" if (k == DC - 1 and n0 == 512) else None)
        na = 16 * (t + 1) * (1 if not final else 3)
        S.wait("vector", "ld_a", na)
        S.wait("vector", "t_wo")
        if not final:
            S.wait("vector", "st_xa", S.val("st_xa"))
            S.op("vector", lambda E: E.tensor_tensor(xab[:], P1[:], auxb[:, 0, :], ALU.add), inc="v_wo")
            S.wait("sync", "v_wo")
            S.dma("sync", xa_out[t * 128:(t + 1) * 128, :], xab[:], "st_xa")
            S.wait("vector", "bc1_v")
            emit_rms(S, xab[:], rstd, sq, n2b[:], xnb2[:])
            S.wait("sync", "v_xn")
            S.dma("sync", xn2_out[t * 128:(t + 1) * 128, :], xnb2[:], "st_xa")
            S.op("vector", lambda E: E.engine_nop(), inc="v_xa", drain=False)
        else:
            S.wait("vector", "st_out", S.val("st_out"))
            S.op("vector", lambda E: E.tensor_tensor(xab[:], P1[:], aux[:, 0, :], ALU.add), inc="v_wo")
            S.op("vector", lambda E: E.tensor_mul(xab[:], xab[:], aux[:, 1, :]), drain=False)
            S.op("vector", lambda E: E.tensor_add(outb[:], outb[:], xab[:]), inc="v_out")
            S.wait("sync", "v_out")
            S.dma("sync", out_out[t * 128:(t + 1) * 128, :], outb[:], "st_out")
    S.wait("sync", "st_xa" if not final else "st_out")



def build_launch6(final=False):
    nc = mkbass()
    mg_in = nc.dram_tensor("mergedT_full", [D, TOK], BF16, kind="ExternalInput").ap()
    wo_in = nc.dram_tensor("wo", [D, D], F32, kind="ExternalInput").ap()
    if not final:
        xn1_in = nc.dram_tensor("xn1", [TOK, D], BF16, kind="ExternalInput").ap()
        n2_in = nc.dram_tensor("norm2_w", [1, D], F32, kind="ExternalInput").ap()
        id_in = nc.dram_tensor("ident", [128, 128], F32, kind="ExternalInput").ap()
        xa_out = nc.dram_tensor("x_a", [TOK, D], F32, kind="ExternalOutput").ap()
        xn2_out = nc.dram_tensor("xn2", [TOK, D], BF16, kind="ExternalOutput").ap()
    else:
        xa_in = nc.dram_tensor("x_a", [TOK, D], F32, kind="ExternalInput").ap()
        x1_in = nc.dram_tensor("x1", [TOK, D], F32, kind="ExternalInput").ap()
        g1_in = nc.dram_tensor("g1", [TOK, D], F32, kind="ExternalInput").ap()
        out_out = nc.dram_tensor("out", [TOK, D], F32, kind="ExternalOutput").ap()
    es = ExitStack()
    S = Sched(nc); S._es = es
    io = dict(mg_in=mg_in, wo_in=wo_in)
    if not final:
        io.update(xn1_in=xn1_in, n2_in=n2_in, id_in=id_in, xa_out=xa_out, xn2_out=xn2_out)
    else:
        io.update(xa_in=xa_in, x1_in=x1_in, g1_in=g1_in, out_out=out_out)
    emit_l6(S, nc, es, io, final)
    with es, nc.Block() as block:
        S.build(block, es)
    return nc


def build_fused(upto=99):
    nc = mkbass()
    EI, EO = "ExternalInput", "ExternalOutput"
    dt_ = lambda n, s, d=F32, k="Internal": nc.dram_tensor(n, s, d, kind=k).ap()
    x_in = dt_("x", [TOK, D], F32, EI)
    wg_sl = dt_("wg_sl", [352, D], F32, EI)
    wu_sl = dt_("wu_sl", [352, D], F32, EI)
    wd_sl = dt_("wd_sl", [128, I_FFN], F32, EI)
    WvT = dt_("WvT", [D, D], F32, EI)
    WqT = dt_("WqT", [D, D], F32, EI)
    WgT = dt_("WgT", [D, D], F32, EI)
    WoT = dt_("WoT", [D, D], F32, EI)
    beta_in = dt_("beta", [1, H], F32, EI)
    gam_in = dt_("ln_gamma", [1, D], F32, EI)
    bet_in = dt_("ln_beta", [1, D], F32, EI)
    id_in = dt_("ident", [128, 128], F32, EI)
    cidx_in = dt_("carry_idx", [NT, 1], I32, EI)
    n1_in = dt_("norm1_w", [1, D], F32, EI)
    n2_in = dt_("norm2_w", [1, D], F32, EI)
    ws_in = dt_("WsT", [D, D], F32, EI)
    we_in = dt_("WeT", [D, D], F32, EI)
    bias_in = dt_("bias", [1, D], F32, EI)
    gwq_in = dt_("g_wq_h", [D, 512], F32, EI)
    kvd_in = dt_("g_kvd", [D, 256], F32, EI)
    ku_in = dt_("g_ku_h", [256, 128], F32, EI)
    vu_in = dt_("g_vu_h", [256, 128], F32, EI)
    gwo_in = dt_("g_wo", [D, D], F32, EI)
    lwq_in = dt_("l_wq_h", [D, 512], F32, EI)
    lwk_in = dt_("l_wk_h", [D, 128], F32, EI)
    lwv_in = dt_("l_wv_h", [D, 128], F32, EI)
    lwo_in = dt_("l_wo", [D, D], F32, EI)
    cosg_in = dt_("cosg", [2 * TOK, 32], F32, EI)
    sing_in = dt_("sing", [2 * TOK, 32], F32, EI)
    cosl_in = dt_("cosl", [2 * TOK, 32], F32, EI)
    sinl_in = dt_("sinl", [2 * TOK, 32], F32, EI)
    maskG_in = dt_("maskG", [4, 128, 512], BF16, EI)
    maskL_in = dt_("maskL", [6, 128, 512], BF16, EI)
    xnidx_g = dt_("xnidx_g", [128, NTF], I32, EI)
    xnidx_l = dt_("xnidx_l", [128, NTF], I32, EI)
    mgidx_g = dt_("mgidx_g", [128, DC], I32, EI)
    mgidx_l = dt_("mgidx_l", [128, DC], I32, EI)
    sums_idx = dt_("sums_idx", [2 * NT, 1], I32, EI)
    out_out = dt_("out", [TOK, D], F32, EO)
    wgq_sl = dt_("wgq_sl_i", [352, D], BF16)
    wuq_sl = dt_("wuq_sl_i", [352, D], BF16)
    wdq_sl = dt_("wdq_sl_i", [128, I_FFN], BF16)
    wgq_all = dt_("wgq_all", [I_FFN, D], BF16)
    wuq_all = dt_("wuq_all", [I_FFN, D], BF16)
    wdq_all = dt_("wdq_all", [D, I_FFN], BF16)
    xn0_i = dt_("xn0_i", [TOK, D], BF16)
    gacc_i = dt_("gacc_i", [TOK, D], F32)
    stl_i = dt_("stl_i", [TOK, D], BF16)
    sums_i = dt_("sums_i", [NT, D], F32)
    sums_ag = dt_("sums_ag", [8 * NT, D], F32)
    sums_st = dt_("sums_st", [2 * NT, D], F32)
    carr_dram = dt_("carr_dram", [2 * NT, D], F32)
    carrmy_dram = dt_("carrmy_dram", [NT, D], F32)
    x1_i = dt_("x1_i", [TOK, D], F32)
    xn1_i = dt_("xn1_i", [TOK, D], BF16)
    xnr1_i = dt_("xnr1_i", [TOK, D], BF16)
    g1_i = dt_("g1_i", [TOK, D], F32)
    xnr1_ag = dt_("xnr1_ag", [8 * TOK, D], BF16)
    mgG_i = dt_("mgG_i", [512, 2 * TOK], BF16)
    mgG_ag = dt_("mgG_ag", [8 * 512, 2 * TOK], BF16)
    xa_i = dt_("xa_i", [TOK, D], F32)
    xn2_i = dt_("xn2_i", [TOK, D], BF16)
    xn2_ag = dt_("xn2_ag", [8 * TOK, D], BF16)
    mgL_i = dt_("mgL_i", [512, 2 * TOK], BF16)
    mgL_ag = dt_("mgL_ag", [8 * 512, 2 * TOK], BF16)

    es_g = ExitStack()
    S = Sched(nc); S._es = es_g
    sbg = lambda n_, s_, d_=F32: es_g.enter_context(nc.sbuf_tensor(n_, s_, d_))
    sums_gsb = sbg("sums_gsb", [2 * NT, D])
    sidxb = sbg("sidxb", [2 * NT, 1], I32)

    RG = [[0, 1, 2, 3, 4, 5, 6, 7]]

    def ag(cc_in, cc_out):
        S.op("gpsimd", lambda E, i=cc_in, o=cc_out: E.collective_compute(
            "AllGather", ALU.bypass, replica_groups=RG, ins=[i.opt()], outs=[o.opt()]),
            inc="!cc", n=1, drain=False)

    PH = [0]
    def phase(pfx, fn):
        if PH[0] > upto:
            return
        PH[0] += 1
        S.pfx = pfx
        S.rebase()
        es_p = ExitStack()
        fn(es_p)
        S.pfx = ""
        S.barrier()
        es_p.close()

    def glue(fn):
        if PH[0] > upto:
            return
        fn()

    phase("p0_", lambda es: emit_l0(S, nc, es, dict(
        wg_in=wg_sl, wu_in=wu_sl, wd_in=wd_sl,
        wg_out=wgq_sl.tensor.ap() if False else wgq_sl, wu_out=wuq_sl, wd_out=wdq_sl)))
    def g0():
        ag(wgq_sl, wgq_all); ag(wuq_sl, wuq_all); ag(wdq_sl, wdq_all)
        S.wait("sync", "!cc", 3); S.wait("gpsimd", "!cc", 3)
    glue(g0)
    phase("p1_", lambda es: emit_l1(S, nc, es, dict(
        x_in=x_in, wgq=wgq_all, wuq=wuq_all, wdq=wdq_all, id_in=id_in,
        xn0_out=xn0_i, gacc_out=gacc_i)))
    phase("p2_", lambda es: emit_l2(S, nc, es, dict(
        xn_in=xn0_i, wv_in=WvT, beta_in=beta_in, id_in=id_in,
        st_out=stl_i, sums_out=sums_i)))
    def g2():
        ag(sums_i, sums_ag)
        S.pfx = "g2_"
        S.rebase()
        S.dma("sync", sidxb[:], sums_idx, "ld_carr")
        S.wait("gpsimd", "!cc", 4)
        S.wait("gpsimd", "ld_carr")
        S.op("gpsimd", lambda E: E.indirect_dma_start(
            sums_gsb[:], None, sums_ag, bass.IndirectOffsetOnAxis(sidxb[:], 0)), inc="g_carr", n=16, drain=False)
        S.wait("gpsimd", "g_carr")
        S.op("gpsimd", lambda E: E.dma_start(out=sums_st, in_=sums_gsb[:]), inc="g_carr", n=16, drain=False)
        S.wait("sync", "g_carr", 32)
        S.pfx = ""
    glue(g2)
    phase("p3_", lambda es: emit_l3(S, nc, es, dict(
        xn_in=xn0_i, stl_in=stl_i, sums_in=sums_st, x_in=x_in, gacc_in=gacc_i,
        wq_in=WqT, wg_in=WgT, wo_in=WoT, beta_in=beta_in, gam_in=gam_in, bet_in=bet_in,
        id_in=id_in, cidx_in=cidx_in, x1_out=x1_i, carr_dram=carr_dram, carrmy_dram=carrmy_dram)))
    phase("p4_", lambda es: emit_l4(S, nc, es, dict(
        x1_in=x1_i, n1_in=n1_in, ws_in=ws_in, we_in=we_in, bias_in=bias_in, id_in=id_in,
        xn1_out=xn1_i, xnr1_out=xnr1_i, g1_out=g1_i)))
    def g4():
        ag(xnr1_i, xnr1_ag)
        S.wait("gpsimd", "!cc", 5)
    glue(g4)
    p5io = dict(
        xn_in=xnr1_ag, xn_idx=xnidx_g, wq_in=gwq_in, kvd_in=kvd_in, ku_in=ku_in, vu_in=vu_in,
        cos_in=cosg_in, sin_in=sing_in, mask_in=maskG_in, id_in=id_in, mg_out=mgG_i)
    if 90 <= upto < 99:
        p5io['dbg_qT'] = nc.dram_tensor("d_qT", [512, 4096], BF16, kind="ExternalOutput").ap()
        p5io['dbg_kT'] = nc.dram_tensor("d_kT", [128, 4096], BF16, kind="ExternalOutput").ap()
        p5io['dbg_vb'] = nc.dram_tensor("d_vb", [128, 4096], BF16, kind="ExternalOutput").ap()
        p5io['dbg_idx'] = nc.dram_tensor("d_idx", [128, NTF], I32, kind="ExternalOutput").ap()
        p5io['dbg_xnb'] = nc.dram_tensor("d_xnb", [128, D], BF16, kind="ExternalOutput").ap()
        p5io['dbg_xnT'] = nc.dram_tensor("d_xnT", [128, D], BF16, kind="ExternalOutput").ap()
        p5io['dbg_qf'] = nc.dram_tensor("d_qf", [128, 512], F32, kind="ExternalOutput").ap()
        p5io['dbg_v0'] = nc.dram_tensor("d_v0", [128, 128], BF16, kind="ExternalOutput").ap()
        p5io['dbg_qT0'] = nc.dram_tensor("d_qT0", [128, 512], BF16, kind="ExternalOutput").ap()
    phase("p5_", lambda es: emit_l5(S, nc, es, p5io, False))
    def g5():
        ag(mgG_i, mgG_ag)
        S.wait("gpsimd", "!cc", 6)
    glue(g5)
    phase("p6_", lambda es: emit_l6(S, nc, es, dict(
        mg_in=mgG_ag.rearrange("(r hh) t -> (r hh) t") if False else mgG_ag.rearrange("r (hh t) -> (r hh) t", hh=2),
        mg_idx=mgidx_g, wo_in=gwo_in, xn1_in=xn1_i, n2_in=n2_in, id_in=id_in,
        xa_out=xa_i, xn2_out=xn2_i), False))
    def g6():
        ag(xn2_i, xn2_ag)
        S.wait("gpsimd", "!cc", 7)
    glue(g6)
    p7io = dict(
        xn_in=xn2_ag, xn_idx=xnidx_l, wq_in=lwq_in, wk_in=lwk_in, wv_in=lwv_in,
        cos_in=cosl_in, sin_in=sinl_in, mask_in=maskL_in, id_in=id_in, mg_out=mgL_i)
    phase("p7_", lambda es: emit_l5(S, nc, es, p7io, True))
    def g7():
        ag(mgL_i, mgL_ag)
        S.wait("gpsimd", "!cc", 8)
    glue(g7)
    phase("p8_", lambda es: emit_l6(S, nc, es, dict(
        mg_in=mgL_ag.rearrange("r (hh t) -> (r hh) t", hh=2),
        mg_idx=mgidx_l, wo_in=lwo_in, xa_in=xa_i, x1_in=x1_i, g1_in=g1_i, out_out=out_out), True))

    if upto < 99:
        S.pfx = "gd_"
        S.rebase()
        srcs = {0: None, 1: gacc_i, 2: gacc_i, 3: x1_i, 4: g1_i, 5: g1_i, 6: xa_i, 7: xa_i}
        sd = srcs.get(min(upto, 7), x1_i)
        if upto >= 90:
            for nm, t_, d_ in [("d_xnrag", xnr1_ag, BF16), ("d_sumsag", sums_ag, F32),
                               ("d_xn2ag", xn2_ag, BF16), ("d_mgL", mgL_i, BF16), ("d_mgLag", mgL_ag, BF16),
                               ("d_xn0", xn0_i, BF16), ("d_gacc", gacc_i, F32), ("d_x1", x1_i, F32),
                               ("d_g1", g1_i, F32), ("d_xa", xa_i, F32), ("d_xnr1", xnr1_i, BF16),
                               ("d_mgG", mgG_i, BF16), ("d_xn2", xn2_i, BF16), ("d_stl", stl_i, BF16)]:
                shp = list(t_.shape)
                dbg = nc.dram_tensor(nm, shp, d_, kind="ExternalOutput").ap()
                S.dma("sync", dbg, t_, "st_dbg")
            S.wait("sync", "st_dbg")
        elif sd is not None:
            S.dma("sync", out_out, sd, "st_dbg")
            S.wait("sync", "st_dbg")

    with es_g, nc.Block() as block:
        S.build(block, es_g)
    return nc



HEAD_ORDER = [0, 4, 1, 5, 2, 6, 3, 7]

def _rope_tables(base, S_=4096, hd=64):
    inv = (1.0 / base ** (np.arange(0, hd, 2, dtype=np.float32) / np.float32(hd))).astype(np.float32)
    fr = np.arange(S_, dtype=np.float32)[:, None] * inv[None, :]
    return np.cos(fr).astype(np.float32), np.sin(fr).astype(np.float32)

def _build_masks(local):
    n = 6 if local else 4
    out = np.zeros((n, 128, 512), np.float32)
    k = np.arange(128)[:, None]
    qf = np.arange(512)[None, :]
    for j in range(n):
        rel = (j - 2) if local else j
        d = qf - (rel * 128 + k)
        vis = (d >= 0) & (d < 256) if local else (d >= 0)
        out[j] = np.where(vis, 0.0, -30000.0)
    return out.astype(BF)

def _head_cols(half):
    return np.concatenate([np.arange((half * 8 + h) * 64, (half * 8 + h) * 64 + 64) for h in HEAD_ORDER])

WO_ROWS = np.concatenate([np.arange(h * 64, h * 64 + 64) for h in
                          [0, 4, 1, 5, 2, 6, 3, 7, 8, 12, 9, 13, 10, 14, 11, 15]])

_BUILDERS = {}

def _get(name, fn):
    if name not in _BUILDERS:
        _BUILDERS[name] = fn()
    return _BUILDERS[name]


def kernel(**inputs):
    g = {k: np.asarray(v, np.float32) for k, v in inputs.items()}
    C = np.ascontiguousarray
    ident = np.eye(128, dtype=np.float32)
    cosg, sing = _rope_tables(1e6)
    cosl, sinl = _rope_tables(1e4)
    maskG = _build_masks(False)
    maskL = _build_masks(True)
    wg, wu, wd = g['L0_ffn_w_gate'], g['L0_ffn_w_up'], g['L0_ffn_w_down']
    p = np.arange(128, dtype=np.int32)[:, None]
    t = np.arange(NTF, dtype=np.int32)[None, :]
    k = np.arange(DC, dtype=np.int32)[None, :]
    maps = []
    for c in range(8):
        b, half = c // 2, c % 2
        xnidx = (b * 4096 + t * 128 + p).astype(np.int32)
        mgidx = (2 * ((2 * b + k // 4) * 512 + (k % 4) * 128 + p) + half).astype(np.int32)
        maps.append(dict(
            x=C(g['L0_attn_x'][b, half*2048:(half+1)*2048]),
            wg_sl=C(wg[c*352:(c+1)*352]), wu_sl=C(wu[c*352:(c+1)*352]), wd_sl=C(wd[c*128:(c+1)*128]),
            WvT=C(g['L0_attn_Wv'].T), WqT=C(g['L0_attn_Wq'].T), WgT=C(g['L0_attn_Wg'].T),
            WoT=C(g['L0_attn_Wo'].T),
            beta=g['L0_attn_beta'].reshape(1, 16),
            ln_gamma=g['L0_attn_ln_gamma'].reshape(1, 1024), ln_beta=g['L0_attn_ln_beta'].reshape(1, 1024),
            ident=ident,
            carry_idx=(np.arange(16, dtype=np.int32) + 16 * half).reshape(16, 1),
            norm1_w=g['L1_attn_norm1_w'].reshape(1, 1024), norm2_w=g['L1_attn_norm2_w'].reshape(1, 1024),
            WsT=C(g['L1_ffn_W_shared'].T), WeT=C(g['L1_ffn_W_expert'].T),
            bias=g['L1_ffn_bias'].reshape(1, 1024),
            g_wq_h=C(g['L1_attn_g_wq'][:, _head_cols(half)]),
            g_kvd=C(g['L1_attn_g_kv_down']),
            g_ku_h=C(g['L1_attn_g_k_up'][:, half*128:(half+1)*128]),
            g_vu_h=C(g['L1_attn_g_v_up'][:, half*128:(half+1)*128]),
            g_wo=C(g['L1_attn_g_wo'][WO_ROWS, :]),
            l_wq_h=C(g['L1_attn_l_wq'][:, _head_cols(half)]),
            l_wk_h=C(g['L1_attn_l_wk'][:, half*128:(half+1)*128]),
            l_wv_h=C(g['L1_attn_l_wv'][:, half*128:(half+1)*128]),
            l_wo=C(g['L1_attn_l_wo'][WO_ROWS, :]),
            cosg=cosg, sing=sing, cosl=cosl, sinl=sinl,
            maskG=maskG, maskL=maskL,
            xnidx_g=xnidx, xnidx_l=xnidx, mgidx_g=mgidx, mgidx_l=mgidx,
            sums_idx=(np.arange(32, dtype=np.int32) + 32 * b).reshape(32, 1),
        ))
    res = run_bass_kernel_spmd(_get('fused', build_fused), maps, core_ids=list(range(8)))
    out = np.zeros((4, 4096, 1024), np.float32)
    for c in range(8):
        out[c//2, (c%2)*2048:(c%2+1)*2048] = res.results[c]['out']
    return out


# version 10 (final)
# speedup vs baseline: 1.0098x; 1.0020x over previous
import numpy as np
import ml_dtypes
import concourse.bass as bass
import concourse.mybir as mybir
from concourse.bass_utils import run_bass_kernel_spmd
from contextlib import ExitStack

F32 = mybir.dt.float32
BF16 = mybir.dt.bfloat16
I32 = mybir.dt.int32
I16 = mybir.dt.int16
U8 = mybir.dt.uint8
AF = mybir.ActivationFunctionType
ALU = mybir.AluOpType
AX = mybir.AxisListType
BF = ml_dtypes.bfloat16

NCORES = 8
TOK = 2048
NT = TOK // 128
D = 1024
DC = D // 128
I_FFN = 2816
I_PHASES = [(0, 768), (768, 768), (1536, 768), (2304, 512)]
H = 16
DH = 64
LN2 = float(np.log(2.0))
NUDGE6 = float(np.float32(6.0 * (1 - 2**-20)))

ENGS = ("sync", "gpsimd", "vector", "scalar", "tensor")


class Sched:

    def __init__(self, nc):
        self.nc = nc
        self.q = {e: [] for e in ENGS}
        self.cnt = {}
        self.sems = {}
        self._es = None
        self.pfx = ""
        self.base = {}
        self.name_map = {}
        self.trace = {e: [] for e in ENGS}

    def _n(self, name):
        if name.startswith("!"):
            return name
        return self.name_map.setdefault(name, "s%d" % len(self.name_map))

    def _sem_raw(self, name):
        if name not in self.sems:
            self.sems[name] = self._es.enter_context(self.nc.semaphore(name))
            self.cnt[name] = 0
        return self.sems[name]

    def sem(self, name):
        return self._sem_raw(self._n(name))

    def op(self, eng, fn, inc=None, n=1, drain=None):
        if drain is None:
            drain = eng in ("vector", "gpsimd", "scalar")
        if inc is not None:
            inc = self._n(inc)
            self._sem_raw(inc)
            self.cnt[inc] += n
        self.trace[eng].append(("inc", inc, n) if inc else ("op", None, 0))
        def run(E, fn=fn, inc=inc, n=n, drain=drain):
            ins = fn(E)
            if drain:
                d = E.drain()
                if inc is not None:
                    d.then_inc(self.sems[inc], n)
            elif inc is not None:
                ins.then_inc(self.sems[inc], n)
        self.q[eng].append(run)

    def dma(self, eng, out, in_, inc, **kw):
        self.op(eng, lambda E, o=out, i=in_: E.dma_start(out=o, in_=i, **kw), inc=inc, n=16, drain=False)

    def rebase(self):
        self.base = dict(self.cnt)
        self.name_map = {}

    def _b(self, name):
        return 0 if name.startswith("!") else self.base.get(name, 0)

    def wait(self, eng, semname, val=None):
        semname = self._n(semname)
        self._sem_raw(semname)
        v = self.cnt[semname] if val is None else self._b(semname) + val
        if v == 0:
            return
        self.trace[eng].append(("wait", semname, v))
        self.q[eng].append(lambda E, s=semname, v=v: E.wait_ge(self.sems[s], v))

    def val(self, semname):
        n = self._n(semname)
        return self.cnt.get(n, 0) - self._b(n)

    def barrier(self):
        for e in ENGS:
            def mk(e=e):
                def run(E):
                    d = E.drain()
                    d.then_inc(self.sems["!bar"], 1)
                return run
            self._sem_raw("!bar")
            self.cnt["!bar"] += 1
            self.trace[e].append(("inc", "!bar", 1))
            self.q[e].append(mk())
        tot = self.cnt["!bar"]
        for e in ENGS:
            self.trace[e].append(("wait", "!bar", tot))
            self.q[e].append(lambda E, v=tot: E.wait_ge(self.sems["!bar"], v))

    def setup_consts(self, values, sb):
        cst = sb("cst", [128, len(values)])
        self.consts = {}
        self.consts = {}
        for i, v in enumerate(values):
            ap = cst[:, i:i + 1]
            self.consts[v] = ap
            self.op("gpsimd", lambda E, ap=ap, v=v: E.memset(ap, float(v)),
                    inc="c_init" if i == len(values) - 1 else None)
        return self.consts

    def check(self):
        pos = {e: 0 for e in ENGS}
        cnt = {}
        progress = True
        while progress:
            progress = False
            for e in ENGS:
                tr = self.trace[e]
                while pos[e] < len(tr):
                    kind, sem, v = tr[pos[e]]
                    if kind == "wait" and cnt.get(sem, 0) < v:
                        break
                    if kind == "inc":
                        cnt[sem] = cnt.get(sem, 0) + v
                    pos[e] += 1
                    progress = True
        stuck = {e: (pos[e], len(self.trace[e]), self.trace[e][pos[e]] if pos[e] < len(self.trace[e]) else None)
                 for e in ENGS if pos[e] < len(self.trace[e])}
        if stuck:
            for e, (p, n, item) in stuck.items():
                print(f"DEADLOCK {e}: op {p}/{n} waiting {item}; sem at {cnt.get(item[1], 0)}")
            raise RuntimeError(f"schedule deadlock: {stuck}")

    def build(self, block, es):
        self.check()
        for e in ENGS:
            ops = self.q[e]
            if not ops:
                continue
            def emit(E, ops=ops):
                for f in ops:
                    f(E)
            getattr(block, e)(emit)


def mkbass():
    return bass.Bass("TRN2", target_bir_lowering=False)



def emit_ln(S, x_tile, mu, rstd, sq, xnf_tile, eps=1e-5):
    v, s = "vector", "scalar"
    S.op(v, lambda E: E.reduce_sum(mu[:], x_tile, AX.X))
    S.op(v, lambda E: E.tensor_mul(sq[:], x_tile, x_tile), drain=False)
    S.op(v, lambda E: E.reduce_sum(rstd[:], sq[:], AX.X))
    S.op(v, lambda E: E.tensor_scalar_mul(mu[:], mu[:], 1.0 / 1024))
    S.op(v, lambda E: E.tensor_scalar_mul(rstd[:], rstd[:], 1.0 / 1024))
    S.op(v, lambda E: E.tensor_mul(sq[:, 0:1], mu[:], mu[:]))
    S.op(v, lambda E: E.tensor_sub(rstd[:], rstd[:], sq[:, 0:1]), inc="v_ln")
    S.wait(s, "v_ln")
    c_eps, c_mh = S.consts[eps], S.consts[-0.5]
    S.op(s, lambda E, c=c_eps: E.activation(rstd[:], rstd[:], AF.Ln, bias=c))
    S.op(s, lambda E, c=c_mh: E.activation(rstd[:], rstd[:], AF.Exp, scale=c), inc="s_ln")
    S.wait(v, "s_ln")
    S.op(v, lambda E: E.tensor_scalar(xnf_tile, x_tile, mu[:], rstd[:], ALU.subtract, ALU.mult), inc="v_xn")


def emit_quant(S, xnf, F, amax, recip, hi_scr, r_bf, m_f, xq_out, sgn_scr):
    v = "vector"
    nb = F // 16
    P = xnf.shape[0]
    xnf3 = xnf.rearrange("p (a b) -> p a b", b=16)
    mf = m_f[:, 0:F]
    mf3 = mf.rearrange("p (a b) -> p a b", b=16)
    S.op(v, lambda E: E.reduce_max(amax[:, 0:nb], xnf3, AX.X, apply_absolute_value=True))
    S.op(v, lambda E: E.reciprocal(recip[:, 0:nb], amax[:, 0:nb]))
    S.op(v, lambda E: E.tensor_scalar(recip[:, 0:nb], recip[:, 0:nb], 1e12, NUDGE6, ALU.min, ALU.mult))
    rb = recip[:, 0:nb].unsqueeze(2).broadcast_to([P, nb, 16])
    S.op(v, lambda E: E.tensor_tensor(mf3, xnf3, rb, ALU.mult), drain=False)
    xn_hi = mf.bitcast(I16).rearrange("p (a b) -> p a b", b=2)[:, :, 1:2].squeeze(2)
    hi_i = hi_scr.bitcast(I16)[:, 0:F]
    sg_i = sgn_scr.bitcast(I16)[:, 0:F]
    r_i = r_bf.bitcast(I16)[:, 0:F]
    S.op(v, lambda E: E.tensor_scalar(hi_i, xn_hi, 0x7FFF, None, ALU.bitwise_and), drain=False)
    S.op(v, lambda E: E.tensor_scalar(sg_i, xn_hi, -32768, 0x3F80, ALU.bitwise_and, ALU.bitwise_or), drain=False)
    S.op(v, lambda E: E.tensor_scalar(r_i, hi_i, 0x3F80, 0x0020, ALU.max, ALU.add), drain=False)
    S.op(v, lambda E: E.tensor_scalar(r_i, r_i, -64, None, ALU.bitwise_and), drain=False)
    S.op(v, lambda E: E.tensor_scalar(mf, hi_i, 0x3E80, None, ALU.is_ge), drain=False)
    S.op(v, lambda E: E.tensor_scalar(xq_out, hi_i, 0x3F40, None, ALU.is_ge), drain=False)
    S.op(v, lambda E: E.tensor_tensor(mf, mf, xq_out, ALU.add), drain=False)
    S.op(v, lambda E: E.scalar_tensor_tensor(mf, mf, 0.5, r_bf[:, 0:F], ALU.mult, ALU.mult), drain=False)
    S.op(v, lambda E: E.tensor_scalar_mul(amax[:, 0:nb], amax[:, 0:nb], 1.0 / 6.0))
    S.op(v, lambda E: E.tensor_mul(mf, mf, sgn_scr[:, 0:F]), drain=False)
    ab = amax[:, 0:nb].unsqueeze(2).broadcast_to([P, nb, 16])
    S.op(v, lambda E: E.tensor_tensor(xq_out.rearrange("p (a b) -> p a b", b=16), mf3, ab, ALU.mult), inc="q_done")


def emit_transposes(S, src_bf16, dst, tile_cols, ps_a, ps_b, ident, tag, evac="scalar"):
    t, v = "tensor", "vector"
    bufs = [ps_a, ps_b]
    n = len(tile_cols)
    groups = [tile_cols[i:i + 4] for i in range(0, n, 4)]
    start_v = S.val(tag + "_v")
    S.wait(t, tag + "_v", start_v)
    for gi, grp in enumerate(groups):
        ps = bufs[gi % 2]
        if gi >= 2:
            S.wait(t, tag + "_v", start_v + gi - 1)
        for j, (c0, dst_ap) in enumerate(grp):
            S.op(t, lambda E, c0=c0, j=j, ps=ps: E.transpose(ps[:, j * 128:(j + 1) * 128], src_bf16[:, c0:c0 + 128], ident[:]),
                 inc=(tag + "_t") if j == len(grp) - 1 else None)
        S.wait(evac, tag + "_t")
        for j, (c0, dst_ap) in enumerate(grp):
            last = j == len(grp) - 1
            if evac == "scalar":
                S.op("scalar", lambda E, j=j, ps=ps, dst_ap=dst_ap: E.activation(dst_ap, ps[:, j * 128:(j + 1) * 128], AF.Copy),
                     inc=(tag + "_v") if last else None, drain=last)
            else:
                S.op("vector", lambda E, j=j, ps=ps, dst_ap=dst_ap: E.tensor_copy(dst_ap, ps[:, j * 128:(j + 1) * 128]),
                     inc=(tag + "_v") if last else None, drain=last)


def emit_l1(S, nc, es, io):
    x_in = io['x_in']
    wgq = io['wgq']
    wuq = io['wuq']
    wdq = io['wdq']
    id_in = io['id_in']
    xn0_out = io['xn0_out']
    gacc_out = io['gacc_out']
    sb = lambda n_, s_, d_=F32: es.enter_context(nc.sbuf_tensor(S.pfx + n_, s_, d_))
    ps = lambda n_, s_, d_=F32: es.enter_context(nc.psum_tensor(S.pfx + n_, s_, d_))

    xtile = sb("xtile", [128, 2, D])
    xq = sb("xq", [128, D], BF16)
    xqT = sb("xqT", [128, DC, TOK], BF16)
    xn_f = sb("xn_f", [128, D])
    gacc = sb("gacc_sb", [128, NT, D])
    amax = sb("amax", [128, I_FFN // 16])
    recip = sb("recip", [128, I_FFN // 16])
    hi_scr = sb("hi_scr", [128, D], BF16)
    r_bf = sb("r_bf", [128, D], BF16)
    m_f = sb("m_f", [128, D])
    sgn = sb("sgn", [128, D], BF16)
    mu = sb("mu", [128, 1]); rstd = sb("rstd", [128, 1]); sq = sb("sq", [128, D])
    xnb = sb("xnb", [128, D], BF16)
    wg_s = sb("wg_s", [128, DC, 768], BF16)
    wu_s = sb("wu_s", [128, DC, 768], BF16)
    wd_s = sb("wd_s", [128, 6, D], BF16)
    hsil = sb("hsil", [128, 768])
    h_f = sb("h_f", [128, 768])
    hq = sb("hq", [128, 768], BF16)
    hqT = sb("hqT", [128, 6, 128], BF16)
    ident = sb("identb", [128, 128], BF16)
    ps_ga = ps("ps_ga", [128, 1024])
    ps_up = ps("ps_up", [128, 1024])
    ps_dn = ps("ps_dn", [128, D])
    ps_ta = ps("ps_ta", [128, 1024], BF16)
    ps_tb = ps("ps_tb", [128, 1024], BF16)

    S.setup_consts([1e-5, -0.5], sb)
    S.wait("scalar", "c_init")

    S.op("gpsimd", lambda E: E.dma_start(out=ident[:], in_=id_in), inc="ld_w", n=16, drain=False)
    S.dma("sync", xtile[:, 0, :], x_in[0:128, :], "ld_x")

    for t in range(NT):
        if t + 1 < NT:
            if t + 1 >= 2:
                S.wait("sync", "v_xn", t)
            S.dma("sync", xtile[:, (t + 1) % 2, :], x_in[(t + 1) * 128:(t + 2) * 128, :], "ld_x")
        S.wait("vector", "ld_x", 16 * (t + 1))
        xt = xtile[:, t % 2, :]
        emit_ln(S, xt, mu, rstd, sq, xn_f[:])
        S.wait("vector", "st_xn", S.val("st_xn"))
        S.op("vector", lambda E: E.tensor_copy(xnb[:], xn_f[:]), inc="v_xnb")
        S.wait("sync", "v_xnb")
        S.dma("sync", xn0_out[t * 128:(t + 1) * 128, :], xnb[:], "st_xn")
        S.wait("vector", "tp_v", S.val("tp_v"))
        emit_quant(S, xn_f[:], D, amax, recip, hi_scr, r_bf, m_f, xq[:], sgn)
        S.wait("tensor", "q_done")
        S.wait("tensor", "ld_w", 16)
        cols = [(k * 128, xqT[:, k, t * 128:(t + 1) * 128]) for k in range(DC)]
        emit_transposes(S, xq[:], None, cols, ps_ta, ps_tb, ident, "tp")

    for pi, (i0, iw) in enumerate(I_PHASES):
        nkd = iw // 128
        S.wait("sync", "t_gu", S.val("t_gu"))
        S.wait("sync", "t_dn", S.val("t_dn"))
        for k in range(DC):
            S.op("sync", lambda E, k=k, i0=i0, iw=iw: E.dma_start_transpose(
                wg_s[:, k, 0:iw], wgq[i0:i0 + iw, k * 128:(k + 1) * 128]), inc="ld_w", n=16, drain=False)
            S.op("sync", lambda E, k=k, i0=i0, iw=iw: E.dma_start_transpose(
                wu_s[:, k, 0:iw], wuq[i0:i0 + iw, k * 128:(k + 1) * 128]), inc="ld_w", n=16, drain=False)
        for k in range(nkd):
            S.op("sync", lambda E, k=k, i0=i0: E.dma_start_transpose(
                wd_s[:, k, :], wdq[:, i0 + k * 128:i0 + (k + 1) * 128]), inc="ld_w", n=16, drain=False)
        S.wait("tensor", "ld_w")
        for t in range(NT):
            nchunks = [(0, 512), (512, iw - 512)] if iw > 512 else [(0, iw)]
            S.wait("tensor", "v_h", S.val("v_h"))
            S.wait("tensor", "s_sil", S.val("s_sil"))
            for k in range(DC):
                lhsT = xqT[:, k, t * 128:(t + 1) * 128]
                for ci, (n0, nw) in enumerate(nchunks):
                    S.op("tensor", lambda E, lhsT=lhsT, k=k, n0=n0, nw=nw: E.matmul(
                        ps_ga[:, n0:n0 + nw], lhsT, wg_s[:, k, n0:n0 + nw], start=(k == 0), stop=(k == DC - 1)))
                for ci, (n0, nw) in enumerate(nchunks):
                    last = (k == DC - 1) and (ci == len(nchunks) - 1)
                    S.op("tensor", lambda E, lhsT=lhsT, k=k, n0=n0, nw=nw: E.matmul(
                        ps_up[:, n0:n0 + nw], lhsT, wu_s[:, k, n0:n0 + nw], start=(k == 0), stop=(k == DC - 1)),
                        inc="t_gu" if last else None)
            S.wait("scalar", "t_gu")
            S.wait("scalar", "v_h", S.val("v_h"))
            S.op("scalar", lambda E, iw=iw: E.activation(hsil[:, 0:iw], ps_ga[:, 0:iw], AF.Silu), inc="s_sil")
            S.wait("vector", "s_sil")
            S.wait("vector", "q_xn", S.val("q_xn"))
            S.op("vector", lambda E, iw=iw: E.tensor_tensor(h_f[:, 0:iw], hsil[:, 0:iw], ps_up[:, 0:iw], ALU.mult), inc="v_h")
            S.wait("vector", "tp_v", S.val("tp_v"))
            emit_quant(S, h_f[:, 0:iw], iw, amax, recip, hi_scr, r_bf, m_f, hq[:, 0:iw], sgn)
            S.wait("tensor", "q_done")
            cols = [(k * 128, hqT[:, k, :]) for k in range(nkd)]
            emit_transposes(S, hq[:], None, cols, ps_ta, ps_tb, ident, "tp")
            S.wait("tensor", "tp_v")
            S.wait("tensor", "v_acc", S.val("v_acc"))
            for k in range(nkd):
                for (n0, nw) in [(0, 512), (512, 512)]:
                    S.op("tensor", lambda E, k=k, n0=n0, nw=nw: E.matmul(
                        ps_dn[:, n0:n0 + nw], hqT[:, k, :], wd_s[:, k, n0:n0 + nw], start=(k == 0), stop=(k == nkd - 1)),
                        inc="t_dn" if (k == nkd - 1 and n0 == 512) else None)
            S.wait("vector", "t_dn")
            if pi == 0:
                S.op("vector", lambda E, t=t: E.tensor_copy(gacc[:, t, :], ps_dn[:]), inc="v_acc")
            else:
                S.op("vector", lambda E, t=t: E.tensor_add(gacc[:, t, :], gacc[:, t, :], ps_dn[:]), inc="v_acc")

    S.wait("sync", "v_acc")
    for t in range(NT):
        S.dma("sync", gacc_out[t * 128:(t + 1) * 128, :], gacc[:, t, :], "st_g")
    S.wait("sync", "st_g")
    S.wait("sync", "st_xn")


def build_launch1():
    nc = mkbass()
    x_in = nc.dram_tensor("x", [TOK, D], F32, kind="ExternalInput").ap()
    wgq = nc.dram_tensor("wgq", [I_FFN, D], BF16, kind="ExternalInput").ap()
    wuq = nc.dram_tensor("wuq", [I_FFN, D], BF16, kind="ExternalInput").ap()
    wdq = nc.dram_tensor("wdq", [D, I_FFN], BF16, kind="ExternalInput").ap()
    id_in = nc.dram_tensor("ident", [128, 128], F32, kind="ExternalInput").ap()
    xn0_out = nc.dram_tensor("xn0", [TOK, D], BF16, kind="ExternalOutput").ap()
    gacc_out = nc.dram_tensor("gacc", [TOK, D], F32, kind="ExternalOutput").ap()
    es = ExitStack()
    S = Sched(nc); S._es = es
    io = dict(x_in=x_in, wgq=wgq, wuq=wuq, wdq=wdq, id_in=id_in, xn0_out=xn0_out, gacc_out=gacc_out)
    emit_l1(S, nc, es, io)
    with es, nc.Block() as block:
        S.build(block, es)
    return nc


def emit_l0(S, nc, es, io):
    wg_in = io['wg_in']
    wu_in = io['wu_in']
    wd_in = io['wd_in']
    wg_out = io['wg_out']
    wu_out = io['wu_out']
    wd_out = io['wd_out']
    sb = lambda n_, s_, d_=F32: es.enter_context(nc.sbuf_tensor(S.pfx + n_, s_, d_))
    ps = lambda n_, s_, d_=F32: es.enter_context(nc.psum_tensor(S.pfx + n_, s_, d_))
    wf = sb("wf", [128, I_FFN])
    wq = sb("wq", [128, I_FFN], BF16)
    amax = sb("amax", [128, I_FFN // 16]); recip = sb("recip", [128, I_FFN // 16])
    hi_scr = sb("hi_scr", [128, I_FFN], BF16); r_bf = sb("r_bf", [128, I_FFN], BF16)
    m_f = sb("m_f", [128, I_FFN]); sgn = sb("sgn", [128, I_FFN], BF16)

    jobs = []
    for r0, rw in [(0, 128), (128, 128), (256, 96)]:
        jobs.append((wg_in[r0:r0 + rw, :], wg_out[r0:r0 + rw, :], rw, D))
        jobs.append((wu_in[r0:r0 + rw, :], wu_out[r0:r0 + rw, :], rw, D))
    jobs.append((wd_in, wd_out, 128, I_FFN))

    for src_ap, dst_ap, rw, F in jobs:
        S.dma("sync", wf[0:rw, 0:F], src_ap, "ld_x")
        S.wait("vector", "ld_x")
        emit_quant(S, wf[0:rw, 0:F], F, amax[0:rw], recip[0:rw], hi_scr[0:rw],
                   r_bf[0:rw], m_f[0:rw], wq[0:rw, 0:F], sgn[0:rw])
        S.wait("sync", "q_done")
        S.dma("sync", dst_ap, wq[0:rw, 0:F], "st_xn")
        S.wait("vector", "st_xn", S.val("st_xn"))
    S.wait("sync", "st_xn")


def build_launch0():
    nc = mkbass()
    wg_in = nc.dram_tensor("wg_sl", [352, D], F32, kind="ExternalInput").ap()
    wu_in = nc.dram_tensor("wu_sl", [352, D], F32, kind="ExternalInput").ap()
    wd_in = nc.dram_tensor("wd_sl", [128, I_FFN], F32, kind="ExternalInput").ap()
    wg_out = nc.dram_tensor("wgq_sl", [352, D], BF16, kind="ExternalOutput").ap()
    wu_out = nc.dram_tensor("wuq_sl", [352, D], BF16, kind="ExternalOutput").ap()
    wd_out = nc.dram_tensor("wdq_sl", [128, I_FFN], BF16, kind="ExternalOutput").ap()
    es = ExitStack()
    S = Sched(nc); S._es = es
    io = dict(wg_in=wg_in, wu_in=wu_in, wd_in=wd_in, wg_out=wg_out, wu_out=wu_out, wd_out=wd_out)
    emit_l0(S, nc, es, io)
    with es, nc.Block() as block:
        S.build(block, es)
    return nc


def emit_xnT(S, xn_dram, xnT, xnb_scr, ps_ta, ps_tb, ident, ld_sem="ld_xn"):
    for t in range(NT):
        S.dma("sync", xnb_scr[:, t % 2, :], xn_dram[t * 128:(t + 1) * 128, :], ld_sem)
    for t in range(NT):
        S.wait("tensor", ld_sem, 16 * (t + 1))
        cols = [(k * 128, xnT[:, k, t * 128:(t + 1) * 128]) for k in range(DC)]
        emit_transposes(S, xnb_scr[:, t % 2, :], None, cols, ps_ta, ps_tb, ident, "txn")


def emit_lam(S, beta_in, lnl_b, M_iota, Mp_iota, tmask, Tmat, pv2, lam128c, onesrow, cst, ps_misc, build_T=True, build_lamv=True, build_pv2=True):
    v, s, g, t = "vector", "scalar", "gpsimd", "tensor"
    beta_sb = lnl_b
    br = S.beta_row
    lamv_ap = S.lamv
    S.wait(s, "ld_beta")
    S.wait(s, "c_init")
    c_ln2 = S.consts[-LN2]
    lam_row_ap = S.lam_row
    S.op(s, lambda E, c=c_ln2, lr=lam_row_ap: E.activation(lr, br, AF.Exp, scale=c[0:1]), inc="s_lam")
    S.wait(v, "s_lam")
    S.op(v, lambda E, lr=lam_row_ap: E.tensor_scalar(lr, lr, -1.0, 1.0, ALU.mult, ALU.add))
    S.op(v, lambda E, lr=lam_row_ap: E.tensor_scalar(lr, lr, 1e-38, 1.0 - 1e-9, ALU.max, ALU.min), inc="v_lam")
    S.wait(s, "v_lam")
    S.op(s, lambda E, lr=lam_row_ap: E.activation(lr, lr, AF.Ln), inc="s_lnl")
    S.wait(t, "s_lnl")
    S.wait(t, "c_ones")
    S.op(t, lambda E, lr=lam_row_ap: E.matmul(ps_misc[:, 0:16], onesrow[:], lr, start=True, stop=True), inc="t_lnlb")
    S.wait(v, "t_lnlb")
    S.op(v, lambda E: E.tensor_copy(lnl_b[:, 0:16], ps_misc[:, 0:16]), inc="v_lnlb")
    S.wait(t, "v_lnlb")
    S.op(t, lambda E, lr=lam_row_ap: E.matmul(ps_misc[0:16, 16:17], lr, onesrow[0:1, 0:1], start=True, stop=True), inc="t_lnlc")
    S.wait(s, "t_lnlc")
    c128 = S.consts[128.0]
    S.op(s, lambda E, c=c128: E.activation(lam128c[:, 0:1], ps_misc[0:16, 16:17], AF.Exp, scale=c[0:16]), inc="s_l128")
    S.op(g, lambda E: E.iota(M_iota[:], [[1, 128]], channel_multiplier=0, allow_small_or_imprecise_dtypes=True))
    S.op(g, lambda E: E.iota(Mp_iota[:], [[0, 128]], channel_multiplier=1, allow_small_or_imprecise_dtypes=True))
    S.op(g, lambda E: E.tensor_tensor(M_iota[:], M_iota[:], Mp_iota[:], ALU.subtract), inc="g_M")
    S.wait(v, "g_M")
    S.op(v, lambda E: E.tensor_scalar(tmask[:], M_iota[:], 0.0, None, ALU.is_ge), inc="v_msk")
    if build_T:
        S.wait(s, "v_lnlb")
        S.wait(s, "g_M")
        for h in range(H):
            S.op(s, lambda E, h=h: E.activation(Tmat[:, h, :], M_iota[:], AF.Exp, scale=lnl_b[:, h:h + 1]),
                 inc="s_T" if h == H - 1 else None)
        S.wait(v, "s_T")
        S.wait(v, "v_msk")
        for h in range(H):
            S.op(v, lambda E, h=h: E.tensor_tensor(Tmat[:, h, :], Tmat[:, h, :], tmask[:], ALU.mult),
                 inc="v_T" if h == H - 1 else None)
    if build_pv2:
        S.op(g, lambda E: E.iota(Mp_iota[0:1, :], [[1, 128]], base=1, channel_multiplier=0,
                                 allow_small_or_imprecise_dtypes=True), inc="g_pv")
        S.wait(s, "g_pv")
        S.wait(s, "s_lnl")
        for h in range(H):
            S.op(s, lambda E, h=h, lr=lam_row_ap: E.activation(pv2[0:1, h * 128:(h + 1) * 128], Mp_iota[0:1, :], AF.Exp,
                                                scale=lr[0:1, h:h + 1]),
                 inc="s_pv" if h == H - 1 else None)
    if build_lamv:
        S.op(g, lambda E: E.iota(Mp_iota[:, 0:1], [[0, 1]], base=127, channel_multiplier=-1,
                                 allow_small_or_imprecise_dtypes=True), inc="g_lv")
        S.wait(s, "g_lv")
        for h in range(H):
            S.op(s, lambda E, h=h, lv=lamv_ap: E.activation(lv[:, h:h + 1], Mp_iota[:, 0:1], AF.Exp,
                                                scale=lnl_b[:, h:h + 1]),
                 inc="s_lv" if h == H - 1 else None)


def emit_l2(S, nc, es, io):
    xn_in = io['xn_in']
    wv_in = io['wv_in']
    beta_in = io['beta_in']
    id_in = io['id_in']
    st_out = io['st_out']
    sums_out = io['sums_out']
    sb = lambda n_, s_, d_=F32: es.enter_context(nc.sbuf_tensor(S.pfx + n_, s_, d_))
    ps = lambda n_, s_, d_=F32: es.enter_context(nc.psum_tensor(S.pfx + n_, s_, d_))
    xnb = sb("xnb", [128, 2, D], BF16)
    xnT = sb("xnT", [128, DC, TOK], BF16)
    wv_s = sb("wv_s", [128, DC, D], BF16)
    vbuf = sb("vbuf", [128, NT, D], BF16)
    stb = sb("stb", [128, D], BF16)
    sumrow = sb("sumrow", [1, NT * D])
    ident = sb("identb", [128, 128], BF16)
    onesrow = sb("onesrow", [1, 128])
    lnl_b = sb("lnl_b", [128, 16]); M_iota = sb("M_iota", [128, 128]); Mp_iota = sb("Mp_iota", [128, 128])
    tmask = sb("tmask", [128, 128]); Tmat = sb("Tmat", [128, H, 128], BF16)
    pv2 = sb("pv2", [1, H * 128], BF16); lam128c = sb("lam128c", [16, 1])
    lamv = sb("lamv", [128, H], BF16)
    beta_row = sb("beta_row", [1, H]); lam_row = sb("lam_row", [1, H])
    ps_ta = ps("ps_ta", [128, 1024], BF16)
    ps_tb = ps("ps_tb", [128, 1024], BF16)
    ps_v = ps("ps_v", [128, D])
    ps_sc = ps("ps_sc", [128, D])
    ps_sum = ps("ps_sum", [1, D])

    S.setup_consts([-LN2, 128.0], sb)
    S.beta_row = beta_row[:]; S.lam_row = lam_row[:]; S.lamv = lamv[:]

    S.op("gpsimd", lambda E: E.dma_start(out=ident[:], in_=id_in), inc="ld_w", n=16, drain=False)
    S.op("gpsimd", lambda E: E.memset(onesrow[:], 1.0), inc="c_ones")
    S.dma("sync", beta_row[:], beta_in, "ld_beta")
    S.dma("gpsimd", wv_s[:].rearrange("p c d -> (c p) d") if False else wv_s[:, 0, :], wv_in[0:128, :], "ld_w")
    for k in range(1, DC):
        S.dma("gpsimd", wv_s[:, k, :], wv_in[k * 128:(k + 1) * 128, :], "ld_w")
    emit_lam(S, beta_in, lnl_b, M_iota, Mp_iota, tmask, Tmat, pv2, lam128c, onesrow, None, ps_v)

    for t in range(NT):
        if t >= 2:
            S.wait("sync", "tp_t", 2 * (t - 1))
        S.dma("sync", xnb[:, t % 2, :], xn_in[t * 128:(t + 1) * 128, :], "ld_xn")
        S.wait("tensor", "ld_xn", 16 * (t + 1))
        S.wait("tensor", "ld_w", 16)
        cols = [(k * 128, xnT[:, k, t * 128:(t + 1) * 128]) for k in range(DC)]
        emit_transposes(S, xnb[:, t % 2, :], None, cols, ps_ta, ps_tb, ident, "tp")
        S.wait("tensor", "tp_v", 2 * (t + 1))
        S.wait("tensor", "ld_w", 16 * 9)
        S.wait("tensor", "s_l128", S.val("s_l128"))
        S.wait("tensor", "v_vev", S.val("v_vev"))
        for k in range(DC):
            for n0 in (0, 512):
                S.op("tensor", lambda E, k=k, t=t, n0=n0: E.matmul(
                    ps_v[:, n0:n0 + 512], xnT[:, k, t * 128:(t + 1) * 128], wv_s[:, k, n0:n0 + 512],
                    start=(k == 0), stop=(k == DC - 1)), inc="t_v" if (k == DC - 1 and n0 == 512) else None)
        S.wait("vector", "t_v")
        S.op("vector", lambda E, t=t: E.tensor_copy(vbuf[:, t, :], ps_v[:]), inc="v_vev")

    S.wait("tensor", "v_T")
    S.wait("tensor", "s_lv")
    for t in range(NT):
        S.wait("tensor", "v_vev", min(S.val("v_vev"), t + 1))
        S.wait("tensor", "v_stev", S.val("v_stev"))
        S.wait("tensor", "v_smev", S.val("v_smev"))
        for h in range(H):
            S.op("tensor", lambda E, t=t, h=h: E.matmul(
                ps_sc[:, h * DH:(h + 1) * DH], Tmat[:, h, :], vbuf[:, t, h * DH:(h + 1) * DH],
                start=True, stop=True), inc="t_sc" if h == H - 1 else None)
            S.op("tensor", lambda E, t=t, h=h: E.matmul(
                ps_sum[0:1, h * DH:(h + 1) * DH], lamv[:, h:h + 1], vbuf[:, t, h * DH:(h + 1) * DH],
                start=True, stop=True), inc="t_sm" if h == H - 1 else None)
        S.wait("vector", "t_sc")
        S.wait("vector", "st_st", S.val("st_st"))
        S.op("vector", lambda E, t=t: E.tensor_copy(stb[:], ps_sc[:]), inc="v_stev")
        S.wait("sync", "v_stev")
        S.dma("sync", st_out[t * 128:(t + 1) * 128, :], stb[:], "st_st")
        S.wait("vector", "t_sm")
        S.op("vector", lambda E, t=t: E.tensor_copy(sumrow[0:1, t * D:(t + 1) * D], ps_sum[0:1, :]), inc="v_smev")
    S.wait("sync", "v_smev")
    S.dma("sync", sums_out.rearrange("t d -> (t d)").unsqueeze(0), sumrow[:], "st_sm")
    S.wait("sync", "st_sm")
    S.wait("sync", "st_st")



def build_launch2():
    nc = mkbass()
    xn_in = nc.dram_tensor("xn0", [TOK, D], BF16, kind="ExternalInput").ap()
    wv_in = nc.dram_tensor("WvT", [D, D], F32, kind="ExternalInput").ap()
    beta_in = nc.dram_tensor("beta", [1, H], F32, kind="ExternalInput").ap()
    id_in = nc.dram_tensor("ident", [128, 128], F32, kind="ExternalInput").ap()
    st_out = nc.dram_tensor("state_local", [TOK, D], BF16, kind="ExternalOutput").ap()
    sums_out = nc.dram_tensor("sums", [NT, D], F32, kind="ExternalOutput").ap()
    es = ExitStack()
    S = Sched(nc); S._es = es
    io = dict(xn_in=xn_in, wv_in=wv_in, beta_in=beta_in, id_in=id_in, st_out=st_out, sums_out=sums_out)
    emit_l2(S, nc, es, io)
    with es, nc.Block() as block:
        S.build(block, es)
    return nc


def emit_l3(S, nc, es, io):
    xn_in = io['xn_in']
    stl_in = io['stl_in']
    sums_in = io['sums_in']
    x_in = io['x_in']
    gacc_in = io['gacc_in']
    wq_in = io['wq_in']
    wg_in = io['wg_in']
    wo_in = io['wo_in']
    beta_in = io['beta_in']
    gam_in = io['gam_in']
    bet_in = io['bet_in']
    id_in = io['id_in']
    cidx_in = io['cidx_in']
    x1_out = io['x1_out']
    carr_dram = io['carr_dram']
    carrmy_dram = io['carrmy_dram']
    sb = lambda n_, s_, d_=F32: es.enter_context(nc.sbuf_tensor(S.pfx + n_, s_, d_))
    ps = lambda n_, s_, d_=F32: es.enter_context(nc.psum_tensor(S.pfx + n_, s_, d_))
    xnb = sb("xnb", [128, 2, D], BF16)
    xnT_t = sb("xnT_t", [128, DC, 128], BF16)
    wq_s = sb("wq_s", [128, DC, D], BF16)
    wg_s = sb("wg_s", [128, DC, D], BF16)
    wo_s = sb("wo_s", [128, DC, D], BF16)
    ident = sb("identb", [128, 128], BF16)
    onesrow = sb("onesrow", [1, 128])
    lnl_b = sb("lnl_b", [128, 16]); M_iota = sb("M_iota", [128, 128]); Mp_iota = sb("Mp_iota", [128, 128])
    tmask = sb("tmask", [128, 128])
    pv2 = sb("pv2", [1, H * 128], BF16); lam128c = sb("lam128c", [16, 1])
    beta_row = sb("beta_row", [1, H]); lam_row = sb("lam_row", [1, H])
    sums_sb = sb("sums_sb", [16, 2 * NT * DH])
    carr_sb = sb("carr_sb", [16, 2 * NT * DH])
    carrg = sb("carrg", [NT, D])
    carr1p = sb("carr1p", [1, NT * D], BF16)
    gbrow = sb("gbrow", [1, 2 * D])
    cidx = sb("cidx", [NT, 1], I32)
    gb = sb("gb", [128, D]); bb = sb("bb", [128, D])
    stl_b = sb("stl_b", [128, D], BF16)
    stf = sb("stf", [128, D])
    ret_f = sb("ret_f", [128, D])
    y_f = sb("y_f", [128, D])
    g0_f = sb("g0_f", [128, D])
    u_b = sb("u_b", [128, D], BF16)
    uT = sb("uT", [128, DC, 128], BF16)
    xt = sb("xt", [128, D]); gat = sb("gat", [128, D]); gsil = sb("gsil", [128, D])
    x1_b = sb("x1_b", [128, D])
    mu = sb("mu", [128, 1]); rstd = sb("rstd", [128, 1]); sq = sb("sq", [128, D])
    ps_c = ps("ps_c", [128, D])
    ps_q = ps("ps_q", [128, D])
    ps_g = ps("ps_g", [128, D])
    ps_ta = ps("ps_ta", [128, 1024], BF16)
    ps_tb = ps("ps_tb", [128, 1024], BF16)

    S.setup_consts([-LN2, 128.0, 1e-5, -0.5], sb)
    S.wait("scalar", "c_init")
    S.beta_row = beta_row[:]; S.lam_row = lam_row[:]; S.lamv = None

    S.op("gpsimd", lambda E: E.dma_start(out=ident[:], in_=id_in), inc="ld_w", n=16, drain=False)
    S.op("gpsimd", lambda E: E.memset(onesrow[:], 1.0), inc="c_ones")
    S.dma("sync", beta_row[:], beta_in, "ld_beta")
    S.dma("sync", cidx[:], cidx_in, "ld_misc")
    for name, src_ap, dst in [("wq", wq_in, wq_s), ("wg", wg_in, wg_s), ("wo", wo_in, wo_s)]:
        for k in range(DC):
            S.dma("gpsimd", dst[:, k, :], src_ap[k * 128:(k + 1) * 128, :], "ld_w")
    emit_lam(S, beta_in, lnl_b, M_iota, Mp_iota, tmask, None, pv2, lam128c, onesrow, None, ps_q,
             build_T=False, build_lamv=False)
    S.dma("sync", gbrow[0:1, 0:D], gam_in, "ld_misc")
    S.dma("sync", gbrow[0:1, D:2 * D], bet_in, "ld_misc")
    S.wait("tensor", "ld_misc", 16 * 3)
    S.wait("tensor", "c_ones")
    for n0 in (0, 512):
        S.op("tensor", lambda E, n0=n0: E.matmul(ps_c[:, n0:n0 + 512], onesrow[:], gbrow[0:1, n0:n0 + 512], start=True, stop=True))
    for n0 in (0, 512):
        S.op("tensor", lambda E, n0=n0: E.matmul(ps_g[:, n0:n0 + 512], onesrow[:], gbrow[0:1, D + n0:D + n0 + 512], start=True, stop=True),
             inc="t_gbb" if n0 == 512 else None)
    S.wait("vector", "t_gbb")
    S.op("vector", lambda E: E.tensor_copy(gb[:], ps_c[:]))
    S.op("vector", lambda E: E.tensor_copy(bb[:], ps_g[:]), inc="v_gbb")

    S.dma("sync", sums_sb[:].rearrange("h (c d) -> h c d", d=DH),
          sums_in.rearrange("c (h d) -> h c d", d=DH), "ld_sums")
    S.wait("vector", "ld_sums")
    S.wait("vector", "s_l128")
    S.op("vector", lambda E: E.memset(carr_sb[:, 0:DH], 0.0))
    for c in range(1, 2 * NT):
        S.op("vector", lambda E, c=c: E.scalar_tensor_tensor(
            carr_sb[:, c * DH:(c + 1) * DH], carr_sb[:, (c - 1) * DH:c * DH], lam128c[:, 0:1],
            sums_sb[:, (c - 1) * DH:c * DH], ALU.mult, ALU.add),
            inc="v_chain" if c == 2 * NT - 1 else None)
    S.wait("sync", "v_chain")
    S.dma("sync", carr_dram.rearrange("c (h d) -> h c d", d=DH),
          carr_sb[:].rearrange("h (c d) -> h c d", d=DH), "st_carr")
    S.wait("gpsimd", "st_carr")
    S.wait("gpsimd", "ld_misc", 16)
    S.op("gpsimd", lambda E: E.indirect_dma_start(carrg[:], None, carr_dram,
                                                  bass.IndirectOffsetOnAxis(cidx[:], 0)),
         inc="g_carr", n=16, drain=False)
    S.wait("sync", "g_carr")
    S.dma("sync", carrmy_dram, carrg[:], "st_carr")
    S.wait("sync", "st_carr", 32)
    S.wait("gpsimd", "st_carr", 32)
    S.op("gpsimd", lambda E: E.dma_start(out=carr1p[:], in_=carrmy_dram.rearrange("c d -> (c d)").unsqueeze(0)),
         inc="ld_carr", n=16, drain=False)

    for t in range(NT):
        if t >= 2:
            S.wait("sync", "tp_t", 4 * (t - 2) + 2)
        S.dma("sync", xnb[:, t % 2, :], xn_in[t * 128:(t + 1) * 128, :], "ld_xn")
        S.wait("tensor", "ld_xn", 16 * (t + 1))
        S.wait("tensor", "ld_w", 16)
        S.wait("tensor", "s_l128", 1)
        cols = [(k * 128, xnT_t[:, k, :]) for k in range(DC)]
        S.wait("tensor", "t_q", S.val("t_q"))
        emit_transposes(S, xnb[:, t % 2, :], None, cols, ps_ta, ps_tb, ident, "tp")
        txn_v_done = S.val("tp_v")
        S.wait("tensor", "ld_carr")
        S.wait("tensor", "s_pv")
        S.wait("tensor", "v_ret", S.val("v_ret"))
        S.wait("tensor", "v_gbb")
        for h in range(H):
            S.op("tensor", lambda E, t=t, h=h: E.matmul(
                ps_c[:, h * DH:(h + 1) * DH], pv2[0:1, h * 128:(h + 1) * 128],
                carr1p[0:1, t * D + h * DH:t * D + (h + 1) * DH], start=True, stop=True),
                inc="t_corr" if h == H - 1 else None)
        S.wait("tensor", "tp_v", txn_v_done)
        S.wait("tensor", "ld_w", 16 * (1 + 3 * DC))
        S.wait("tensor", "v_ret", S.val("v_ret"))
        S.wait("tensor", "s_g0", S.val("s_g0"))
        for k in range(DC):
            for n0 in (0, 512):
                S.op("tensor", lambda E, k=k, n0=n0: E.matmul(
                    ps_q[:, n0:n0 + 512], xnT_t[:, k, :], wq_s[:, k, n0:n0 + 512],
                    start=(k == 0), stop=(k == DC - 1)))
        for k in range(DC):
            for n0 in (0, 512):
                S.op("tensor", lambda E, k=k, n0=n0: E.matmul(
                    ps_g[:, n0:n0 + 512], xnT_t[:, k, :], wg_s[:, k, n0:n0 + 512],
                    start=(k == 0), stop=(k == DC - 1)),
                    inc="t_q" if (k == DC - 1 and n0 == 512) else None)
        S.dma("sync", stl_b[:], stl_in[t * 128:(t + 1) * 128, :], "ld_stl")
        S.wait("vector", "ld_stl", 16 * (t + 1))
        S.wait("vector", "t_corr")
        S.op("vector", lambda E: E.tensor_tensor(stf[:], stl_b[:], ps_c[:], ALU.add))
        S.wait("vector", "t_q")
        S.op("vector", lambda E: E.tensor_tensor(ret_f[:], ps_q[:], stf[:], ALU.mult), inc="v_ret")
        emit_ln(S, ret_f[:], mu, rstd, sq, y_f[:])
        S.op("vector", lambda E: E.tensor_mul(y_f[:], y_f[:], gb[:]))
        S.op("vector", lambda E: E.tensor_add(y_f[:], y_f[:], bb[:]), inc="v_y")
        S.wait("scalar", "t_q")
        S.wait("scalar", "v_u", S.val("v_u"))
        S.op("scalar", lambda E: E.activation(g0_f[:], ps_g[:], AF.Silu), inc="s_g0")
        S.wait("vector", "s_g0")
        S.wait("vector", "tp_t", S.val("tp_t"))
        S.op("vector", lambda E: E.tensor_tensor(u_b[:], y_f[:], g0_f[:], ALU.mult), inc="v_u")
        S.wait("tensor", "v_u")
        colsu = [(k * 128, uT[:, k, :]) for k in range(DC)]
        emit_transposes(S, u_b[:], None, colsu, ps_ta, ps_tb, ident, "tp")
        S.wait("tensor", "tp_v")
        S.wait("tensor", "v_x1", S.val("v_x1"))
        for k in range(DC):
            for n0 in (0, 512):
                S.op("tensor", lambda E, k=k, n0=n0: E.matmul(
                    ps_q[:, n0:n0 + 512], uT[:, k, :], wo_s[:, k, n0:n0 + 512],
                    start=(k == 0), stop=(k == DC - 1)),
                    inc="t_s" if (k == DC - 1 and n0 == 512) else None)
        S.wait("sync", "v_x1", S.val("v_x1"))
        S.dma("sync", xt[:], x_in[t * 128:(t + 1) * 128, :], "ld_x")
        S.dma("sync", gat[:], gacc_in[t * 128:(t + 1) * 128, :], "ld_x")
        S.wait("scalar", "ld_x", 32 * (t + 1))
        S.op("scalar", lambda E: E.activation(gsil[:], gat[:], AF.Silu), inc="s_gs")
        S.wait("vector", "s_gs")
        S.wait("vector", "t_s")
        S.wait("vector", "st_x1", S.val("st_x1"))
        S.op("vector", lambda E: E.tensor_tensor(x1_b[:], ps_q[:], gsil[:], ALU.mult))
        S.op("vector", lambda E: E.tensor_add(x1_b[:], x1_b[:], xt[:]), inc="v_x1")
        S.wait("sync", "v_x1")
        S.dma("sync", x1_out[t * 128:(t + 1) * 128, :], x1_b[:], "st_x1")
    S.wait("sync", "st_x1")



def build_launch3():
    nc = mkbass()
    xn_in = nc.dram_tensor("xn0", [TOK, D], BF16, kind="ExternalInput").ap()
    stl_in = nc.dram_tensor("state_local", [TOK, D], BF16, kind="ExternalInput").ap()
    sums_in = nc.dram_tensor("sums_all", [2 * NT, D], F32, kind="ExternalInput").ap()
    x_in = nc.dram_tensor("x", [TOK, D], F32, kind="ExternalInput").ap()
    gacc_in = nc.dram_tensor("gacc", [TOK, D], F32, kind="ExternalInput").ap()
    wq_in = nc.dram_tensor("WqT", [D, D], F32, kind="ExternalInput").ap()
    wg_in = nc.dram_tensor("WgT", [D, D], F32, kind="ExternalInput").ap()
    wo_in = nc.dram_tensor("WoT", [D, D], F32, kind="ExternalInput").ap()
    beta_in = nc.dram_tensor("beta", [1, H], F32, kind="ExternalInput").ap()
    gam_in = nc.dram_tensor("ln_gamma", [1, D], F32, kind="ExternalInput").ap()
    bet_in = nc.dram_tensor("ln_beta", [1, D], F32, kind="ExternalInput").ap()
    id_in = nc.dram_tensor("ident", [128, 128], F32, kind="ExternalInput").ap()
    cidx_in = nc.dram_tensor("carry_idx", [NT, 1], I32, kind="ExternalInput").ap()
    x1_out = nc.dram_tensor("x1", [TOK, D], F32, kind="ExternalOutput").ap()
    carr_dram = nc.dram_tensor("carr_dram", [2 * NT, D], F32).ap()
    carrmy_dram = nc.dram_tensor("carrmy_dram", [NT, D], F32).ap()
    es = ExitStack()
    S = Sched(nc); S._es = es
    io = dict(xn_in=xn_in, stl_in=stl_in, sums_in=sums_in, x_in=x_in, gacc_in=gacc_in, wq_in=wq_in, wg_in=wg_in, wo_in=wo_in, beta_in=beta_in, gam_in=gam_in, bet_in=bet_in, id_in=id_in, cidx_in=cidx_in, x1_out=x1_out, carr_dram=carr_dram, carrmy_dram=carrmy_dram)
    emit_l3(S, nc, es, io)
    with es, nc.Block() as block:
        S.build(block, es)
    return nc


def emit_rms(S, x_tile, rstd, sq, wb_tile, out_bf, eps=1e-6):
    v, s = "vector", "scalar"
    S.op(v, lambda E: E.tensor_mul(sq[:], x_tile, x_tile), drain=False)
    S.op(v, lambda E: E.reduce_sum(rstd[:], sq[:], AX.X))
    S.op(v, lambda E: E.tensor_scalar_mul(rstd[:], rstd[:], 1.0 / 1024), inc="v_rms")
    S.wait(s, "v_rms")
    c_eps, c_mh = S.consts[eps], S.consts[-0.5]
    S.op(s, lambda E, c=c_eps: E.activation(rstd[:], rstd[:], AF.Ln, bias=c))
    S.op(s, lambda E, c=c_mh: E.activation(rstd[:], rstd[:], AF.Exp, scale=c), inc="s_rms")
    S.wait(v, "s_rms")
    S.op(v, lambda E: E.tensor_scalar(sq[:], x_tile, rstd[:], None, ALU.mult), drain=False)
    S.op(v, lambda E: E.tensor_tensor(out_bf, sq[:], wb_tile, ALU.mult), inc="v_xn")


def emit_bcast_row(S, row_ap, onesrow, ps_full, dst, tag):
    for n0 in (0, 512):
        S.op("tensor", lambda E, n0=n0: E.matmul(ps_full[:, n0:n0 + 512], onesrow[:], row_ap[0:1, n0:n0 + 512],
                                                 start=True, stop=True), inc=tag + "_t" if n0 == 512 else None)
    S.wait("vector", tag + "_t")
    S.op("vector", lambda E: E.tensor_copy(dst, ps_full[:]), inc=tag + "_v")


def emit_l4(S, nc, es, io):
    x1_in = io['x1_in']
    n1_in = io['n1_in']
    ws_in = io['ws_in']
    we_in = io['we_in']
    bias_in = io['bias_in']
    id_in = io['id_in']
    xn1_out = io['xn1_out']
    xnr1_out = io['xnr1_out']
    g1_out = io['g1_out']
    sb = lambda n_, s_, d_=F32: es.enter_context(nc.sbuf_tensor(S.pfx + n_, s_, d_))
    ps = lambda n_, s_, d_=F32: es.enter_context(nc.psum_tensor(S.pfx + n_, s_, d_))
    xt = sb("xt", [128, 2, D])
    wc = sb("wc", [128, DC, D], BF16)
    wscr = sb("wscr", [128, 2, D])
    ident = sb("identb", [128, 128], BF16)
    onesrow = sb("onesrow", [1, 128])
    n1b = sb("n1b", [128, D]); biasb = sb("biasb", [128, D])
    rowscr = sb("rowscr", [1, 2 * D])
    xnb = sb("xnb", [128, D], BF16)
    xnrb = sb("xnrb", [128, D], BF16)
    mu = sb("mu", [128, 1])
    xnT_t = sb("xnT_t", [128, DC, 128], BF16)
    gf = sb("gf", [128, D])
    rstd = sb("rstd", [128, 1]); sq = sb("sq", [128, D])
    ps_a = ps("ps_a", [128, 1024])
    ps_b = ps("ps_b", [128, 1024])
    ps_ta = ps("ps_ta", [128, 1024], BF16)
    ps_tb = ps("ps_tb", [128, 1024], BF16)

    S.setup_consts([1e-6, -0.5, 1e-5], sb)
    S.wait("scalar", "c_init")
    S.op("gpsimd", lambda E: E.dma_start(out=ident[:], in_=id_in), inc="ld_w", n=16, drain=False)
    S.op("gpsimd", lambda E: E.memset(onesrow[:], 1.0), inc="c_ones")
    for k in range(DC):
        S.dma("sync", wscr[:, 0, :], ws_in[k * 128:(k + 1) * 128, :], "ld_stl")
        S.dma("sync", wscr[:, 1, :], we_in[k * 128:(k + 1) * 128, :], "ld_stl")
        S.wait("vector", "ld_stl", 32 * (k + 1))
        S.op("vector", lambda E, k=k: E.tensor_add(wc[:, k, :], wscr[:, 0, :], wscr[:, 1, :]), inc="v_wc")
        if k < DC - 1:
            S.wait("sync", "v_wc")
    S.dma("sync", rowscr[0:1, 0:D], n1_in, "ld_misc")
    S.dma("sync", rowscr[0:1, D:2 * D], bias_in, "ld_misc")
    S.wait("tensor", "ld_misc")
    S.wait("tensor", "c_ones")
    emit_bcast_row(S, rowscr[0:1, 0:D], onesrow, ps_a, n1b[:], "bc1")
    S.wait("tensor", "bc1_v")
    emit_bcast_row(S, rowscr[0:1, D:2 * D], onesrow, ps_a, biasb[:], "bc2")

    for t in range(NT):
        if t >= 2:
            S.wait("sync", "v_xn", t - 1)
        S.dma("sync", xt[:, t % 2, :], x1_in[t * 128:(t + 1) * 128, :], "ld_x")
        S.wait("vector", "ld_x", 16 * (t + 1))
        S.wait("vector", "bc1_v"); S.wait("vector", "bc2_v")
        S.wait("vector", "tp_t", S.val("tp_t"))
        S.wait("vector", "st_xn", S.val("st_xn"))
        emit_ln(S, xt[:, t % 2, :], mu, rstd, sq, sq[:], eps=1e-5)
        S.op("vector", lambda E: E.tensor_copy(xnb[:], sq[:]), inc="v_xnb")
        emit_rms(S, xnb[:], rstd, sq, n1b[:], xnrb[:])
        S.wait("sync", "v_xn")
        S.dma("sync", xn1_out[t * 128:(t + 1) * 128, :], xnb[:], "st_xn")
        S.dma("sync", xnr1_out[t * 128:(t + 1) * 128, :], xnrb[:], "st_xn")
        S.wait("tensor", "v_xn", S.val("v_xn"))
        S.wait("tensor", "ld_w", 16)
        cols = [(k * 128, xnT_t[:, k, :]) for k in range(DC)]
        S.wait("tensor", "t_g", S.val("t_g"))
        emit_transposes(S, xnb[:], None, cols, ps_ta, ps_tb, ident, "tp")
        S.wait("tensor", "tp_v")
        S.wait("tensor", "v_wc", DC)
        S.wait("tensor", "v_g1", S.val("v_g1"))
        for k in range(DC):
            for n0 in (0, 512):
                S.op("tensor", lambda E, k=k, n0=n0: E.matmul(
                    ps_b[:, n0:n0 + 512], xnT_t[:, k, :], wc[:, k, n0:n0 + 512],
                    start=(k == 0), stop=(k == DC - 1)),
                    inc="t_g" if (k == DC - 1 and n0 == 512) else None)
        S.wait("vector", "t_g")
        S.wait("scalar", "v_g1", S.val("v_g1"))
        S.op("vector", lambda E: E.tensor_tensor(gf[:], ps_b[:], biasb[:], ALU.add), inc="v_gb")
        S.wait("scalar", "v_gb")
        S.op("scalar", lambda E: E.activation(gf[:], gf[:], AF.Silu), inc="s_g1")
        S.wait("sync", "s_g1")
        S.dma("sync", g1_out[t * 128:(t + 1) * 128, :], gf[:], "st_g1")
        S.wait("vector", "st_g1", S.val("st_g1"))
        S.op("vector", lambda E: E.engine_nop(), inc="v_g1", drain=False)
    S.wait("sync", "st_g1")
    S.wait("sync", "st_xn")



def build_launch4():
    nc = mkbass()
    x1_in = nc.dram_tensor("x1", [TOK, D], F32, kind="ExternalInput").ap()
    n1_in = nc.dram_tensor("norm1_w", [1, D], F32, kind="ExternalInput").ap()
    ws_in = nc.dram_tensor("WsT", [D, D], F32, kind="ExternalInput").ap()
    we_in = nc.dram_tensor("WeT", [D, D], F32, kind="ExternalInput").ap()
    bias_in = nc.dram_tensor("bias", [1, D], F32, kind="ExternalInput").ap()
    id_in = nc.dram_tensor("ident", [128, 128], F32, kind="ExternalInput").ap()
    xn1_out = nc.dram_tensor("xn1", [TOK, D], BF16, kind="ExternalOutput").ap()
    xnr1_out = nc.dram_tensor("xnr1", [TOK, D], BF16, kind="ExternalOutput").ap()
    g1_out = nc.dram_tensor("g1", [TOK, D], F32, kind="ExternalOutput").ap()
    es = ExitStack()
    S = Sched(nc); S._es = es
    io = dict(x1_in=x1_in, n1_in=n1_in, ws_in=ws_in, we_in=we_in, bias_in=bias_in, id_in=id_in, xn1_out=xn1_out, xnr1_out=xnr1_out, g1_out=g1_out)
    emit_l4(S, nc, es, io)
    with es, nc.Block() as block:
        S.build(block, es)
    return nc


NTF = 32


def emit_rope_qknorm(S, src_ps, nheads, ct, st, qf, qr, sqs, rstd8, out_bf, eps=1e-6):
    v, s = "vector", "scalar"
    W = nheads * DH
    S.op(v, lambda E: E.tensor_copy(qf[:, 0:W], src_ps), inc="v_qev")
    qe = qf[:, 0:W].rearrange("p (h f two) -> p h f two", h=nheads, two=2)[:, :, :, 0:1].squeeze(3)
    qo = qf[:, 0:W].rearrange("p (h f two) -> p h f two", h=nheads, two=2)[:, :, :, 1:2].squeeze(3)
    re = qr[:, 0:W].rearrange("p (h f two) -> p h f two", h=nheads, two=2)[:, :, :, 0:1].squeeze(3)
    im = qr[:, 0:W].rearrange("p (h f two) -> p h f two", h=nheads, two=2)[:, :, :, 1:2].squeeze(3)
    cb = ct.unsqueeze(1).broadcast_to([128, nheads, 32])
    sb_ = st.unsqueeze(1).broadcast_to([128, nheads, 32])
    s1 = sqs[:, 0:W // 2].rearrange("p (h f) -> p h f", h=nheads)
    S.op(v, lambda E: E.tensor_tensor(re, qe, cb, ALU.mult), drain=False)
    S.op(v, lambda E: E.tensor_tensor(s1, qo, sb_, ALU.mult), drain=False)
    S.op(v, lambda E: E.tensor_tensor(re, re, s1, ALU.subtract), drain=False)
    S.op(v, lambda E: E.tensor_tensor(im, qe, sb_, ALU.mult), drain=False)
    S.op(v, lambda E: E.tensor_tensor(s1, qo, cb, ALU.mult), drain=False)
    S.op(v, lambda E: E.tensor_tensor(im, im, s1, ALU.add), drain=False)
    S.op(v, lambda E: E.tensor_mul(sqs[:, 0:W], qr[:, 0:W], qr[:, 0:W]), drain=False)
    S.op(v, lambda E: E.tensor_reduce(rstd8[:, 0:nheads], sqs[:, 0:W].rearrange("p (h f) -> p h f", h=nheads),
                                      AX.X, ALU.add))
    S.op(v, lambda E: E.tensor_scalar_mul(rstd8[:, 0:nheads], rstd8[:, 0:nheads], 1.0 / DH), inc="v_qn")
    S.wait(s, "v_qn")
    c_eps, c_mh = S.consts[eps], S.consts[-0.5]
    S.op(s, lambda E, c=c_eps: E.activation(rstd8[:, 0:nheads], rstd8[:, 0:nheads], AF.Ln, bias=c))
    S.op(s, lambda E, c=c_mh: E.activation(rstd8[:, 0:nheads], rstd8[:, 0:nheads], AF.Exp, scale=c), inc="s_qn")
    S.wait(v, "s_qn")
    rb = rstd8[:, 0:nheads].unsqueeze(2).broadcast_to([128, nheads, DH])
    S.op(v, lambda E: E.tensor_tensor(out_bf.rearrange("p (h f) -> p h f", h=nheads),
                                      qr[:, 0:W].rearrange("p (h f) -> p h f", h=nheads), rb, ALU.mult),
         inc="v_qb")


def emit_l5(S, nc, es, io, local):
    nmask = 6 if local else 4
    xn_in = io['xn_in']; wq_in = io['wq_in']; cos_in = io['cos_in']; sin_in = io['sin_in']
    mask_in = io['mask_in']; id_in = io['id_in']; mg_out = io['mg_out']
    if not local:
        kvd_in = io['kvd_in']; ku_in = io['ku_in']; vu_in = io['vu_in']
    else:
        wk_in = io['wk_in']; wv_in = io['wv_in']
    xn_idx = io.get('xn_idx')
    sb = lambda n_, s_, d_=F32: es.enter_context(nc.sbuf_tensor(S.pfx + n_, s_, d_))
    ps = lambda n_, s_, d_=F32: es.enter_context(nc.psum_tensor(S.pfx + n_, s_, d_))
    xnb = sb("xnb", [128, 2, D], BF16)
    xnT_t = sb("xnT_t", [128, DC, 128], BF16)
    wq_s = sb("wq_s", [128, DC, 512], BF16)
    if not local:
        kvd_s = sb("kvd_s", [128, DC, 256], BF16)
        ku_s = sb("ku_s", [128, 2, 128], BF16)
        vu_s = sb("vu_s", [128, 2, 128], BF16)
    else:
        wk_s = sb("wk_s", [128, DC, 128], BF16)
        wv_s = sb("wv_s", [128, DC, 128], BF16)
    qT = sb("qT", [128, 4, 2 * TOK], BF16)
    kT = sb("kT", [128, 2 * TOK], BF16)
    vb = sb("vb", [128, NTF, 128], BF16)
    merged = sb("merged", [128, 4, 2 * TOK], BF16)
    mask_s = sb("mask_s", [128, nmask, 512], BF16)
    ident = sb("identb", [128, 128], BF16)
    onesrow = sb("onesrow", [1, 128])
    onesp = sb("onesp", [64, 128])
    onescol_b = sb("onescol_b", [128, 1], BF16)
    ctile = sb("ctile", [128, 2, 32]); stile = sb("stile", [128, 2, 32])
    qf = sb("qf", [128, 512]); qr = sb("qr", [128, 512]); sqs = sb("sqs", [128, 512])
    rstd8 = sb("rstd8", [128, 8])
    latb = sb("latb", [128, 256], BF16)
    latT_t = sb("latT_t", [128, 2, 128], BF16)
    kb = sb("kb", [128, 128], BF16)
    pAB = sb("pAB", [128, 3, 1024], BF16)
    rs_sb = sb("rs_sb", [128, 512])
    rs2 = sb("rs2", [128, 512])
    P1 = ps("P1", [128, 1024])
    P2 = ps("P2", [128, 1024])
    P3 = ps("P3", [128, 1024])
    es_proj = ExitStack()
    psp = lambda n_, s_, d_=F32: es_proj.enter_context(nc.psum_tensor(S.pfx + n_, s_, d_))
    ps_ta = psp("ps_ta", [128, 1024], BF16)
    ps_tb = psp("ps_tb", [128, 1024], BF16)

    S.setup_consts([1e-6, -0.5], sb)
    S.wait("scalar", "c_init")
    S.op("gpsimd", lambda E: E.dma_start(out=ident[:], in_=id_in), inc="ld_w", n=16, drain=False)
    S.op("gpsimd", lambda E: E.memset(onesrow[:], 1.0), inc="c_ones")
    S.op("gpsimd", lambda E: E.memset(onesp[:], 1.0), inc="c_ones")
    S.op("gpsimd", lambda E: E.memset(onescol_b[:], 1.0), inc="c_ones")
    for k in range(DC):
        S.dma("gpsimd", wq_s[:, k, :], wq_in[k * 128:(k + 1) * 128, :], "ld_w")
    if not local:
        for k in range(DC):
            S.dma("gpsimd", kvd_s[:, k, :], kvd_in[k * 128:(k + 1) * 128, :], "ld_w")
        for j in range(2):
            S.dma("gpsimd", ku_s[:, j, :], ku_in[j * 128:(j + 1) * 128, :], "ld_w")
            S.dma("gpsimd", vu_s[:, j, :], vu_in[j * 128:(j + 1) * 128, :], "ld_w")
        n_wdma = DC * 2 + 4
    else:
        for k in range(DC):
            S.dma("gpsimd", wk_s[:, k, :], wk_in[k * 128:(k + 1) * 128, :], "ld_w")
            S.dma("gpsimd", wv_s[:, k, :], wv_in[k * 128:(k + 1) * 128, :], "ld_w")
        n_wdma = DC * 3
    for j in range(nmask):
        S.dma("sync", mask_s[:, j, :], mask_in[j], "ld_m")

    if xn_idx is not None:
        idxb5 = sb("idxb5", [128, NTF], I32)
        S.dma("sync", idxb5[:], xn_idx, "ld_ix")
        if 'cc_wait' in io:
            S.wait("gpsimd", "!cc", io['cc_wait'])
        S.wait("gpsimd", "ld_ix")
    for t in range(NTF):
        if t >= 2:
            S.wait("sync", "tp_t", S.val("tp_t") - 2)
            if xn_idx is not None:
                S.wait("gpsimd", "tp_t", S.val("tp_t") - 2)
        if t >= 2:
            S.wait("sync", "v_qb", 2 * (t - 1))
        if xn_idx is None:
            S.dma("sync", xnb[:, t % 2, :], xn_in[t * 128:(t + 1) * 128, :], "ld_xb")
        else:
            S.op("gpsimd", lambda E, t=t: E.indirect_dma_start(
                xnb[:, t % 2, :], None, xn_in, bass.IndirectOffsetOnAxis(idxb5[:, t:t + 1], 0)),
                inc="ld_xb", n=16, drain=False)
        S.dma("sync", ctile[:, t % 2, :], cos_in[t * 128:(t + 1) * 128, :], "ld_xn")
        S.dma("sync", stile[:, t % 2, :], sin_in[t * 128:(t + 1) * 128, :], "ld_xn")
        S.wait("tensor", "ld_xb", 16 * (t + 1))
        S.wait("tensor", "ld_xn", 32 * (t + 1))
        S.wait("vector", "ld_xn", 32 * (t + 1))
        S.wait("vector", "tp_t", S.val("tp_t"))
        S.wait("tensor", "ld_w", 16)
        cols = [(k * 128, xnT_t[:, k, :]) for k in range(DC)]
        S.wait("tensor", "t_proj", S.val("t_proj"))
        emit_transposes(S, xnb[:, t % 2, :], None, cols, ps_ta, ps_tb, ident, "tp")
        S.wait("tensor", "tp_v")
        if t == 0 and 'dbg_xnT' in io:
            S.wait("sync", "tp_v", S.val("tp_v"))
            S.dma("sync", io['dbg_xnT'], xnT_t[:].rearrange("p c f -> p (c f)"), "st_mg")
        S.wait("tensor", "ld_w", 16 * (1 + n_wdma))
        S.wait("tensor", "v_qev", S.val("v_qev"))
        S.wait("tensor", "v_lat", S.val("v_lat"))
        for k in range(DC):
            S.op("tensor", lambda E, k=k: E.matmul(P1[:, 0:512], xnT_t[:, k, :], wq_s[:, k, :],
                                                   start=(k == 0), stop=(k == DC - 1)))
        if not local:
            for k in range(DC):
                S.op("tensor", lambda E, k=k: E.matmul(P1[:, 512:768], xnT_t[:, k, :], kvd_s[:, k, :],
                                                       start=(k == 0), stop=(k == DC - 1)),
                     inc="t_proj" if k == DC - 1 else None)
        else:
            S.wait("tensor", "v_kev", S.val("v_kev"))
            S.wait("tensor", "v_vev", S.val("v_vev"))
            for k in range(DC):
                S.op("tensor", lambda E, k=k: E.matmul(P2[:, 0:128], xnT_t[:, k, :], wk_s[:, k, :],
                                                       start=(k == 0), stop=(k == DC - 1)))
            for k in range(DC):
                S.op("tensor", lambda E, k=k: E.matmul(P2[:, 128:256], xnT_t[:, k, :], wv_s[:, k, :],
                                                       start=(k == 0), stop=(k == DC - 1)),
                     inc="t_proj" if k == DC - 1 else None)
        S.wait("vector", "t_proj")
        emit_rope_qknorm(S, P1[:, 0:512], 8, ctile[:, t % 2, :], stile[:, t % 2, :],
                         qf, qr, sqs, rstd8, sqs[:, 0:512].bitcast(BF16)[:, 0:512])
        if t == 0 and 'dbg_qf' in io:
            S.wait("sync", "v_qev", S.val("v_qev"))
            S.dma("sync", io['dbg_qf'], qf[:, 0:512], "st_mg")
        qbv = sqs[:, 0:512].bitcast(BF16)[:, 0:512]
        S.wait("tensor", "v_qb")
        colsq = [(j * 128, qT[:, j, t * 128:(t + 1) * 128]) for j in range(4)]
        emit_transposes(S, qbv, None, colsq, ps_ta, ps_tb, ident, "tp")
        if t == 0 and 'dbg_qT0' in io:
            S.wait("sync", "tp_v", S.val("tp_v"))
            for j in range(4):
                S.dma("sync", io['dbg_qT0'][:, j * 128:(j + 1) * 128], qT[:, j, 0:128], "st_mg")
        if not local:
            S.op("vector", lambda E: E.tensor_copy(latb[:], P1[:, 512:768]), inc="v_lat")
            S.wait("tensor", "v_lat")
            colsl = [(j * 128, latT_t[:, j, :]) for j in range(2)]
            S.wait("tensor", "t_kv", S.val("t_kv"))
            emit_transposes(S, latb[:].bitcast(BF16) if False else latb[:], None, colsl, ps_ta, ps_tb, ident, "tp")
            S.wait("tensor", "tp_v")
            S.wait("tensor", "v_kev", S.val("v_kev"))
            S.wait("tensor", "v_vev", S.val("v_vev"))
            for j in range(2):
                S.op("tensor", lambda E, j=j: E.matmul(P2[:, 0:128], latT_t[:, j, :], ku_s[:, j, :],
                                                       start=(j == 0), stop=(j == 1)))
            for j in range(2):
                S.op("tensor", lambda E, j=j: E.matmul(P2[:, 128:256], latT_t[:, j, :], vu_s[:, j, :],
                                                       start=(j == 0), stop=(j == 1)), inc="t_kv" if j == 1 else None)
            S.wait("vector", "t_kv")
        S.wait("vector", "tp_t", S.val("tp_t"))
        emit_rope_qknorm(S, P2[:, 0:128], 2, ctile[:, t % 2, :], stile[:, t % 2, :],
                         qf, qr, sqs, rstd8, kb[:])
        S.op("vector", lambda E, t=t: E.tensor_copy(vb[:, t, :], P2[:, 128:256]), inc="v_vev")
        if t == 0 and 'dbg_qf' in io:
            S.wait("sync", "v_vev", S.val("v_vev"))
            S.dma("sync", io['dbg_v0'], vb[:, 0, :], "st_mg")
        S.op("vector", lambda E: E.engine_nop(), inc="v_kev", drain=False)
        S.wait("tensor", "v_qb")
        colsk = [(0, kT[:, t * 128:(t + 1) * 128])]
        emit_transposes(S, kb[:], None, colsk, ps_ta, ps_tb, ident, "tp")

    es_proj.close()
    P4 = ps("P4", [128, 1024])
    S.wait("tensor", "ld_m")
    proj_done_v = S.val("tp_v")
    S.wait("tensor", "tp_v", proj_done_v)
    S.wait("tensor", "v_qev", S.val("v_qev"))
    S.wait("tensor", "v_lat", S.val("v_lat"))
    S.wait("tensor", "v_kev", S.val("v_kev"))
    S.wait("tensor", "v_vev", S.val("v_vev"))
    Sbufs = [P1, P2, P4]
    gi = [0]

    def emit_S(kt, qg, hp):
        j = gi[0]
        buf = Sbufs[j % 3]
        ksl = slice(kt * 128, (kt + 1) * 128)
        qsl = slice(qg * 512, (qg + 1) * 512)
        diag_j = kt - 4 * qg + (2 if local else 0)
        masked = (0 <= diag_j < nmask)
        S.wait("tensor", "s_exp", max(0, S.val("s_exp") - 2))
        S.op("tensor", lambda E, buf=buf, ksl=ksl, qsl=qsl, hp=hp, masked=masked: E.matmul(
            buf[:, 0:512], kT[0:64, ksl], qT[0:64, hp, qsl],
            start=True, stop=not masked, tile_position=(0, 0)))
        S.op("tensor", lambda E, buf=buf, ksl=ksl, qsl=qsl, hp=hp, masked=masked: E.matmul(
            buf[:, 512:1024], kT[64:128, ksl], qT[64:128, hp, qsl],
            start=True, stop=not masked, tile_position=(64, 0)),
            inc=None if masked else "t_S")
        if masked:
            S.op("tensor", lambda E, buf=buf, j=diag_j: E.matmul(
                buf[:, 0:512], ident[:], mask_s[:, j, :], start=False, stop=True))
            S.op("tensor", lambda E, buf=buf, j=diag_j: E.matmul(
                buf[:, 512:1024], ident[:], mask_s[:, j, :], start=False, stop=True), inc="t_S")

    def emit_exp():
        j = gi[0]
        buf = Sbufs[j % 3]
        S.wait("scalar", "t_S", S.val("t_S"))
        S.wait("scalar", "t_av", max(0, S.val("t_av") - 1))
        S.op("scalar", lambda E, buf=buf, s=j % 3: E.activation(pAB[:, s, :], buf[:, 0:1024], AF.Exp, scale=0.125),
             inc="s_exp")

    def emit_av(kt, first, last, lag):
        s = gi[0] % 3
        S.wait("tensor", "s_exp", S.val("s_exp") - lag)
        if first:
            S.wait("tensor", "v_mg", S.val("v_mg"))
        S.op("tensor", lambda E, s=s, first=first, last=last: E.matmul(
            P3[0:1, 512:1024], onescol_b[:], pAB[:, s, 0:512], start=first, stop=last, tile_position=(0, 0)))
        S.op("tensor", lambda E, s=s, first=first, last=last: E.matmul(
            P3[32:33, 512:1024], onescol_b[:], pAB[:, s, 512:1024], start=first, stop=last, tile_position=(0, 32)))
        S.op("tensor", lambda E, s=s, kt=kt, first=first, last=last: E.matmul(
            P3[0:64, 0:512], vb[:, kt, 0:64], pAB[:, s, 0:512], start=first, stop=last, tile_position=(0, 0)))
        S.op("tensor", lambda E, s=s, kt=kt, first=first, last=last: E.matmul(
            P3[64:128, 0:512], vb[:, kt, 64:128], pAB[:, s, 512:1024], start=first, stop=last, tile_position=(0, 64)),
            inc="t_av")

    for hp in range(4):
        for qg in range(8):
            kt_lo = max(0, 4 * qg - 2) if local else 0
            kts = list(range(kt_lo, 4 * qg + 4))
            qsl = slice(qg * 512, (qg + 1) * 512)
            n = len(kts)
            for i, kt in enumerate(kts):
                emit_S(kt, qg, hp)
                emit_exp()
                if i >= 1:
                    gi[0] -= 1
                    emit_av(kts[i - 1], i - 1 == 0, False, 1)
                    gi[0] += 1
                gi[0] += 1
            gi[0] -= 1
            emit_av(kts[n - 1], n == 1, True, 0)
            gi[0] += 1
            S.wait("scalar", "t_av", S.val("t_av"))
            S.op("scalar", lambda E: E.activation(rs_sb[:], P3[:, 512:1024], AF.Ln))
            S.op("scalar", lambda E: E.activation(rs2[:], rs_sb[:], AF.Exp, scale=-1.0), inc="s_rs")
            S.wait("tensor", "s_rs", S.val("s_rs"))
            S.op("tensor", lambda E: E.matmul(P3[0:64, 512:1024], onesp[0:1, 0:64], rs2[0:1, :],
                                              start=True, stop=True, tile_position=(0, 0)))
            S.op("tensor", lambda E: E.matmul(P3[64:128, 512:1024], onesp[32:33, 0:64], rs2[32:33, :],
                                              start=True, stop=True, tile_position=(32, 64)), inc="t_bc")
            S.wait("vector", "t_av", S.val("t_av"))
            S.op("vector", lambda E, hp=hp, qsl=qsl: E.tensor_copy(merged[:, hp, qsl], P3[:, 0:512]), drain=False)
            S.wait("vector", "t_bc", S.val("t_bc"))
            S.op("vector", lambda E, hp=hp, qsl=qsl: E.tensor_tensor(
                merged[:, hp, qsl], merged[:, hp, qsl], P3[:, 512:1024], ALU.mult), inc="v_mg")
    S.wait("sync", "v_mg")
    for j in range(4):
        S.dma("sync", mg_out[j * 128:(j + 1) * 128, :], merged[:, j, :], "st_mg")
    if 'dbg_qT' in io:
        for j in range(4):
            S.dma("sync", io['dbg_qT'][j * 128:(j + 1) * 128, :], qT[:, j, :], "st_mg")
        S.dma("sync", io['dbg_kT'], kT[:], "st_mg")
        S.dma("sync", io['dbg_vb'], vb[:].rearrange("p t d -> p (t d)"), "st_mg")
        if xn_idx is not None:
            S.dma("sync", io['dbg_idx'], idxb5[:], "st_mg")
    S.wait("sync", "st_mg")



def build_launch5(local=False):
    nmask = 6 if local else 4
    nc = mkbass()
    xn_in = nc.dram_tensor("xn_full", [2 * TOK, D], BF16, kind="ExternalInput").ap()
    wq_in = nc.dram_tensor("wq_h", [D, 512], F32, kind="ExternalInput").ap()
    if not local:
        kvd_in = nc.dram_tensor("kvd", [D, 256], F32, kind="ExternalInput").ap()
        ku_in = nc.dram_tensor("ku_h", [256, 128], F32, kind="ExternalInput").ap()
        vu_in = nc.dram_tensor("vu_h", [256, 128], F32, kind="ExternalInput").ap()
    else:
        wk_in = nc.dram_tensor("wk_h", [D, 128], F32, kind="ExternalInput").ap()
        wv_in = nc.dram_tensor("wv_h", [D, 128], F32, kind="ExternalInput").ap()
    cos_in = nc.dram_tensor("cosr", [2 * TOK, 32], F32, kind="ExternalInput").ap()
    sin_in = nc.dram_tensor("sinr", [2 * TOK, 32], F32, kind="ExternalInput").ap()
    mask_in = nc.dram_tensor("maskadd", [nmask, 128, 512], BF16, kind="ExternalInput").ap()
    id_in = nc.dram_tensor("ident", [128, 128], F32, kind="ExternalInput").ap()
    mg_out = nc.dram_tensor("mergedT", [512, 2 * TOK], BF16, kind="ExternalOutput").ap()
    es = ExitStack()
    S = Sched(nc); S._es = es
    io = dict(xn_in=xn_in, wq_in=wq_in, cos_in=cos_in, sin_in=sin_in, mask_in=mask_in,
              id_in=id_in, mg_out=mg_out)
    if not local:
        io.update(kvd_in=kvd_in, ku_in=ku_in, vu_in=vu_in)
    else:
        io.update(wk_in=wk_in, wv_in=wv_in)
    emit_l5(S, nc, es, io, local)
    with es, nc.Block() as block:
        S.build(block, es)
    return nc


def emit_l6(S, nc, es, io, final):
    mg_in = io['mg_in']; wo_in = io['wo_in']
    if not final:
        xn1_in = io['xn1_in']; n2_in = io['n2_in']; id_in = io['id_in']
        xa_out = io['xa_out']; xn2_out = io['xn2_out']
    else:
        xa_in = io['xa_in']; x1_in = io['x1_in']; g1_in = io['g1_in']; out_out = io['out_out']
    mg_idx = io.get('mg_idx')
    sb = lambda n_, s_, d_=F32: es.enter_context(nc.sbuf_tensor(S.pfx + n_, s_, d_))
    ps = lambda n_, s_, d_=F32: es.enter_context(nc.psum_tensor(S.pfx + n_, s_, d_))
    mg_s = sb("mg_s", [128, DC, TOK], BF16)
    wo_s = sb("wo_s", [128, DC, D], BF16)
    aux = sb("aux", [128, 2, D])
    auxb = sb("auxb", [128, 2, D], BF16)
    xab = sb("xab", [128, D])
    outb = sb("outb", [128, D])
    rstd = sb("rstd", [128, 1]); sq = sb("sq", [128, D])
    xnb2 = sb("xnb2", [128, D], BF16)
    n2b = sb("n2b", [128, D]); rowscr = sb("rowscr", [1, D])
    onesrow = sb("onesrow", [1, 128])
    P1 = ps("P1", [128, 1024])
    P2 = ps("P2", [128, 1024])

    S.setup_consts([1e-6, -0.5], sb)
    S.wait("scalar", "c_init")
    if mg_idx is not None:
        idxb6 = sb("idxb6", [128, DC], I32)
        S.dma("sync", idxb6[:], mg_idx, "ld_ix")
        S.wait("gpsimd", "ld_ix")
    for k in range(DC):
        S.dma("gpsimd", wo_s[:, k, :], wo_in[k * 128:(k + 1) * 128, :], "ld_w")
        if mg_idx is None:
            S.dma("sync", mg_s[:, k, :], mg_in[k * 128:(k + 1) * 128, :], "ld_m")
        else:
            S.op("gpsimd", lambda E, k=k: E.indirect_dma_start(
                mg_s[:, k, :], None, mg_in, bass.IndirectOffsetOnAxis(idxb6[:, k:k + 1], 0)),
                inc="ld_m", n=16, drain=False)
    if not final:
        S.op("gpsimd", lambda E: E.memset(onesrow[:], 1.0), inc="c_ones")
        S.dma("sync", rowscr[0:1, :], n2_in, "ld_misc")
        S.wait("tensor", "ld_misc")
        S.wait("tensor", "c_ones")
        emit_bcast_row(S, rowscr[0:1, :], onesrow, P2, n2b[:], "bc1")

    for t in range(NT):
        if not final:
            S.wait("sync", "v_xa", S.val("v_xa"))
            S.dma("sync", auxb[:, 0, :], xn1_in[t * 128:(t + 1) * 128, :], "ld_a")
        else:
            S.wait("sync", "v_out", S.val("v_out"))
            S.wait("sync", "st_out", S.val("st_out"))
            S.dma("sync", aux[:, 0, :], xa_in[t * 128:(t + 1) * 128, :], "ld_a")
            S.dma("sync", aux[:, 1, :], g1_in[t * 128:(t + 1) * 128, :], "ld_a")
            S.dma("sync", outb[:], x1_in[t * 128:(t + 1) * 128, :], "ld_a")
        S.wait("tensor", "ld_w")
        S.wait("tensor", "ld_m")
        S.wait("tensor", "v_wo", S.val("v_wo"))
        for k in range(DC):
            for n0 in (0, 512):
                S.op("tensor", lambda E, k=k, t=t, n0=n0: E.matmul(
                    P1[:, n0:n0 + 512], mg_s[:, k, t * 128:(t + 1) * 128], wo_s[:, k, n0:n0 + 512],
                    start=(k == 0), stop=(k == DC - 1)),
                    inc="t_wo" if (k == DC - 1 and n0 == 512) else None)
        na = 16 * (t + 1) * (1 if not final else 3)
        S.wait("vector", "ld_a", na)
        S.wait("vector", "t_wo")
        if not final:
            S.wait("vector", "st_xa", S.val("st_xa"))
            S.op("vector", lambda E: E.tensor_tensor(xab[:], P1[:], auxb[:, 0, :], ALU.add), inc="v_wo")
            S.wait("sync", "v_wo")
            S.dma("sync", xa_out[t * 128:(t + 1) * 128, :], xab[:], "st_xa")
            S.wait("vector", "bc1_v")
            emit_rms(S, xab[:], rstd, sq, n2b[:], xnb2[:])
            S.wait("sync", "v_xn")
            S.dma("sync", xn2_out[t * 128:(t + 1) * 128, :], xnb2[:], "st_xa")
            S.op("vector", lambda E: E.engine_nop(), inc="v_xa", drain=False)
        else:
            S.wait("vector", "st_out", S.val("st_out"))
            S.op("vector", lambda E: E.tensor_tensor(xab[:], P1[:], aux[:, 0, :], ALU.add), inc="v_wo")
            S.op("vector", lambda E: E.tensor_mul(xab[:], xab[:], aux[:, 1, :]), drain=False)
            S.op("vector", lambda E: E.tensor_add(outb[:], outb[:], xab[:]), inc="v_out")
            S.wait("sync", "v_out")
            S.dma("sync", out_out[t * 128:(t + 1) * 128, :], outb[:], "st_out")
    S.wait("sync", "st_xa" if not final else "st_out")



def build_launch6(final=False):
    nc = mkbass()
    mg_in = nc.dram_tensor("mergedT_full", [D, TOK], BF16, kind="ExternalInput").ap()
    wo_in = nc.dram_tensor("wo", [D, D], F32, kind="ExternalInput").ap()
    if not final:
        xn1_in = nc.dram_tensor("xn1", [TOK, D], BF16, kind="ExternalInput").ap()
        n2_in = nc.dram_tensor("norm2_w", [1, D], F32, kind="ExternalInput").ap()
        id_in = nc.dram_tensor("ident", [128, 128], F32, kind="ExternalInput").ap()
        xa_out = nc.dram_tensor("x_a", [TOK, D], F32, kind="ExternalOutput").ap()
        xn2_out = nc.dram_tensor("xn2", [TOK, D], BF16, kind="ExternalOutput").ap()
    else:
        xa_in = nc.dram_tensor("x_a", [TOK, D], F32, kind="ExternalInput").ap()
        x1_in = nc.dram_tensor("x1", [TOK, D], F32, kind="ExternalInput").ap()
        g1_in = nc.dram_tensor("g1", [TOK, D], F32, kind="ExternalInput").ap()
        out_out = nc.dram_tensor("out", [TOK, D], F32, kind="ExternalOutput").ap()
    es = ExitStack()
    S = Sched(nc); S._es = es
    io = dict(mg_in=mg_in, wo_in=wo_in)
    if not final:
        io.update(xn1_in=xn1_in, n2_in=n2_in, id_in=id_in, xa_out=xa_out, xn2_out=xn2_out)
    else:
        io.update(xa_in=xa_in, x1_in=x1_in, g1_in=g1_in, out_out=out_out)
    emit_l6(S, nc, es, io, final)
    with es, nc.Block() as block:
        S.build(block, es)
    return nc


def build_fused(upto=99):
    nc = mkbass()
    EI, EO = "ExternalInput", "ExternalOutput"
    dt_ = lambda n, s, d=F32, k="Internal": nc.dram_tensor(n, s, d, kind=k).ap()
    x_in = dt_("x", [TOK, D], F32, EI)
    wg_sl = dt_("wg_sl", [352, D], F32, EI)
    wu_sl = dt_("wu_sl", [352, D], F32, EI)
    wd_sl = dt_("wd_sl", [128, I_FFN], F32, EI)
    WvT = dt_("WvT", [D, D], F32, EI)
    WqT = dt_("WqT", [D, D], F32, EI)
    WgT = dt_("WgT", [D, D], F32, EI)
    WoT = dt_("WoT", [D, D], F32, EI)
    beta_in = dt_("beta", [1, H], F32, EI)
    gam_in = dt_("ln_gamma", [1, D], F32, EI)
    bet_in = dt_("ln_beta", [1, D], F32, EI)
    id_in = dt_("ident", [128, 128], F32, EI)
    cidx_in = dt_("carry_idx", [NT, 1], I32, EI)
    n1_in = dt_("norm1_w", [1, D], F32, EI)
    n2_in = dt_("norm2_w", [1, D], F32, EI)
    ws_in = dt_("WsT", [D, D], F32, EI)
    we_in = dt_("WeT", [D, D], F32, EI)
    bias_in = dt_("bias", [1, D], F32, EI)
    gwq_in = dt_("g_wq_h", [D, 512], F32, EI)
    kvd_in = dt_("g_kvd", [D, 256], F32, EI)
    ku_in = dt_("g_ku_h", [256, 128], F32, EI)
    vu_in = dt_("g_vu_h", [256, 128], F32, EI)
    gwo_in = dt_("g_wo", [D, D], F32, EI)
    lwq_in = dt_("l_wq_h", [D, 512], F32, EI)
    lwk_in = dt_("l_wk_h", [D, 128], F32, EI)
    lwv_in = dt_("l_wv_h", [D, 128], F32, EI)
    lwo_in = dt_("l_wo", [D, D], F32, EI)
    cosg_in = dt_("cosg", [2 * TOK, 32], F32, EI)
    sing_in = dt_("sing", [2 * TOK, 32], F32, EI)
    cosl_in = dt_("cosl", [2 * TOK, 32], F32, EI)
    sinl_in = dt_("sinl", [2 * TOK, 32], F32, EI)
    maskG_in = dt_("maskG", [4, 128, 512], BF16, EI)
    maskL_in = dt_("maskL", [6, 128, 512], BF16, EI)
    xnidx_g = dt_("xnidx_g", [128, NTF], I32, EI)
    xnidx_l = dt_("xnidx_l", [128, NTF], I32, EI)
    mgidx_g = dt_("mgidx_g", [128, DC], I32, EI)
    mgidx_l = dt_("mgidx_l", [128, DC], I32, EI)
    sums_idx = dt_("sums_idx", [2 * NT, 1], I32, EI)
    out_out = dt_("out", [TOK, D], F32, EO)
    wgq_sl = dt_("wgq_sl_i", [352, D], BF16)
    wuq_sl = dt_("wuq_sl_i", [352, D], BF16)
    wdq_sl = dt_("wdq_sl_i", [128, I_FFN], BF16)
    wgq_all = dt_("wgq_all", [I_FFN, D], BF16)
    wuq_all = dt_("wuq_all", [I_FFN, D], BF16)
    wdq_all = dt_("wdq_all", [D, I_FFN], BF16)
    xn0_i = dt_("xn0_i", [TOK, D], BF16)
    gacc_i = dt_("gacc_i", [TOK, D], F32)
    stl_i = dt_("stl_i", [TOK, D], BF16)
    sums_i = dt_("sums_i", [NT, D], F32)
    sums_ag = dt_("sums_ag", [8 * NT, D], F32)
    sums_st = dt_("sums_st", [2 * NT, D], F32)
    carr_dram = dt_("carr_dram", [2 * NT, D], F32)
    carrmy_dram = dt_("carrmy_dram", [NT, D], F32)
    x1_i = dt_("x1_i", [TOK, D], F32)
    xn1_i = dt_("xn1_i", [TOK, D], BF16)
    xnr1_i = dt_("xnr1_i", [TOK, D], BF16)
    g1_i = dt_("g1_i", [TOK, D], F32)
    xnr1_ag = dt_("xnr1_ag", [8 * TOK, D], BF16)
    mgG_i = dt_("mgG_i", [512, 2 * TOK], BF16)
    mgG_ag = dt_("mgG_ag", [8 * 512, 2 * TOK], BF16)
    xa_i = dt_("xa_i", [TOK, D], F32)
    xn2_i = dt_("xn2_i", [TOK, D], BF16)
    xn2_ag = dt_("xn2_ag", [8 * TOK, D], BF16)
    mgL_i = dt_("mgL_i", [512, 2 * TOK], BF16)
    mgL_ag = dt_("mgL_ag", [8 * 512, 2 * TOK], BF16)

    es_g = ExitStack()
    S = Sched(nc); S._es = es_g
    sbg = lambda n_, s_, d_=F32: es_g.enter_context(nc.sbuf_tensor(n_, s_, d_))
    sums_gsb = sbg("sums_gsb", [2 * NT, D])
    sidxb = sbg("sidxb", [2 * NT, 1], I32)

    RG = [[0, 1, 2, 3, 4, 5, 6, 7]]

    def ag(cc_in, cc_out):
        S.op("gpsimd", lambda E, i=cc_in, o=cc_out: E.collective_compute(
            "AllGather", ALU.bypass, replica_groups=RG, ins=[i.opt()], outs=[o.opt()]),
            inc="!cc", n=1, drain=False)

    PH = [0]
    def phase(pfx, fn):
        if PH[0] > upto:
            return
        PH[0] += 1
        S.pfx = pfx
        S.rebase()
        es_p = ExitStack()
        fn(es_p)
        S.pfx = ""
        S.barrier()
        es_p.close()

    def glue(fn):
        if PH[0] > upto:
            return
        fn()

    phase("p0_", lambda es: emit_l0(S, nc, es, dict(
        wg_in=wg_sl, wu_in=wu_sl, wd_in=wd_sl,
        wg_out=wgq_sl.tensor.ap() if False else wgq_sl, wu_out=wuq_sl, wd_out=wdq_sl)))
    def g0():
        ag(wgq_sl, wgq_all); ag(wuq_sl, wuq_all); ag(wdq_sl, wdq_all)
        S.wait("sync", "!cc", 3); S.wait("gpsimd", "!cc", 3)
    glue(g0)
    phase("p1_", lambda es: emit_l1(S, nc, es, dict(
        x_in=x_in, wgq=wgq_all, wuq=wuq_all, wdq=wdq_all, id_in=id_in,
        xn0_out=xn0_i, gacc_out=gacc_i)))
    phase("p2_", lambda es: emit_l2(S, nc, es, dict(
        xn_in=xn0_i, wv_in=WvT, beta_in=beta_in, id_in=id_in,
        st_out=stl_i, sums_out=sums_i)))
    def g2():
        ag(sums_i, sums_ag)
        S.pfx = "g2_"
        S.rebase()
        S.dma("sync", sidxb[:], sums_idx, "ld_carr")
        S.wait("gpsimd", "!cc", 4)
        S.wait("gpsimd", "ld_carr")
        S.op("gpsimd", lambda E: E.indirect_dma_start(
            sums_gsb[:], None, sums_ag, bass.IndirectOffsetOnAxis(sidxb[:], 0)), inc="g_carr", n=16, drain=False)
        S.wait("gpsimd", "g_carr")
        S.op("gpsimd", lambda E: E.dma_start(out=sums_st, in_=sums_gsb[:]), inc="g_carr", n=16, drain=False)
        S.wait("sync", "g_carr", 32)
        S.pfx = ""
    glue(g2)
    phase("p3_", lambda es: emit_l3(S, nc, es, dict(
        xn_in=xn0_i, stl_in=stl_i, sums_in=sums_st, x_in=x_in, gacc_in=gacc_i,
        wq_in=WqT, wg_in=WgT, wo_in=WoT, beta_in=beta_in, gam_in=gam_in, bet_in=bet_in,
        id_in=id_in, cidx_in=cidx_in, x1_out=x1_i, carr_dram=carr_dram, carrmy_dram=carrmy_dram)))
    phase("p4_", lambda es: emit_l4(S, nc, es, dict(
        x1_in=x1_i, n1_in=n1_in, ws_in=ws_in, we_in=we_in, bias_in=bias_in, id_in=id_in,
        xn1_out=xn1_i, xnr1_out=xnr1_i, g1_out=g1_i)))
    def g4():
        ag(xnr1_i, xnr1_ag)
    glue(g4)
    p5io = dict(
        xn_in=xnr1_ag, xn_idx=xnidx_g, wq_in=gwq_in, kvd_in=kvd_in, ku_in=ku_in, vu_in=vu_in,
        cos_in=cosg_in, sin_in=sing_in, mask_in=maskG_in, id_in=id_in, mg_out=mgG_i, cc_wait=5)
    if 90 <= upto < 99:
        p5io['dbg_qT'] = nc.dram_tensor("d_qT", [512, 4096], BF16, kind="ExternalOutput").ap()
        p5io['dbg_kT'] = nc.dram_tensor("d_kT", [128, 4096], BF16, kind="ExternalOutput").ap()
        p5io['dbg_vb'] = nc.dram_tensor("d_vb", [128, 4096], BF16, kind="ExternalOutput").ap()
        p5io['dbg_idx'] = nc.dram_tensor("d_idx", [128, NTF], I32, kind="ExternalOutput").ap()
        p5io['dbg_xnb'] = nc.dram_tensor("d_xnb", [128, D], BF16, kind="ExternalOutput").ap()
        p5io['dbg_xnT'] = nc.dram_tensor("d_xnT", [128, D], BF16, kind="ExternalOutput").ap()
        p5io['dbg_qf'] = nc.dram_tensor("d_qf", [128, 512], F32, kind="ExternalOutput").ap()
        p5io['dbg_v0'] = nc.dram_tensor("d_v0", [128, 128], BF16, kind="ExternalOutput").ap()
        p5io['dbg_qT0'] = nc.dram_tensor("d_qT0", [128, 512], BF16, kind="ExternalOutput").ap()
    phase("p5_", lambda es: emit_l5(S, nc, es, p5io, False))
    def g5():
        ag(mgG_i, mgG_ag)
        S.wait("gpsimd", "!cc", 6)
    glue(g5)
    phase("p6_", lambda es: emit_l6(S, nc, es, dict(
        mg_in=mgG_ag.rearrange("(r hh) t -> (r hh) t") if False else mgG_ag.rearrange("r (hh t) -> (r hh) t", hh=2),
        mg_idx=mgidx_g, wo_in=gwo_in, xn1_in=xn1_i, n2_in=n2_in, id_in=id_in,
        xa_out=xa_i, xn2_out=xn2_i), False))
    def g6():
        ag(xn2_i, xn2_ag)
    glue(g6)
    p7io = dict(
        xn_in=xn2_ag, xn_idx=xnidx_l, wq_in=lwq_in, wk_in=lwk_in, wv_in=lwv_in,
        cos_in=cosl_in, sin_in=sinl_in, mask_in=maskL_in, id_in=id_in, mg_out=mgL_i, cc_wait=7)
    phase("p7_", lambda es: emit_l5(S, nc, es, p7io, True))
    def g7():
        ag(mgL_i, mgL_ag)
        S.wait("gpsimd", "!cc", 8)
    glue(g7)
    phase("p8_", lambda es: emit_l6(S, nc, es, dict(
        mg_in=mgL_ag.rearrange("r (hh t) -> (r hh) t", hh=2),
        mg_idx=mgidx_l, wo_in=lwo_in, xa_in=xa_i, x1_in=x1_i, g1_in=g1_i, out_out=out_out), True))

    if upto < 99:
        S.pfx = "gd_"
        S.rebase()
        srcs = {0: None, 1: gacc_i, 2: gacc_i, 3: x1_i, 4: g1_i, 5: g1_i, 6: xa_i, 7: xa_i}
        sd = srcs.get(min(upto, 7), x1_i)
        if upto >= 90:
            for nm, t_, d_ in [("d_xnrag", xnr1_ag, BF16), ("d_sumsag", sums_ag, F32),
                               ("d_xn2ag", xn2_ag, BF16), ("d_mgL", mgL_i, BF16), ("d_mgLag", mgL_ag, BF16),
                               ("d_xn0", xn0_i, BF16), ("d_gacc", gacc_i, F32), ("d_x1", x1_i, F32),
                               ("d_g1", g1_i, F32), ("d_xa", xa_i, F32), ("d_xnr1", xnr1_i, BF16),
                               ("d_mgG", mgG_i, BF16), ("d_xn2", xn2_i, BF16), ("d_stl", stl_i, BF16)]:
                shp = list(t_.shape)
                dbg = nc.dram_tensor(nm, shp, d_, kind="ExternalOutput").ap()
                S.dma("sync", dbg, t_, "st_dbg")
            S.wait("sync", "st_dbg")
        elif sd is not None:
            S.dma("sync", out_out, sd, "st_dbg")
            S.wait("sync", "st_dbg")

    with es_g, nc.Block() as block:
        S.build(block, es_g)
    return nc



HEAD_ORDER = [0, 4, 1, 5, 2, 6, 3, 7]

def _rope_tables(base, S_=4096, hd=64):
    inv = (1.0 / base ** (np.arange(0, hd, 2, dtype=np.float32) / np.float32(hd))).astype(np.float32)
    fr = np.arange(S_, dtype=np.float32)[:, None] * inv[None, :]
    return np.cos(fr).astype(np.float32), np.sin(fr).astype(np.float32)

def _build_masks(local):
    n = 6 if local else 4
    out = np.zeros((n, 128, 512), np.float32)
    k = np.arange(128)[:, None]
    qf = np.arange(512)[None, :]
    for j in range(n):
        rel = (j - 2) if local else j
        d = qf - (rel * 128 + k)
        vis = (d >= 0) & (d < 256) if local else (d >= 0)
        out[j] = np.where(vis, 0.0, -30000.0)
    return out.astype(BF)

def _head_cols(half):
    return np.concatenate([np.arange((half * 8 + h) * 64, (half * 8 + h) * 64 + 64) for h in HEAD_ORDER])

WO_ROWS = np.concatenate([np.arange(h * 64, h * 64 + 64) for h in
                          [0, 4, 1, 5, 2, 6, 3, 7, 8, 12, 9, 13, 10, 14, 11, 15]])

_BUILDERS = {}

def _get(name, fn):
    if name not in _BUILDERS:
        _BUILDERS[name] = fn()
    return _BUILDERS[name]


def kernel(**inputs):
    g = {k: np.asarray(v, np.float32) for k, v in inputs.items()}
    C = np.ascontiguousarray
    ident = np.eye(128, dtype=np.float32)
    cosg, sing = _rope_tables(1e6)
    cosl, sinl = _rope_tables(1e4)
    maskG = _build_masks(False)
    maskL = _build_masks(True)
    wg, wu, wd = g['L0_ffn_w_gate'], g['L0_ffn_w_up'], g['L0_ffn_w_down']
    p = np.arange(128, dtype=np.int32)[:, None]
    t = np.arange(NTF, dtype=np.int32)[None, :]
    k = np.arange(DC, dtype=np.int32)[None, :]
    maps = []
    for c in range(8):
        b, half = c // 2, c % 2
        xnidx = (b * 4096 + t * 128 + p).astype(np.int32)
        mgidx = (2 * ((2 * b + k // 4) * 512 + (k % 4) * 128 + p) + half).astype(np.int32)
        maps.append(dict(
            x=C(g['L0_attn_x'][b, half*2048:(half+1)*2048]),
            wg_sl=C(wg[c*352:(c+1)*352]), wu_sl=C(wu[c*352:(c+1)*352]), wd_sl=C(wd[c*128:(c+1)*128]),
            WvT=C(g['L0_attn_Wv'].T), WqT=C(g['L0_attn_Wq'].T), WgT=C(g['L0_attn_Wg'].T),
            WoT=C(g['L0_attn_Wo'].T),
            beta=g['L0_attn_beta'].reshape(1, 16),
            ln_gamma=g['L0_attn_ln_gamma'].reshape(1, 1024), ln_beta=g['L0_attn_ln_beta'].reshape(1, 1024),
            ident=ident,
            carry_idx=(np.arange(16, dtype=np.int32) + 16 * half).reshape(16, 1),
            norm1_w=g['L1_attn_norm1_w'].reshape(1, 1024), norm2_w=g['L1_attn_norm2_w'].reshape(1, 1024),
            WsT=C(g['L1_ffn_W_shared'].T), WeT=C(g['L1_ffn_W_expert'].T),
            bias=g['L1_ffn_bias'].reshape(1, 1024),
            g_wq_h=C(g['L1_attn_g_wq'][:, _head_cols(half)]),
            g_kvd=C(g['L1_attn_g_kv_down']),
            g_ku_h=C(g['L1_attn_g_k_up'][:, half*128:(half+1)*128]),
            g_vu_h=C(g['L1_attn_g_v_up'][:, half*128:(half+1)*128]),
            g_wo=C(g['L1_attn_g_wo'][WO_ROWS, :]),
            l_wq_h=C(g['L1_attn_l_wq'][:, _head_cols(half)]),
            l_wk_h=C(g['L1_attn_l_wk'][:, half*128:(half+1)*128]),
            l_wv_h=C(g['L1_attn_l_wv'][:, half*128:(half+1)*128]),
            l_wo=C(g['L1_attn_l_wo'][WO_ROWS, :]),
            cosg=cosg, sing=sing, cosl=cosl, sinl=sinl,
            maskG=maskG, maskL=maskL,
            xnidx_g=xnidx, xnidx_l=xnidx, mgidx_g=mgidx, mgidx_l=mgidx,
            sums_idx=(np.arange(32, dtype=np.int32) + 32 * b).reshape(32, 1),
        ))
    res = run_bass_kernel_spmd(_get('fused', build_fused), maps, core_ids=list(range(8)))
    out = np.zeros((4, 4096, 1024), np.float32)
    for c in range(8):
        out[c//2, (c%2)*2048:(c%2+1)*2048] = res.results[c]['out']
    return out
